# version 01
# baseline (speedup 1.0000x reference)
import numpy as np
from contextlib import ExitStack

import concourse.bass as bass
import concourse.mybir as mybir
from concourse.bass_utils import run_bass_kernel_spmd

F32 = mybir.dt.float32
BF16 = mybir.dt.bfloat16
AF = mybir.ActivationFunctionType
ALU = mybir.AluOpType

T = 2048
D = 1024
G = D // 128
PAD = 8
TW = T + PAD
TC = 512

MAGIC = 12582912.0

ENGS = ("pe", "dve", "act", "pool", "sp")


class Prog:

    QRING = {"sp": 6, "pool": 8, "act": 4}

    def __init__(self, nc, es):
        self.nc = nc
        self.items = {e: [] for e in ENGS}
        self.sems = {}
        for s in ("s_pe", "s_dve", "s_act", "s_pool", "s_sp"):
            self.sems[s] = [es.enter_context(nc.semaphore(s)), 0]
        self.qrings = {}
        self.qcount = {}
        for q, n in self.QRING.items():
            names = []
            for i in range(n):
                s = "q_%s_%d" % (q, i)
                self.sems[s] = [es.enter_context(nc.semaphore(s)), 0]
                names.append(s)
            self.qrings[q] = names
            self.qcount[q] = 0

    def _wait(self, eng, tok):
        if tok is None:
            return
        sem, val = self.sems[tok[0]][0], tok[1]
        if val <= 0:
            return
        self.items[eng].append(lambda e, s=sem, v=val: e.wait_ge(s, v))

    def op(self, eng, fn, waits=(), sig=False):
        for t in waits:
            self._wait(eng, t)
        if sig:
            rec = self.sems["s_" + eng]
            rec[1] += 1
            tok = ("s_" + eng, rec[1])
            self.items[eng].append(
                lambda e, f=fn, s=rec[0]: f(e).then_inc(s, 1))
            return tok
        self.items[eng].append(lambda e, f=fn: f(e))
        return None

    def dma(self, q, out, in_, waits=()):
        ring = self.qrings[q]
        k = self.qcount[q]
        self.qcount[q] += 1
        sname = ring[k % len(ring)]
        rec = self.sems[sname]
        if rec[1] > 0:
            self._wait(q, (sname, rec[1]))
        for t in waits:
            self._wait(q, t)
        rec[1] += 16
        tok = (sname, rec[1])
        self.items[q].append(
            lambda e, o=out, i=in_, s=rec[0]: e.dma_start(out=o, in_=i).then_inc(s, 16))
        return tok

    def all_tokens(self):
        return [(name, rec[1]) for name, rec in self.sems.items() if rec[1] > 0]

    def barrier(self):
        toks = self.all_tokens()
        for e in ENGS:
            for t in toks:
                self._wait(e, t)

    def finalize(self):
        nc = self.nc
        with nc.Block() as block:
            @block.tensor
            def _(e):
                for fn in self.items["pe"]:
                    fn(e)

            @block.vector
            def _(e):
                for fn in self.items["dve"]:
                    fn(e)

            @block.scalar
            def _(e):
                for fn in self.items["act"]:
                    fn(e)

            @block.gpsimd
            def _(e):
                for fn in self.items["pool"]:
                    fn(e)

            @block.sync
            def _(e):
                for fn in self.items["sp"]:
                    fn(e)


SCRF_COLS = 6144
SCRB_COLS = 24576


class Env:

    def __init__(self, nc, es, P):
        self.nc = nc
        self.es = es
        self.P = P
        sb = lambda n, shp, dt: es.enter_context(nc.sbuf_tensor(n, shp, dt))
        self.XRES = sb("XRES", [128, G * T], F32)
        self.XNBF = sb("XNBF", [128, G * TW], BF16)
        self.SCRF = sb("SCRF", [128, SCRF_COLS], F32)
        self.SCRB = sb("SCRB", [128, SCRB_COLS], BF16)
        self.WB = [sb("WBA", [128, 2176], BF16), sb("WBB", [128, 2176], BF16)]
        self.wb_tok = [None, None]
        self.wb_idx = 0
        self.ROWS = sb("ROWS", [1, 6 * 512], F32)
        self.ROWSB = sb("ROWSB", [1, 2 * 512], BF16)
        self.ONE_L = sb("ONE_L", [128, 128], BF16)
        self.ONE_R = sb("ONE_R", [1, 512], BF16)
        self.CONS = sb("CONS", [128, 1024], F32)
        self.cons_off = 0
        self.banks = []
        for i in range(8):
            ap = es.enter_context(nc.psum_tensor("pb%d" % i, [128, 512], F32))
            self.banks.append([ap, None, False])
        self.bank_rr = 0
        self.scrf_off = 0
        self.scrb_off = 0
        P.op("dve", lambda e: e.memset(self.ONE_L[:], 1.0))
        P.op("dve", lambda e: e.memset(self.ONE_R[:], 1.0))
        self.init_tok = P.op("dve", lambda e: e.memset(self.XNBF[:], 0.0), sig=True)

    def cons(self, ncol):
        v = self.CONS[:, self.cons_off:self.cons_off + ncol]
        self.cons_off += ncol
        assert self.cons_off <= 1024
        return v

    def constf(self, val):
        if not hasattr(self, "_constf"):
            self._constf = {}
        if val in self._constf:
            return self._constf[val]
        ap = self.cons(1)
        tok = self.P.op("dve", lambda e, o=ap, v=val: e.memset(o, v), sig=True)
        self._constf[val] = (ap, tok)
        return ap, tok

    def scr_reset(self):
        self.scrf_off = 0
        self.scrb_off = 0

    def scrf(self, cols):
        v = self.SCRF[:, self.scrf_off:self.scrf_off + cols]
        self.scrf_off += cols
        assert self.scrf_off <= SCRF_COLS, "SCRF overflow"
        return v

    def scrb(self, cols):
        v = self.SCRB[:, self.scrb_off:self.scrb_off + cols]
        self.scrb_off += cols
        assert self.scrb_off <= SCRB_COLS, "SCRB overflow"
        return v

    def bank(self, hold=False):
        for _ in range(8):
            i = self.bank_rr
            self.bank_rr = (self.bank_rr + 1) % 8
            if not self.banks[i][2]:
                break
        else:
            raise RuntimeError("all psum banks held")
        ap, tok, _ = self.banks[i]
        self.banks[i][2] = hold
        return i, ap, tok

    def bank_free(self, i, tok):
        self.banks[i][1] = tok
        self.banks[i][2] = False

    def xres(self, g, t0=0, n=T):
        return self.XRES[:, g * T + t0: g * T + t0 + n]

    def xn(self, g, t0=0, n=T):
        return self.XNBF[:, g * TW + PAD + t0: g * TW + PAD + t0 + n]

    def xn_shift(self, g, shift, t0=0, n=T):
        c = g * TW + PAD + t0 - shift
        return self.XNBF[:, c: c + n]


def load_w_tile(P, E, w_dram, m0, nm, waits=()):
    idx = E.wb_idx
    E.wb_idx ^= 1
    buf = E.WB[idx]
    K, M = w_dram.shape
    nkg = (K + 127) // 128
    assert nkg * nm <= 2176, (nkg, nm)
    src = w_dram[:, m0:m0 + nm]
    if nkg > 1:
        assert K % 128 == 0, K
        src = src.rearrange("(kg p) m -> p kg m", p=128)
        dst = buf[:, 0:nkg * nm].rearrange("p (kg m) -> p kg m", kg=nkg)
    else:
        src = src.unsqueeze(1)
        dst = buf[0:K, 0:nm].unsqueeze(1)
    w8 = list(waits)
    if E.wb_tok[idx] is not None:
        w8.append(E.wb_tok[idx])
    q = "pool" if w_dram.dtype == F32 else "sp"
    tok = P.dma(q, dst, src, waits=w8)
    return idx, buf, tok


def mm_fm(P, E, segments, out_cb, M, Ms=None, col_tiles=None,
          extra_rows=None):
    if col_tiles is None:
        col_tiles = [(c, min(512, T - c)) for c in range(0, T, 512)]
    if Ms is None:
        Ms = [(m, min(128, M - m)) for m in range(0, M, 128)]
    assert sum(1 for s in segments
               if not (isinstance(s[0], tuple) and s[0][0] == "sbuf")) <= 2, \
        "at most 2 streamed weight segments (2 weight buffers)"
    for (m0, nm) in Ms:
        loaded = []
        for seg in segments:
            w_src, xcols, x_tok = seg[0], seg[1], seg[2]
            if isinstance(w_src, tuple) and w_src[0] == "sbuf":
                loaded.append((None, None, w_src[1], w_src[2], xcols, x_tok))
            else:
                K = w_src.shape[0]
                bidx, buf, wtok = load_w_tile(P, E, w_src, m0, nm)
                tile_fn = (lambda kg, mm0, mnm, b=buf, n=nm, KK=K:
                           b[0:min(128, KK - kg * 128), kg * n: kg * n + mnm])
                loaded.append((bidx, wtok, tile_fn, K, xcols, x_tok))
        first = True
        last_pe = None
        for (c0, nc) in col_tiles:
            bi, ps, ftok = E.bank()
            pa = ps[0:nm, 0:nc]
            tok = None
            total = sum((K + 127) // 128 for (_, _, _, K, _, _) in loaded)
            idx = 0
            for (bidx, wtok, tile_fn, K, xcols, x_tok) in loaded:
                nkg = (K + 127) // 128
                for kg in range(nkg):
                    kp = min(128, K - kg * 128)
                    lhs = tile_fn(kg, m0, nm)
                    rhs = xcols(kg, c0, nc)
                    waits = []
                    if first:
                        for (b2, w2, t2, K2, xc2, xt2) in loaded:
                            if w2 is not None:
                                waits.append(w2)
                            if xt2 is not None:
                                waits.append(xt2)
                        first = False
                    if idx == 0 and ftok is not None:
                        waits.append(ftok)
                    stop = (idx == total - 1) and extra_rows is None
                    tok = P.op("pe", lambda e, o=pa, l=lhs, r=rhs, st=(idx == 0), sp=stop:
                               e.matmul(o, l, r, start=st, stop=sp),
                               waits=waits, sig=stop)
                    idx += 1
            if extra_rows is not None:
                lrow = extra_rows[0](m0, nm)
                rrow = extra_rows[1](c0, nc)
                w = [extra_rows[2]] if (len(extra_rows) > 2 and extra_rows[2]) else []
                tok = P.op("pe", lambda e, o=pa, l=lrow, r=rrow:
                           e.matmul(o, l, r, start=False, stop=True),
                           waits=w, sig=True)
            last_pe = tok
            free_tok = out_cb(m0 // 128, m0, nm, c0, nc, pa, tok)
            E.bank_free(bi, free_tok)
        for (bidx, wtok, tile_fn, K, xcols, x_tok) in loaded:
            if bidx is not None:
                E.wb_tok[bidx] = last_pe


def bcast_row(P, E, row_bf_ap, ncols, waits=()):
    bi, ps, ftok = E.bank()
    w = list(waits)
    if ftok is not None:
        w.append(ftok)
    tok = P.op("pe", lambda e, o=ps[:, 0:ncols], r=row_bf_ap:
               e.matmul(o, E.ONE_R[:, 0:128], r, start=True, stop=True),
               waits=w, sig=True)
    return bi, ps[:, 0:ncols], tok


def emit_ln(P, E, xb, dst, ncols=T, affine=None, src=None, src_tok=None,
            eps=1e-5, ngroups=G, sq_scr=None, apply_src=None):
    nD = ngroups * 128
    ctoks = [src_tok] if src_tok else []
    if src is not None:
        ctoks = []
        for g in range(ngroups):
            w0 = [src_tok] if (g == 0 and src_tok) else []
            if g == 0 and getattr(E, "init_tok", None):
                w0.append(E.init_tok)
            tk = P.op("dve", lambda e, o=xb(g), i=src(g): e.tensor_copy(o, i),
                      waits=w0, sig=True)
        ctoks = [tk]
    sq = sq_scr if sq_scr is not None else E.scrb(512)
    lasttk = None
    for c0 in range(0, ncols, 512):
        nc = min(512, ncols - c0)
        bi1, ps1, ft1 = E.bank()
        bi2, ps2, ft2 = E.bank()
        mtok2 = None
        for g in range(ngroups):
            aw = list(ctoks if g == 0 else [])
            if g == 0 and ft2 is not None:
                aw.append(ft2)
            if mtok2 is not None:
                aw.append(mtok2)
            atok = P.op("act", lambda e, o=sq[:, 0:nc], i=xb(g)[:, c0:c0 + nc]:
                        e.activation(o, i, AF.Square), waits=aw, sig=True)
            w1 = list(ctoks if g == 0 else [])
            if g == 0 and ft1 is not None:
                w1.append(ft1)
            P.op("pe", lambda e, o=ps1[:, 0:nc], r=xb(g)[:, c0:c0 + nc],
                 st=(g == 0), sp=(g == ngroups - 1):
                 e.matmul(o, E.ONE_L[:], r, start=st, stop=sp),
                 waits=w1, sig=(g == ngroups - 1))
            mtok2 = P.op("pe", lambda e, o=ps2[:, 0:nc], r=sq[:, 0:nc],
                         st=(g == 0), sp=(g == ngroups - 1):
                         e.matmul(o, E.ONE_L[:], r, start=st, stop=sp),
                         waits=[atok], sig=True)
        sum_tok = mtok2
        mu = E.ROWS[:, 0:nc]
        var = E.ROWS[:, 512:512 + nc]
        rstd = E.ROWS[:, 1024:1024 + nc]
        m2 = E.ROWS[:, 1536:1536 + nc]
        tmp = E.ROWS[:, 2048:2048 + nc]
        P.op("dve", lambda e, o=mu, i=ps1[0:1, 0:nc]:
             e.tensor_scalar_mul(o, i, 1.0 / nD), waits=[sum_tok])
        P.op("dve", lambda e, o=var, i=ps2[0:1, 0:nc]:
             e.tensor_scalar_mul(o, i, 1.0 / nD))
        P.op("dve", lambda e, o=tmp, a=mu, b=mu: e.tensor_tensor(o, a, b, ALU.mult))
        P.op("dve", lambda e, o=var, a=var, b=tmp:
             e.tensor_tensor(o, a, b, ALU.subtract), sig=True)
        eps_ap, eps_tok = E.constf(eps)
        atk = P.op("act", lambda e, o=rstd, i=var:
                   e.activation(o, i, AF.Sqrt, bias=eps_ap[0:1, :]),
                   waits=[("s_dve", P.sems["s_dve"][1]), eps_tok], sig=True)
        P.op("dve", lambda e, o=rstd, i=rstd: e.reciprocal(o, i), waits=[atk])
        P.op("dve", lambda e, o=m2, a=mu, b=rstd: e.tensor_tensor(o, a, b, ALU.mult))
        P.op("dve", lambda e, o=m2, i=m2: e.tensor_scalar_mul(o, i, -1.0))
        rb = E.ROWSB[:, 0:nc]
        m2b = E.ROWSB[:, 512:512 + nc]
        P.op("dve", lambda e, o=rb, i=rstd: e.tensor_copy(o, i))
        rowtok = P.op("dve", lambda e, o=m2b, i=m2: e.tensor_copy(o, i), sig=True)
        b1, prb, t1 = bcast_row(P, E, rb, nc, waits=[rowtok])
        b2, pm2, t2 = bcast_row(P, E, m2b, nc, waits=[rowtok])
        for g in range(ngroups):
            dv = dst(g)[:, c0:c0 + nc]
            sv = (apply_src or xb)(g)[:, c0:c0 + nc]
            P.op("dve", lambda e, o=dv, a=sv, b=prb: e.tensor_tensor(o, a, b, ALU.mult),
                 waits=[t1, t2] if g == 0 else ())
            sig_here = affine is None
            lasttk = P.op("dve", lambda e, o=dv, a=dv, b=pm2:
                          e.tensor_tensor(o, a, b, ALU.add), sig=sig_here)
            if affine is not None:
                ga, ba = affine
                lasttk = P.op("dve", lambda e, o=dv, i=dv, s1=ga[:, g:g + 1], s2=ba[:, g:g + 1]:
                              e.tensor_scalar(o, i, s1, s2, ALU.mult, ALU.add), sig=True)
        E.bank_free(b1, lasttk)
        E.bank_free(b2, lasttk)
        E.bank_free(bi1, lasttk)
        E.bank_free(bi2, lasttk)
    return lasttk



def emit_l3_ffn(P, E, W):
    E.scr_reset()
    ln_tok = emit_ln(P, E, lambda g: E.xn(g), lambda g: E.xn(g),
                     src=lambda g: E.xres(g))
    xr = E.scrb(G * TC)
    xk = E.scrb(G * TC)
    rr = E.scrb(G * TC)
    xrv = lambda g: xr[:, g * TC:(g + 1) * TC]
    xkv = lambda g: xk[:, g * TC:(g + 1) * TC]
    rv = lambda g: rr[:, g * TC:(g + 1) * TC]
    mu_r, mu_k = W["mu_r"], W["mu_k"]
    for c0 in range(0, T, TC):
        x_tok = None
        for g in range(G):
            P.op("dve", lambda e, o=xkv(g), a=E.xn(g, c0, TC), b=E.xn_shift(g, 1, c0, TC):
                 e.tensor_tensor(o, a, b, ALU.subtract),
                 waits=[ln_tok] if (g == 0 and c0 == 0) else ())
            P.op("dve", lambda e, o=xrv(g), i=xkv(g), s=mu_r[:, g:g + 1],
                 b=E.xn_shift(g, 1, c0, TC):
                 e.scalar_tensor_tensor(o, i, s, b, ALU.mult, ALU.add))
            x_tok = P.op("dve", lambda e, o=xkv(g), i=xkv(g), s=mu_k[:, g:g + 1],
                         b=E.xn_shift(g, 1, c0, TC):
                         e.scalar_tensor_tensor(o, i, s, b, ALU.mult, ALU.add), sig=True)
        ct = [(0, TC)]
        last = {}

        def out_r(mg, m0, nm, cc0, ncc, pa, tok):
            tk = P.op("act", lambda e, o=rv(mg)[:, cc0:cc0 + ncc], i=pa:
                      e.activation(o, i, AF.Sigmoid), waits=[tok], sig=True)
            last["r"] = tk
            return tk
        mm_fm(P, E, [(W["W_rT"], lambda kg, cc0, ncc: xrv(kg)[:, cc0:cc0 + ncc], x_tok)],
              out_r, M=D, col_tiles=ct)

        def out_k(mg, m0, nm, cc0, ncc, pa, tok):
            w = [tok] + ([last.pop("r")] if "r" in last else [])
            t1 = P.op("dve", lambda e, o=pa, i=pa: e.tensor_scalar_max(o, i, 0.0),
                      waits=w, sig=True)
            tk = P.op("act", lambda e, o=xrv(mg)[:, cc0:cc0 + ncc], i=pa:
                      e.activation(o, i, AF.Square), waits=[t1], sig=True)
            last["k"] = tk
            return tk
        mm_fm(P, E, [(W["W_kT"], lambda kg, cc0, ncc: xkv(kg)[:, cc0:cc0 + ncc], None)],
              out_k, M=D, col_tiles=ct)

        def out_v(mg, m0, nm, cc0, ncc, pa, tok):
            P.op("dve", lambda e, o=pa, a=pa, b=rv(mg)[:, cc0:cc0 + ncc]:
                 e.tensor_tensor(o, a, b, ALU.mult), waits=[tok])
            return P.op("dve", lambda e, o=E.xres(mg, c0 + cc0, ncc), i=pa,
                        b=E.xres(mg, c0 + cc0, ncc):
                        e.scalar_tensor_tensor(o, i, 0.5, b, ALU.mult, ALU.add),
                        sig=True)
        mm_fm(P, E, [(W["W_vT"], lambda kg, cc0, ncc: xrv(kg)[:, cc0:cc0 + ncc], last.pop("k"))],
              out_v, M=D, col_tiles=ct)
    P.barrier()


def emit_l0ffn_build(P, E, W):
    a_t = E.SCRF[0:4, 0:2048]
    b_t = E.SCRF[0:4, 2048:2560]
    t1 = P.dma("sp", a_t[:, 0:1024], W["A_R"])
    t2 = P.dma("sp", a_t[:, 1024:2048], W["A_I"])
    t3 = P.dma("sp", b_t[:, 0:256], W["B_R"])
    t4 = P.dma("sp", b_t[:, 256:512], W["B_I"])
    wblk = E.SCRB[:, 0:256]
    wneg = E.SCRB[:, 256:512]
    dtoks = [t1, t2, t3, t4]
    last_dma = []
    for mi, mat in enumerate(("R", "I")):
        ao = 1024 * mi
        bo = 256 * mi
        for ch in range(8):
            bi, ps, ftok = E.bank()
            pa = ps[:, 0:256]
            w = list(dtoks) if (mi == 0 and ch == 0) else []
            if ftok is not None:
                w.append(ftok)
            mtok = P.op("pe", lambda e, o=pa, l=a_t[:, ao + ch * 128: ao + (ch + 1) * 128],
                        r=b_t[:, bo:bo + 256]:
                        e.matmul(o, l, r, start=True, stop=True), waits=w, sig=True)
            ctok = P.op("dve", lambda e, o=wblk, i=pa: e.tensor_copy(o, i),
                        waits=[mtok] + last_dma, sig=True)
            E.bank_free(bi, ctok)
            ntok = None
            if mat == "I":
                ntok = P.op("dve", lambda e, o=wneg, i=wblk:
                            e.tensor_single_scalar(o, i, -1.0, ALU.mult), sig=True)
            last_dma = []
            targets = [("W0R" if mat == "R" else "W0I", wblk, ctok)]
            if mat == "I":
                targets.append(("W0In", wneg, ntok))
            for name, srctile, tok in targets:
                dst_full = W[name]
                for b_lo in range(4):
                    b = ch * 4 + b_lo
                    dst = dst_full.rearrange("(d g) (b e) -> d g b e", g=16, e=16)
                    dst = dst[:, :, b, :]
                    src = srctile[b_lo * 32:(b_lo + 1) * 32, :].rearrange(
                        "d (g e) -> d g e", g=16)
                    last_dma.append(P.dma("sp", dst, src, waits=[tok]))
    return last_dma


def emit_l0_ffn(P, E, W, build=False):
    E.scr_reset()
    btoks = []
    if build:
        btoks = emit_l0ffn_build(P, E, W)
        E.scr_reset()
    ln_tok = emit_ln(P, E, lambda g: E.xn(g), lambda g: E.xn(g),
                     src=lambda g: E.xres(g))
    xcr = lambda kg, c0, nc: E.xn(kg, c0, nc)
    xci = lambda kg, c0, nc: E.xn(4 + kg, c0, nc)
    br = W["bias_real"]
    bi_ = W["bias_imag"]

    def mk_cb(goff, bias_col):
        def cb(mg, m0, nm, c0, nc, pa, tok):
            atok = P.op("act", lambda e, o=pa, i=pa, b=bias_col[:, mg:mg + 1]:
                        e.activation(o, i, AF.Identity, bias=b), waits=[tok], sig=True)
            return P.op("dve", lambda e, o=E.xres(goff + mg, c0, nc), i=pa,
                        b=E.xres(goff + mg, c0, nc):
                        e.scalar_tensor_tensor(o, i, 0.5, b, ALU.mult, ALU.add),
                        waits=[atok], sig=True)
        return cb
    wt = btoks[-1] if btoks else None
    mm_fm(P, E, [(W["W0R"], xcr, ln_tok), (W["W0In"], xci, wt)],
          mk_cb(0, br), M=512)
    mm_fm(P, E, [(W["W0I"], xcr, None), (W["W0R"], xci, None)],
          mk_cb(4, bi_), M=512)
    P.barrier()


def emit_l3_attn(P, E, W):
    DT = 0.01
    E.scr_reset()
    ln_tok = emit_ln(P, E, lambda g: E.xn(g), lambda g: E.xn(g),
                     src=lambda g: E.xres(g))
    Tc = 256
    n64 = lambda: E.scrf(Tc)
    OM, AS, SS, DS = n64(), n64(), n64(), n64()
    A0, B0, C0, DD0 = n64(), n64(), n64(), n64()
    A1, B1, C1, DD1 = n64(), n64(), n64(), n64()
    FZ0, FY0, FZ1, FY1 = n64(), n64(), n64(), n64()
    TMP = n64()
    ST = E.scrf(2)
    FZB = E.scrb(Tc)
    FYB = E.scrb(Tc)
    h = lambda a: a[0:64, :]
    one_ap, one_tok = E.constf(1.0)
    P.op("dve", lambda e: e.memset(ST[0:64, :], 0.0), sig=True)
    b_om, b_z, b_B = W["b_omega"], W["b_zeta"], W["b_B"]
    tk = [None]

    def chain(eng, fn, extra=()):
        tk[0] = P.op(eng, fn, waits=[tk[0]] + list(extra), sig=True)
        return tk[0]

    for c0 in range(0, T, Tc):
        ctl = [(c0, Tc)]

        def om_cb(mg, m0, nm, cc0, nc, pa, tok):
            chain("act", lambda e, o=h(TMP), i=pa[0:64, :], b=b_om[0:64, :]:
                  e.activation(o, i, AF.Abs, bias=b), extra=[tok])
            chain("act", lambda e, o=h(TMP), i=h(TMP):
                  e.activation(o, i, AF.Exp, scale=-1.0))
            chain("act", lambda e, o=h(TMP), i=h(TMP), b=one_ap[0:64, :]:
                  e.activation(o, i, AF.Ln, bias=b), extra=[one_tok])
            chain("dve", lambda e, o=h(OM), i=pa[0:64, :], b=b_om[0:64, :]:
                  e.tensor_scalar(o, i, b, 0.0, ALU.add, ALU.max))
            chain("dve", lambda e, o=h(OM), a=h(OM), b2=h(TMP):
                  e.tensor_tensor(o, a, b2, ALU.add))
            return chain("dve", lambda e, o=h(OM), i=h(OM):
                         e.tensor_scalar(o, i, 1e-4, 100.0, ALU.max, ALU.min))
        mm_fm(P, E, [(W["W_omT"], lambda kg, a, n: E.xn(kg, a, n),
                      ln_tok if c0 == 0 else None)], om_cb, M=64, col_tiles=ctl)

        def z_cb(mg, m0, nm, cc0, nc, pa, tok):
            chain("act", lambda e, o=h(TMP), i=pa[0:64, :], b=b_z[0:64, :]:
                  e.activation(o, i, AF.Sigmoid, bias=b), extra=[tok])
            return chain("dve", lambda e, o=h(DS), i=h(TMP):
                         e.tensor_scalar(o, i, -1.0, 1.0, ALU.mult, ALU.add))
        mm_fm(P, E, [(W["W_zT"], lambda kg, a, n: E.xn(kg, a, n), None)],
              z_cb, M=64, col_tiles=ctl)

        chain("act", lambda e, o=h(AS), i=h(OM): e.activation(o, i, AF.Square))
        chain("dve", lambda e, o=h(TMP), i=h(AS):
              e.tensor_scalar(o, i, DT * DT, 1.0, ALU.mult, ALU.add))
        chain("dve", lambda e, o=h(SS), i=h(TMP): e.reciprocal(o, i))
        chain("dve", lambda e, o=h(A0), a=h(DS), b2=h(SS):
              e.tensor_tensor(o, a, b2, ALU.mult))
        chain("dve", lambda e, o=h(DD0), i=h(A0): e.tensor_copy(o, i))
        chain("dve", lambda e, o=h(B0), i=h(AS), b2=h(A0):
              e.scalar_tensor_tensor(o, i, -DT, b2, ALU.mult, ALU.mult))
        chain("dve", lambda e, o=h(C0), i=h(A0): e.tensor_scalar_mul(o, i, DT))

        def f_cb(mg, m0, nm, cc0, nc, pa, tok):
            dst = FZ0 if m0 == 0 else FY0
            return chain("dve", lambda e, o=h(dst), i=pa[0:64, :],
                         b=b_B[m0:m0 + 64, :]:
                         e.tensor_scalar(o, i, b, None, ALU.add), extra=[tok])
        mm_fm(P, E, [(W["W_BT"], lambda kg, a, n: E.xn(kg, a, n), None)],
              f_cb, M=128, Ms=[(0, 64), (64, 64)], col_tiles=ctl)

        cur = (A0, B0, C0, DD0, FZ0, FY0)
        nxt = (A1, B1, C1, DD1, FZ1, FY1)
        for k in range(8):
            s = 1 << k
            (a, b, c, dd, fz, fy) = cur
            (an, bn, cn, ddn, fzn, fyn) = nxt
            n = Tc - s
            sh = lambda p: p[0:64, 0:n]
            up = lambda p: p[0:64, s:Tc]
            for (src, dst) in ((a, an), (b, bn), (c, cn), (dd, ddn),
                               (fz, fzn), (fy, fyn)):
                chain("dve", lambda e, o=dst[0:64, 0:s], i=src[0:64, 0:s]:
                      e.tensor_copy(o, i))
            for (dst, u1, v1, u2, v2) in (
                    (an, a, a, b, c), (bn, a, b, b, dd),
                    (cn, c, a, dd, c), (ddn, c, b, dd, dd)):
                chain("dve", lambda e, o=up(dst), x1=up(u1), x2=sh(v1):
                      e.tensor_tensor(o, x1, x2, ALU.mult))
                chain("dve", lambda e, o=up(TMP), x1=up(u2), x2=sh(v2):
                      e.tensor_tensor(o, x1, x2, ALU.mult))
                chain("dve", lambda e, o=up(dst), x1=up(dst), x2=up(TMP):
                      e.tensor_tensor(o, x1, x2, ALU.add))
            for (dst, u1, u2, f2) in ((fzn, a, b, fz), (fyn, c, dd, fy)):
                chain("dve", lambda e, o=up(dst), x1=up(u1), x2=sh(fz):
                      e.tensor_tensor(o, x1, x2, ALU.mult))
                chain("dve", lambda e, o=up(TMP), x1=up(u2), x2=sh(fy):
                      e.tensor_tensor(o, x1, x2, ALU.mult))
                chain("dve", lambda e, o=up(dst), x1=up(dst), x2=up(TMP):
                      e.tensor_tensor(o, x1, x2, ALU.add))
                chain("dve", lambda e, o=up(dst), x1=up(dst), x2=up(f2):
                      e.tensor_tensor(o, x1, x2, ALU.add))
            cur, nxt = nxt, cur
        (a, b, c, dd, fz, fy) = cur
        chain("dve", lambda e, o=h(fz), i=h(a), s1=ST[0:64, 0:1], b2=h(fz):
              e.scalar_tensor_tensor(o, i, s1, b2, ALU.mult, ALU.add))
        chain("dve", lambda e, o=h(fz), i=h(b), s1=ST[0:64, 1:2], b2=h(fz):
              e.scalar_tensor_tensor(o, i, s1, b2, ALU.mult, ALU.add))
        chain("dve", lambda e, o=h(fy), i=h(c), s1=ST[0:64, 0:1], b2=h(fy):
              e.scalar_tensor_tensor(o, i, s1, b2, ALU.mult, ALU.add))
        chain("dve", lambda e, o=h(fy), i=h(dd), s1=ST[0:64, 1:2], b2=h(fy):
              e.scalar_tensor_tensor(o, i, s1, b2, ALU.mult, ALU.add))
        chain("dve", lambda e, o=ST[0:64, 0:1], i=fz[0:64, Tc - 1:Tc]:
              e.tensor_copy(o, i))
        chain("dve", lambda e, o=ST[0:64, 1:2], i=fy[0:64, Tc - 1:Tc]:
              e.tensor_copy(o, i))
        chain("dve", lambda e, o=FZB[0:64, :], i=h(fz): e.tensor_copy(o, i))
        x_tok = chain("dve", lambda e, o=FYB[0:64, :], i=h(fy): e.tensor_copy(o, i))

        def out_cb(mg, m0, nm, cc0, nc, pa, tok):
            return P.op("dve", lambda e, o=E.xres(mg, cc0, nc), a=E.xres(mg, cc0, nc), b2=pa:
                        e.tensor_tensor(o, a, b2, ALU.add), waits=[tok], sig=True)
        mm_fm(P, E, [(W["C_T"][0:64, :], lambda kg, cc0, n: FZB[0:64, 0:n], x_tok),
                     (W["C_T"][64:128, :], lambda kg, cc0, n: FYB[0:64, 0:n], None)],
              out_cb, M=D, col_tiles=ctl)
        mm_fm(P, E, [(W["DmatT"], lambda kg, cc0, n: E.xn(kg, cc0, n), None)],
              out_cb, M=D, col_tiles=ctl)
    P.barrier()


def emit_l0_attn(P, E, W):
    E.scr_reset()
    ln_tok = emit_ln(P, E, lambda g: E.xn(g), lambda g: E.xn(g),
                     src=lambda g: E.xres(g))
    Tc = 256
    HG = 12
    ALPHA = E.scrf(HG * Tc)
    XBETA = E.scrf(HG * Tc)
    GG = E.scrb(HG * Tc)
    XH = E.scrb(HG * (Tc + 3))
    XC = E.scrb(HG * Tc)
    GHB = E.scrb(HG * Tc)
    al = lambda g: ALPHA[:, g * Tc:(g + 1) * Tc]
    xb = lambda g: XBETA[:, g * Tc:(g + 1) * Tc]
    gg = lambda g: GG[:, g * Tc:(g + 1) * Tc]
    xh = lambda g, j=3, n=Tc: XH[:, g * (Tc + 3) + j: g * (Tc + 3) + j + n]
    xc = lambda g: XC[:, g * Tc:(g + 1) * Tc]
    gh = lambda g: GHB[:, g * Tc:(g + 1) * Tc]
    lam, convw, gpb, ipb = W["lam"], W["convw"], W["gp_b"], W["ip_b"]
    c8 = E.cons(HG)
    HC = E.cons(HG)
    one_ap, one_tok = E.constf(1.0)
    tk = [None]

    def chain(eng, fn, extra=()):
        tk[0] = P.op(eng, fn, waits=[tk[0]] + list(extra), sig=True)
        return tk[0]

    chain("act", lambda e, o=c8, i=lam: e.activation(o, i, AF.Abs), extra=[one_tok])
    chain("act", lambda e, o=c8, i=c8: e.activation(o, i, AF.Exp, scale=-1.0))
    chain("act", lambda e, o=c8, i=c8, b=one_ap: e.activation(o, i, AF.Ln, bias=b))
    chain("dve", lambda e, o=HC, i=lam: e.tensor_scalar_max(o, i, 0.0))
    chain("dve", lambda e, o=c8, a=c8, b=HC: e.tensor_tensor(o, a, b, ALU.add))
    chain("dve", lambda e, o=c8, i=c8: e.tensor_scalar_mul(o, i, -8.0))
    chain("dve", lambda e, o=HC, i=HC: e.memset(o, 0.0))
    chain("dve", lambda e, o=XH[:], i=XH[:]: e.memset(o, 0.0))

    for c0 in range(0, T, Tc):
        ctl = [(c0, Tc)]
        if c0 > 0:
            for g in range(HG):
                chain("dve", lambda e, o=xh(g, 0, 3), i=xh(g, Tc, 3):
                      e.tensor_copy(o, i))

        def proj_cb(mg, m0, nm, cc0, nc, pa, tok):
            if mg < HG:
                return chain("act", lambda e, o=gg(mg), i=pa:
                             e.activation(o, i, AF.Gelu), extra=[tok])
            return chain("dve", lambda e, o=xh(mg - HG), i=pa:
                         e.tensor_copy(o, i), extra=[tok])
        mm_fm(P, E, [(W["proj_wT"], lambda kg, a, n: E.xn(kg, a, n),
                      ln_tok if c0 == 0 else None)], proj_cb, M=3072,
              col_tiles=ctl)
        for g in range(HG):
            chain("dve", lambda e, o=xc(g), i=xh(g, 0), s=convw[:, g * 4:g * 4 + 1]:
                  e.tensor_single_scalar(o, i, s, ALU.mult))
            for j in range(1, 4):
                chain("dve", lambda e, o=xc(g), i=xh(g, j), s=convw[:, g * 4 + j:g * 4 + j + 1],
                      b=xc(g): e.scalar_tensor_tensor(o, i, s, b, ALU.mult, ALU.add))
            chain("dve", lambda e, o=xc(g), i=xc(g), s=W["conv_b"][:, g:g + 1]:
                  e.tensor_single_scalar(o, i, s, ALU.add))

        def rec_cb(mg, m0, nm, cc0, nc, pa, tok):
            bi2, ps2, ft2 = E.bank()
            chain("act", lambda e, o=ps2[:, 0:nc], i=pa, b=gpb[:, mg:mg + 1]:
                  e.activation(o, i, AF.Sigmoid, bias=b),
                  extra=[tok] + ([ft2] if ft2 else []))
            chain("dve", lambda e, o=al(mg), i=ps2[:, 0:nc], s=c8[:, mg:mg + 1]:
                  e.tensor_single_scalar(o, i, s, ALU.mult))
            E.bank_free(bi2, tk[0])
            tka = chain("act", lambda e, o=al(mg), i=al(mg): e.activation(o, i, AF.Exp))
            return tka
        mm_fm(P, E, [(W["gp_wT"], lambda kg, a, n: xc(kg), tk[0])], rec_cb,
              M=1536, col_tiles=ctl)

        def inp_cb(mg, m0, nm, cc0, nc, pa, tok):
            bi2, ps2, ft2 = E.bank()
            chain("act", lambda e, o=xb(mg), i=al(mg): e.activation(o, i, AF.Square),
                  extra=[tok] + ([ft2] if ft2 else []))
            chain("dve", lambda e, o=xb(mg), i=xb(mg):
                  e.tensor_scalar(o, i, -1.0, 1.0 + 1e-6, ALU.mult, ALU.add))
            chain("act", lambda e, o=xb(mg), i=xb(mg): e.activation(o, i, AF.Sqrt))
            chain("act", lambda e, o=ps2[:, 0:nc], i=pa, b=ipb[:, mg:mg + 1]:
                  e.activation(o, i, AF.Sigmoid, bias=b))
            chain("dve", lambda e, o=xb(mg), a=xb(mg), b=ps2[:, 0:nc]:
                  e.tensor_tensor(o, a, b, ALU.mult))
            chain("dve", lambda e, o=ps2[:, 0:nc], i=xc(mg): e.tensor_copy(o, i))
            tkd = chain("dve", lambda e, o=xb(mg), a=xb(mg), b=ps2[:, 0:nc]:
                        e.tensor_tensor(o, a, b, ALU.mult))
            E.bank_free(bi2, tkd)
            return tkd
        mm_fm(P, E, [(W["ip_wT"], lambda kg, a, n: xc(kg), None)], inp_cb,
              M=1536, col_tiles=ctl)

        for g in range(HG):
            bi2, ps2, ft2 = E.bank()
            chain("dve", lambda e, o=ps2[:, 0:Tc], a=al(g), d1=xb(g), init=HC[:, g:g + 1]:
                  e.tensor_tensor_scan(o, a, d1, init, ALU.mult, ALU.add),
                  extra=[ft2] if ft2 else [])
            chain("dve", lambda e, o=HC[:, g:g + 1], i=ps2[:, Tc - 1:Tc]:
                  e.tensor_copy(o, i))
            tkd = chain("dve", lambda e, o=gh(g), a=gg(g), b=ps2[:, 0:Tc]:
                        e.tensor_tensor(o, a, b, ALU.mult))
            E.bank_free(bi2, tkd)

        def out_cb(mg, m0, nm, cc0, nc, pa, tok):
            return P.op("dve", lambda e, o=E.xres(mg, cc0, nc), a=E.xres(mg, cc0, nc),
                        b=pa: e.tensor_tensor(o, a, b, ALU.add), waits=[tok], sig=True)
        mm_fm(P, E, [(W["out_wT"], lambda kg, a, n: gh(kg), tk[0])], out_cb,
              M=D, col_tiles=ctl)
    P.barrier()


def emit_l1_attn(P, E, W):
    E.scr_reset()
    ln_tok = emit_ln(P, E, lambda g: E.xn(g), lambda g: E.xn(g),
                     src=lambda g: E.xres(g))
    for g in range(G):
        ln_tok = P.op("dve", lambda e, o=E.xres(g), a=E.xres(g), b=E.xn(g):
                      e.tensor_tensor(o, a, b, ALU.add),
                      waits=[ln_tok] if g == 0 else (), sig=(g == G - 1))
    ln2 = emit_ln(P, E, lambda g: E.xn(g), lambda g: E.xn(g),
                  affine=(W["ln_g"], W["ln_b"]), src_tok=ln_tok)
    Tc = 256
    EG = 16
    Y = E.scrf(EG * Tc)
    DAs = E.scrf(Tc)
    DTBs = E.scrf(Tc)
    Hs = E.scrf(Tc)
    XU = E.scrb(EG * (Tc + 3))
    Z = E.scrb(EG * Tc)
    U2 = E.scrb(EG * Tc)
    DT_ = E.scrb(EG * Tc)
    DTU = E.scrb(EG * Tc)
    XDT = E.scrb(Tc)
    XB_ = E.scrb(Tc)
    XC_ = E.scrb(Tc)
    yv = lambda g: Y[:, g * Tc:(g + 1) * Tc]
    xu = lambda g, j=3, n=Tc: XU[:, g * (Tc + 3) + j: g * (Tc + 3) + j + n]
    zv = lambda g: Z[:, g * Tc:(g + 1) * Tc]
    u2v = lambda g: U2[:, g * Tc:(g + 1) * Tc]
    dtv = lambda g: DT_[:, g * Tc:(g + 1) * Tc]
    dtuv = lambda g: DTU[:, g * Tc:(g + 1) * Tc]
    A_ = E.cons(256)
    HCM = E.cons(256)
    one_ap, one_tok = E.constf(1.0)
    convw, conv_b, b_dt, Dp, rs = (W["convw"], W["conv_b"], W["b_dt"],
                                   W["D_param"], W["rs"])
    tk = [None]

    def chain(eng, fn, extra=()):
        tk[0] = P.op(eng, fn, waits=[tk[0]] + list(extra), sig=True)
        return tk[0]

    chain("act", lambda e, o=A_, i=W["ALOGC"]: e.activation(o, i, AF.Exp),
          extra=[one_tok])
    chain("dve", lambda e, o=A_, i=A_: e.tensor_scalar_mul(o, i, -1.0))
    chain("dve", lambda e, o=HCM, i=HCM: e.memset(o, 0.0))
    chain("dve", lambda e, o=XU[:], i=XU[:]: e.memset(o, 0.0))

    for c0 in range(0, T, Tc):
        ctl = [(c0, Tc)]
        if c0 > 0:
            for g in range(EG):
                chain("dve", lambda e, o=xu(g, 0, 3), i=xu(g, Tc, 3):
                      e.tensor_copy(o, i))

        def in_cb(mg, m0, nm, cc0, nc, pa, tok):
            if mg < EG:
                return chain("dve", lambda e, o=xu(mg), i=pa: e.tensor_copy(o, i),
                             extra=[tok])
            return chain("act", lambda e, o=zv(mg - EG), i=pa:
                         e.activation(o, i, AF.Silu), extra=[tok])
        mm_fm(P, E, [(W["W_in"], lambda kg, a, n: E.xn(kg, a, n),
                      ln2 if c0 == 0 else None)], in_cb, M=4096, col_tiles=ctl)
        for g in range(EG):
            bi2, ps2, ft2 = E.bank()
            chain("dve", lambda e, o=ps2[:, 0:Tc], i=xu(g, 0), s=convw[:, g * 4:g * 4 + 1]:
                  e.tensor_single_scalar(o, i, s, ALU.mult),
                  extra=[ft2] if ft2 else [])
            for j in range(1, 4):
                chain("dve", lambda e, o=ps2[:, 0:Tc], i=xu(g, j),
                      s=convw[:, g * 4 + j:g * 4 + j + 1], b=ps2[:, 0:Tc]:
                      e.scalar_tensor_tensor(o, i, s, b, ALU.mult, ALU.add))
            tka = chain("act", lambda e, o=u2v(g), i=ps2[:, 0:Tc], b=conv_b[:, g:g + 1]:
                        e.activation(o, i, AF.Silu, bias=b))
            E.bank_free(bi2, tka)

        def dbl_cb(dst):
            def cb(mg, m0, nm, cc0, nc, pa, tok):
                return chain("dve", lambda e, o=dst[0:nm, 0:nc], i=pa:
                             e.tensor_copy(o, i), extra=[tok])
            return cb
        mm_fm(P, E, [(W["W_xp_dt"], lambda kg, a, n: u2v(kg), tk[0])],
              dbl_cb(XDT), M=64, col_tiles=ctl)
        mm_fm(P, E, [(W["W_xp_B"], lambda kg, a, n: u2v(kg), None)],
              dbl_cb(XB_), M=16, col_tiles=ctl)
        mm_fm(P, E, [(W["W_xp_C"], lambda kg, a, n: u2v(kg), None)],
              dbl_cb(XC_), M=16, col_tiles=ctl)

        def dt_cb(mg, m0, nm, cc0, nc, pa, tok):
            chain("act", lambda e, o=DAs, i=pa, b=b_dt[:, mg:mg + 1]:
                  e.activation(o, i, AF.Abs, bias=b), extra=[tok])
            chain("act", lambda e, o=DAs, i=DAs: e.activation(o, i, AF.Exp, scale=-1.0))
            chain("act", lambda e, o=DAs, i=DAs, b=one_ap:
                  e.activation(o, i, AF.Ln, bias=b))
            chain("dve", lambda e, o=dtv(mg), i=pa, b=b_dt[:, mg:mg + 1]:
                  e.tensor_scalar(o, i, b, 0.0, ALU.add, ALU.max))
            tkd = chain("dve", lambda e, o=dtv(mg), a=dtv(mg), b=DAs:
                        e.tensor_tensor(o, a, b, ALU.add))
            return tkd
        mm_fm(P, E, [(W["W_dt"], lambda kg, a, n: XDT[0:64, 0:n], tk[0])],
              dt_cb, M=2048, col_tiles=ctl)
        for g in range(EG):
            chain("dve", lambda e, o=dtuv(g), a=dtv(g), b=u2v(g):
                  e.tensor_tensor(o, a, b, ALU.mult))
        chain("dve", lambda e, o=Y[:], i=Y[:]: e.memset(o, 0.0))
        for n in range(16):
            biB, psB, ftB = E.bank()
            biC, psC, ftC = E.bank()
            chain("pe", lambda e, o=psB[:, 0:Tc], l=W["SELB"][:, n * 128:(n + 1) * 128],
                  r=XB_[0:16, 0:Tc]: e.matmul(o, l, r, start=True, stop=True),
                  extra=[ftB] if ftB else [])
            chain("pe", lambda e, o=psC[:, 0:Tc], l=W["SELB"][:, n * 128:(n + 1) * 128],
                  r=XC_[0:16, 0:Tc]: e.matmul(o, l, r, start=True, stop=True),
                  extra=[ftC] if ftC else [])
            for g in range(EG):
                col = n * 16 + g
                chain("dve", lambda e, o=DAs, i=dtv(g), s=A_[:, col:col + 1]:
                      e.tensor_single_scalar(o, i, s, ALU.mult))
                chain("act", lambda e, o=DAs, i=DAs: e.activation(o, i, AF.Exp))
                chain("dve", lambda e, o=DTBs, a=dtuv(g), b=psB[:, 0:Tc]:
                      e.tensor_tensor(o, a, b, ALU.mult))
                chain("dve", lambda e, o=Hs, a=DAs, d1=DTBs, init=HCM[:, col:col + 1]:
                      e.tensor_tensor_scan(o, a, d1, init, ALU.mult, ALU.add))
                chain("dve", lambda e, o=HCM[:, col:col + 1], i=Hs[:, Tc - 1:Tc]:
                      e.tensor_copy(o, i))
                chain("dve", lambda e, o=Hs, a=Hs, b=psC[:, 0:Tc]:
                      e.tensor_tensor(o, a, b, ALU.mult))
                chain("dve", lambda e, o=yv(g), a=yv(g), b=Hs:
                      e.tensor_tensor(o, a, b, ALU.add))
            E.bank_free(biB, tk[0])
            E.bank_free(biC, tk[0])
        for g in range(EG):
            chain("dve", lambda e, o=yv(g), i=u2v(g), s=Dp[:, g:g + 1], b=yv(g):
                  e.scalar_tensor_tensor(o, i, s, b, ALU.mult, ALU.add))
            chain("dve", lambda e, o=yv(g), a=yv(g), b=zv(g):
                  e.tensor_tensor(o, a, b, ALU.mult))
            chain("dve", lambda e, o=dtuv(g), i=yv(g): e.tensor_copy(o, i))

        def out_cb(mg, m0, nm, cc0, nc, pa, tok):
            return P.op("dve", lambda e, o=E.xres(mg, cc0, nc), i=pa, s=rs[:, 0:1],
                        b=E.xres(mg, cc0, nc):
                        e.scalar_tensor_tensor(o, i, s, b, ALU.mult, ALU.add),
                        waits=[tok], sig=True)
        mm_fm(P, E, [(W["W_out"], lambda kg, a, n: dtuv(kg), tk[0])], out_cb,
              M=D, col_tiles=ctl)
    P.barrier()


def emit_top2(P, E, W, chain, LG, GT, nc_cols, renorm):
    idf, ltm, idb8 = W["identf"], W["LTM"], W["IDB8"]
    rt = E.rt_area
    RT = rt[:, 0:8]
    PW = rt[:, 8:72]
    PW2 = rt[:, 72:136]
    RANK = rt[:, 136:144]
    ME = rt[:, 144:152]
    M1 = rt[:, 152:153]
    DEN = rt[:, 153:154]
    R4 = rt[:, 154:158]
    R2 = rt[:, 158:160]
    pw3 = lambda a: a.rearrange("p (e f) -> p e f", e=8)
    for tt in range(nc_cols // 128):
        bi, ps, ft = E.bank()
        chain("pe", lambda e, o=ps[:, 0:8], i=LG[0:8, tt * 128:(tt + 1) * 128],
              d=idf[0:8, 0:8]: e.transpose(o, i, d), extra=[ft] if ft else [])
        chain("dve", lambda e, o=RT, i=ps[:, 0:8]: e.tensor_copy(o, i))
        E.bank_free(bi, tk_cur(P))
        b1 = RT.unsqueeze(2).broadcast_to([128, 8, 8])
        b2 = RT.unsqueeze(1).broadcast_to([128, 8, 8])
        chain("dve", lambda e, o=pw3(PW), a=b1, b=b2:
              e.tensor_tensor(o, a, b, ALU.is_lt))
        chain("dve", lambda e, o=pw3(PW2), a=b1, b=b2:
              e.tensor_tensor(o, a, b, ALU.is_equal))
        chain("dve", lambda e, o=PW2, a=PW2, b=ltm:
              e.tensor_tensor(o, a, b, ALU.mult))
        chain("dve", lambda e, o=PW, a=PW, b=PW2:
              e.tensor_tensor(o, a, b, ALU.add))
        p3 = pw3(PW)
        chain("dve", lambda e, o=p3[:, :, 0:4], a=p3[:, :, 0:4], b=p3[:, :, 4:8]:
              e.tensor_tensor(o, a, b, ALU.add))
        chain("dve", lambda e, o=p3[:, :, 0:2], a=p3[:, :, 0:2], b=p3[:, :, 2:4]:
              e.tensor_tensor(o, a, b, ALU.add))
        chain("dve", lambda e, o=RANK.unsqueeze(2), a=p3[:, :, 0:1], b=p3[:, :, 1:2]:
              e.tensor_tensor(o, a, b, ALU.add))
        chain("dve", lambda e, o=R4, a=RT[:, 0:4], b=RT[:, 4:8]:
              e.tensor_tensor(o, a, b, ALU.max))
        chain("dve", lambda e, o=R2, a=R4[:, 0:2], b=R4[:, 2:4]:
              e.tensor_tensor(o, a, b, ALU.max))
        chain("dve", lambda e, o=M1, a=R2[:, 0:1], b=R2[:, 1:2]:
              e.tensor_tensor(o, a, b, ALU.max))
        chain("dve", lambda e, o=M1, i=M1: e.tensor_scalar_mul(o, i, -1.0))
        chain("act", lambda e, o=RT, i=RT, b=M1: e.activation(o, i, AF.Exp, bias=b))
        chain("dve", lambda e, o=RANK, i=RANK:
              e.tensor_single_scalar(o, i, 1.5, ALU.is_lt))
        chain("dve", lambda e, o=ME, a=RT, b=RANK:
              e.tensor_tensor(o, a, b, ALU.mult))
        S = ME if renorm else RT
        chain("dve", lambda e, o=R4, a=S[:, 0:4], b=S[:, 4:8]:
              e.tensor_tensor(o, a, b, ALU.add))
        chain("dve", lambda e, o=R2, a=R4[:, 0:2], b=R4[:, 2:4]:
              e.tensor_tensor(o, a, b, ALU.add))
        chain("dve", lambda e, o=DEN, a=R2[:, 0:1], b=R2[:, 1:2]:
              e.tensor_tensor(o, a, b, ALU.add))
        chain("dve", lambda e, o=DEN, i=DEN: e.reciprocal(o, i))
        chain("dve", lambda e, o=ME, i=ME, s=DEN:
              e.tensor_single_scalar(o, i, s, ALU.mult))
        bi2, ps2, ft2 = E.bank()
        chain("pe", lambda e, o=ps2[0:8, 0:128], i=ME, d=idf[:]:
              e.transpose(o, i, d), extra=[ft2] if ft2 else [])
        tkd = chain("dve", lambda e, o=GT[0:8, tt * 128:(tt + 1) * 128],
                    i=ps2[0:8, 0:128]: e.tensor_copy(o, i))
        E.bank_free(bi2, tkd)
    return tk_cur(P)


def tk_cur(P):
    for nm in ("s_dve",):
        pass
    return ("s_dve", P.sems["s_dve"][1])


L2_STAGE = 99


def emit_moe(P, E, W, layer):
    E.scr_reset()
    ln_tok = emit_ln(P, E, lambda g: E.xn(g), lambda g: E.xn(g),
                     src=lambda g: E.xres(g))
    Tc = 512
    E.rt_area = E.scrf(176)
    LG = E.scrf(Tc)
    GT = E.scrb(Tc)
    GROW = E.scrb(Tc)
    tk = [None]

    def chain(eng, fn, extra=()):
        tk[0] = P.op(eng, fn, waits=[tk[0]] + list(extra), sig=True)
        return tk[0]
    if layer == 1:
        H_ = E.scrb(32 * Tc)
        ACC = E.scrf(G * Tc)
        hv = lambda g: H_[:, g * Tc:(g + 1) * Tc]
        av = lambda g: ACC[:, g * Tc:(g + 1) * Tc]
        for c0 in range(0, T, Tc):
            ctl = [(c0, Tc)]

            def rout_cb(mg, m0, nm, cc0, ncx, pa, tok):
                return chain("dve", lambda e, o=LG[0:8, 0:Tc], i=pa:
                             e.tensor_copy(o, i), extra=[tok])
            mm_fm(P, E, [(W["router_wT"], lambda kg, a, n: E.xn(kg, a, n),
                          ln_tok if c0 == 0 else None)], rout_cb, M=8,
                  col_tiles=ctl)
            emit_top2(P, E, W, chain, LG, GT, Tc, renorm=False)
            for g in range(G):
                chain("dve", lambda e, o=av(g), i=av(g): e.memset(o, 0.0))
            for ei in range(8):
                bi, ps, ft = E.bank()
                chain("pe", lambda e, o=ps[0:1, 0:Tc], l=W["IDB8"][:, ei:ei + 1],
                      r=GT[0:8, 0:Tc]: e.matmul(o, l, r, start=True, stop=True),
                      extra=[ft] if ft else [])
                chain("dve", lambda e, o=GROW[0:1, 0:Tc], i=ps[0:1, 0:Tc]:
                      e.tensor_copy(o, i))
                biw, psw, ftw = E.bank(hold=True)
                wtok = chain("pe", lambda e, o=psw[:, 0:Tc], r=GROW[0:1, 0:Tc]:
                             e.matmul(o, E.ONE_R[0:1, 0:128], r, start=True, stop=True),
                             extra=[ftw] if ftw else [])
                E.bank_free(bi, wtok)

                def h_cb(mg, m0, nm, cc0, ncx, pa, tok):
                    chain("act", lambda e, o=hv(mg), i=pa:
                          e.activation(o, i, AF.Gelu), extra=[tok])
                    return chain("dve", lambda e, o=hv(mg), a=hv(mg), b=psw[:, 0:Tc]:
                                 e.tensor_tensor(o, a, b, ALU.mult))
                mm_fm(P, E, [(W["w1T"][ei * 1024:(ei + 1) * 1024, :],
                              lambda kg, a, n: E.xn(kg, a, n), tk[0])],
                      h_cb, M=4096, col_tiles=ctl)
                E.bank_free(biw, tk[0])

                def y_cb(mg, m0, nm, cc0, ncx, pa, tok):
                    return chain("dve", lambda e, o=av(mg), a=av(mg), b=pa:
                                 e.tensor_tensor(o, a, b, ALU.add), extra=[tok])
                mm_fm(P, E, [(W["w2T"][ei * 4096:ei * 4096 + 2048, :],
                              lambda kg, a, n: hv(kg), tk[0]),
                             (W["w2T"][ei * 4096 + 2048:(ei + 1) * 4096, :],
                              lambda kg, a, n: hv(16 + kg), None)],
                      y_cb, M=D, col_tiles=ctl)
            for g in range(G):
                chain("dve", lambda e, o=E.xres(g, c0, Tc), i=av(g),
                      b=E.xres(g, c0, Tc):
                      e.scalar_tensor_tensor(o, i, 0.5, b, ALU.mult, ALU.add))
    else:
        SP = E.scrb(16 * Tc)
        CONTB = E.scrb(Tc)
        T1B = E.scrb(Tc)
        H_ = E.scrb(G * Tc)
        ACCB = E.scrb(Tc)
        LNO = E.scrb(G * Tc)
        ACC = E.scrf(Tc)
        DEC = E.scrf(G * Tc)
        spv = lambda g: SP[:, g * Tc:(g + 1) * Tc]
        hv = lambda g: H_[:, g * Tc:(g + 1) * Tc]
        dcv = lambda g: DEC[:, g * Tc:(g + 1) * Tc]
        lnv = lambda g: LNO[:, g * Tc:(g + 1) * Tc]
        for c0 in range(0, T, Tc):
            ctl = [(c0, Tc)]

            def enc_cb(mg, m0, nm, cc0, ncx, pa, tok):
                return chain("act", lambda e, o=spv(mg), i=pa,
                             b=W["enc_b"][:, mg:mg + 1]:
                             e.activation(o, i, AF.Sigmoid, bias=b), extra=[tok])
            mm_fm(P, E, [(W["enc_W"], lambda kg, a, n: E.xn(kg, a, n),
                          ln_tok if c0 == 0 else None)], enc_cb, M=2048,
                  col_tiles=ctl)

            if L2_STAGE < 1:
                continue

            def s2c_cb(mg, m0, nm, cc0, ncx, pa, tok):
                chain("dve", lambda e, o=pa, i=pa, s=W["s2c_b"][0:64, 0:1]:
                      e.tensor_single_scalar(o, i, s, ALU.add), extra=[tok])
                return chain("dve", lambda e, o=CONTB[0:64, 0:Tc], i=pa:
                             e.tensor_copy(o, i))
            mm_fm(P, E, [(W["s2c_W"], lambda kg, a, n: spv(kg), tk[0])],
                  s2c_cb, M=64, col_tiles=ctl)

            def r1_cb(mg, m0, nm, cc0, ncx, pa, tok):
                return chain("act", lambda e, o=T1B[0:64, 0:Tc], i=pa,
                             b=W["rb1"][0:64, 0:1]:
                             e.activation(o, i, AF.Tanh, bias=b), extra=[tok])
            mm_fm(P, E, [(W["rW1"], lambda kg, a, n: CONTB[0:64, 0:n], tk[0])],
                  r1_cb, M=64, col_tiles=ctl)

            def r2_cb(mg, m0, nm, cc0, ncx, pa, tok):
                chain("dve", lambda e, o=pa, i=pa, s=W["rb2"][0:8, 0:1]:
                      e.tensor_single_scalar(o, i, s, ALU.add), extra=[tok])
                return chain("dve", lambda e, o=LG[0:8, 0:Tc], i=pa:
                             e.tensor_copy(o, i))
            mm_fm(P, E, [(W["rW2"], lambda kg, a, n: T1B[0:64, 0:n], tk[0])],
                  r2_cb, M=8, col_tiles=ctl)
            if L2_STAGE < 2:
                continue
            emit_top2(P, E, W, chain, LG, GT, Tc, renorm=True)
            chain("dve", lambda e, o=ACC[0:64, 0:Tc], i=ACC[0:64, 0:Tc]:
                  e.memset(o, 0.0))
            for ei in range(8 if L2_STAGE >= 3 else 0):
                bi, ps, ft = E.bank()
                chain("pe", lambda e, o=ps[0:1, 0:Tc], l=W["IDB8"][:, ei:ei + 1],
                      r=GT[0:8, 0:Tc]: e.matmul(o, l, r, start=True, stop=True),
                      extra=[ft] if ft else [])
                chain("dve", lambda e, o=GROW[0:1, 0:Tc], i=ps[0:1, 0:Tc]:
                      e.tensor_copy(o, i))
                biw, psw, ftw = E.bank(hold=True)
                chain("pe", lambda e, o=psw[:, 0:Tc], r=GROW[0:1, 0:Tc]:
                      e.matmul(o, E.ONE_R[0:1, 0:128], r, start=True, stop=True),
                      extra=[ftw] if ftw else [])
                E.bank_free(bi, tk[0])

                def h_cb(mg, m0, nm, cc0, ncx, pa, tok):
                    chain("act", lambda e, o=hv(mg), i=pa,
                          b=W["be1"][:, ei * 8 + mg:ei * 8 + mg + 1]:
                          e.activation(o, i, AF.Relu, bias=b), extra=[tok])
                    return chain("dve", lambda e, o=hv(mg), a=hv(mg), b=psw[:, 0:Tc]:
                                 e.tensor_tensor(o, a, b, ALU.mult))
                mm_fm(P, E, [(W["We1"][ei * 64:(ei + 1) * 64, :],
                              lambda kg, a, n: CONTB[0:64, 0:n], tk[0])],
                      h_cb, M=1024, col_tiles=ctl)
                E.bank_free(biw, tk[0])
                if L2_STAGE < 4:
                    continue

                def eo_cb(mg, m0, nm, cc0, ncx, pa, tok):
                    return chain("dve", lambda e, o=ACC[0:64, 0:Tc], a=ACC[0:64, 0:Tc],
                                 b=pa: e.tensor_tensor(o, a, b, ALU.add), extra=[tok])
                mm_fm(P, E, [(W["We2"][ei * 1024:(ei + 1) * 1024, :],
                              lambda kg, a, n: hv(kg), tk[0])], eo_cb, M=64,
                      col_tiles=ctl,
                      extra_rows=(lambda m0, nm: W["BE2"][0:1, ei * 64 + m0:ei * 64 + m0 + nm],
                                  lambda a, n: GROW[0:1, 0:n], tk[0]))
            if L2_STAGE < 5:
                continue
            chain("dve", lambda e, o=ACCB[0:64, 0:Tc], i=ACC[0:64, 0:Tc]:
                  e.tensor_copy(o, i))

            def c2s_cb(mg, m0, nm, cc0, ncx, pa, tok):
                return chain("act", lambda e, o=spv(mg), i=pa,
                             b=W["c2s_b"][:, mg:mg + 1]:
                             e.activation(o, i, AF.Sigmoid, bias=b), extra=[tok])
            mm_fm(P, E, [(W["c2s_W"], lambda kg, a, n: ACCB[0:64, 0:n], tk[0])],
                  c2s_cb, M=2048, col_tiles=ctl)

            def dec_cb(mg, m0, nm, cc0, ncx, pa, tok):
                return chain("act", lambda e, o=dcv(mg), i=pa,
                             b=W["dec_b"][:, mg:mg + 1]:
                             e.activation(o, i, AF.Sigmoid, bias=b), extra=[tok])
            mm_fm(P, E, [(W["dec_W"], lambda kg, a, n: spv(kg), tk[0])],
                  dec_cb, M=D, col_tiles=ctl)
            if L2_STAGE < 6:
                continue
            lt = emit_ln(P, E, lambda g: lnv(g), lambda g: lnv(g), ncols=Tc,
                         affine=(W["ln_g"], W["ln_b"]),
                         src=lambda g: dcv(g), src_tok=tk[0], sq_scr=GROW,
                         apply_src=lambda g: dcv(g))
            tk[0] = lt
            for g in range(G):
                chain("dve", lambda e, o=E.xres(g, c0, Tc), i=lnv(g),
                      b=E.xres(g, c0, Tc):
                      e.scalar_tensor_tensor(o, i, 0.5, b, ALU.mult, ALU.add))
    P.barrier()


def emit_l2_attn(P, E, W):
    nc = E.nc
    E.scr_reset()
    ln_tok = emit_ln(P, E, lambda g: E.xn(g), lambda g: E.xn(g),
                     src=lambda g: E.xres(g))
    tk = [None]

    def chain(eng, fn, extra=()):
        tk[0] = P.op(eng, fn, waits=[tk[0]] + list(extra), sig=True)
        return tk[0]
    UTM, KTM, UF, KF, YSP = (W["UTM"], W["KTM"], W["UF"], W["KF"], W["YSP"])

    WINB = E.scrb(8192)
    BROW = E.scrb(1024)
    ESTRIP = E.scrb(512)
    t1 = P.dma("pool", WINB.rearrange("p (kg h) -> p kg h", kg=8),
               W["in_W"].rearrange("(kg p) h -> p kg h", p=128))
    t2 = P.dma("pool", BROW[0:1, :], W["in_b_row"])
    for tc in range(16):
        for hh in range(2):
            bi, ps, ft = E.bank()
            w0 = ([t1, t2, ln_tok] if (tc == 0 and hh == 0) else [])
            if ft is not None:
                w0.append(ft)
            for kg in range(8):
                P.op("pe", lambda e, o=ps[:], l=E.xn(kg, tc * 128, 128),
                     r=WINB[:, kg * 1024 + hh * 512: kg * 1024 + hh * 512 + 512],
                     st=(kg == 0): e.matmul(o, l, r, start=st, stop=False),
                     waits=w0 if kg == 0 else ())
            mtok = P.op("pe", lambda e, o=ps[:], l=E.ONE_R[0:1, 0:128],
                        r=BROW[0:1, hh * 512:hh * 512 + 512]:
                        e.matmul(o, l, r, start=False, stop=True), sig=True)
            ctok = chain("dve", lambda e, o=ESTRIP[:], i=ps[:]: e.tensor_copy(o, i),
                         extra=[mtok])
            dtok = P.dma("sp", UTM[tc * 128:(tc + 1) * 128, hh * 512:hh * 512 + 512],
                         ESTRIP[:], waits=[ctok])
            E.bank_free(bi, ctok)
            tk[0] = dtok
    utm_done = tk[0]

    E.scr_reset()
    SBF = E.SCRB.bitcast(F32)
    sf = lambda i: E.SCRF[0:32, i * 1024:(i + 1) * 1024]
    sb_ = lambda i: SBF[0:32, i * 1024:(i + 1) * 1024]
    DT_B, AR, AI, T1, T2, SQT = sf(0), sf(1), sf(2), sf(3), sf(4), sf(5)
    LR, LI, CR, CI, P64R, P64I = sb_(0), sb_(1), sb_(2), sb_(3), sb_(4), sb_(5)
    lt1 = P.dma("sp", LR[0:1, 0:1024], W["log_dt_row"])
    chain("act", lambda e, o=LR[0:1, :], i=LR[0:1, :]: e.activation(o, i, AF.Exp),
          extra=[lt1])
    chain("dve", lambda e, o=P64R[0:1, 0:32], i=P64R[0:1, 0:32]: e.memset(o, 1.0))
    bi, ps, ft = E.bank()
    for hh in range(2):
        chain("pe", lambda e, o=ps[0:32, 0:512], l=P64R[0:1, 0:32],
              r=LR[0:1, hh * 512:hh * 512 + 512]:
              e.matmul(o, l, r, start=True, stop=True),
              extra=[ft] if (hh == 0 and ft) else [])
        chain("dve", lambda e, o=DT_B[:, hh * 512:hh * 512 + 512], i=ps[0:32, 0:512]:
              e.tensor_copy(o, i))
    E.bank_free(bi, tk[0])
    la1 = P.dma("sp", AR, W["logA_T"])
    la2 = P.dma("sp", AI, W["Aim_T"])
    chain("act", lambda e, o=AR, i=AR: e.activation(o, i, AF.Exp), extra=[la1, la2])
    chain("dve", lambda e, o=AR, a=AR, b=DT_B: e.tensor_tensor(o, a, b, ALU.mult))
    chain("dve", lambda e, o=AR, i=AR: e.tensor_scalar_mul(o, i, -1.0))
    chain("dve", lambda e, o=AI, a=AI, b=DT_B: e.tensor_tensor(o, a, b, ALU.mult))
    chain("act", lambda e, o=T2, i=AR: e.activation(o, i, AF.Exp))

    def sincos(dst, phase):
        chain("dve", lambda e, o=T1, i=AI: e.tensor_scalar(
            o, i, 1.0 / (2 * np.pi), MAGIC, ALU.mult, ALU.add))
        chain("dve", lambda e, o=T1, i=T1:
              e.tensor_scalar(o, i, MAGIC, None, ALU.subtract))
        chain("dve", lambda e, o=dst, i=T1, b=AI:
              e.scalar_tensor_tensor(o, i, -6.28125, b, ALU.mult, ALU.add))
        chain("dve", lambda e, o=dst, i=T1, b=dst:
              e.scalar_tensor_tensor(o, i, -0.0019302368, b, ALU.mult, ALU.add))
        chain("dve", lambda e, o=dst, i=T1, b=dst:
              e.scalar_tensor_tensor(o, i, -6.287769e-06, b, ALU.mult, ALU.add))
        if phase != 0.0:
            chain("dve", lambda e, o=dst, i=dst:
                  e.tensor_scalar(o, i, phase, None, ALU.add))
            chain("dve", lambda e, o=SQT, i=dst:
                  e.tensor_scalar(o, i, np.pi, None, ALU.is_gt))
            chain("dve", lambda e, o=dst, i=SQT, b=dst:
                  e.scalar_tensor_tensor(o, i, -2 * np.pi, b, ALU.mult, ALU.add))
        chain("act", lambda e, o=dst, i=dst: e.activation(o, i, AF.Sin))
    sincos(LI, 0.0)
    sincos(LR, np.pi / 2)
    chain("dve", lambda e, o=LI, a=LI, b=T2: e.tensor_tensor(o, a, b, ALU.mult))
    chain("dve", lambda e, o=LR, a=LR, b=T2: e.tensor_tensor(o, a, b, ALU.mult))
    chain("dve", lambda e, o=T1, i=LR: e.tensor_scalar(o, i, -1.0, None, ALU.add))
    chain("dve", lambda e, o=T2, a=AR, b=AR: e.tensor_tensor(o, a, b, ALU.mult))
    chain("dve", lambda e, o=SQT, a=AI, b=AI: e.tensor_tensor(o, a, b, ALU.mult))
    chain("dve", lambda e, o=T2, a=T2, b=SQT: e.tensor_tensor(o, a, b, ALU.add))
    chain("dve", lambda e, o=T2, i=T2: e.reciprocal(o, i))
    chain("dve", lambda e, o=CR, a=T1, b=AR: e.tensor_tensor(o, a, b, ALU.mult))
    chain("dve", lambda e, o=SQT, a=LI, b=AI: e.tensor_tensor(o, a, b, ALU.mult))
    chain("dve", lambda e, o=CR, a=CR, b=SQT: e.tensor_tensor(o, a, b, ALU.add))
    chain("dve", lambda e, o=CR, a=CR, b=T2: e.tensor_tensor(o, a, b, ALU.mult))
    chain("dve", lambda e, o=CI, a=LI, b=AR: e.tensor_tensor(o, a, b, ALU.mult))
    chain("dve", lambda e, o=SQT, a=T1, b=AI: e.tensor_tensor(o, a, b, ALU.mult))
    chain("dve", lambda e, o=CI, a=CI, b=SQT: e.tensor_tensor(o, a, b, ALU.subtract))
    chain("dve", lambda e, o=CI, a=CI, b=T2: e.tensor_tensor(o, a, b, ALU.mult))
    c1 = P.dma("sp", T1, W["C_reT"], waits=[tk[0]])
    c2 = P.dma("sp", T2, W["C_imT"])
    chain("dve", lambda e, o=AR, a=T1, b=CR: e.tensor_tensor(o, a, b, ALU.mult),
          extra=[c1, c2])
    chain("dve", lambda e, o=SQT, a=T2, b=CI: e.tensor_tensor(o, a, b, ALU.mult))
    chain("dve", lambda e, o=AR, a=AR, b=SQT: e.tensor_tensor(o, a, b, ALU.subtract))
    chain("dve", lambda e, o=AI, a=T1, b=CI: e.tensor_tensor(o, a, b, ALU.mult))
    chain("dve", lambda e, o=SQT, a=T2, b=CR: e.tensor_tensor(o, a, b, ALU.mult))
    chain("dve", lambda e, o=AI, a=AI, b=SQT: e.tensor_tensor(o, a, b, ALU.add))
    chain("dve", lambda e, o=CR, a=AR, b=DT_B: e.tensor_tensor(o, a, b, ALU.mult))
    chain("dve", lambda e, o=CR, i=CR: e.tensor_scalar_mul(o, i, 2.0))
    chain("dve", lambda e, o=CI, a=AI, b=DT_B: e.tensor_tensor(o, a, b, ALU.mult))
    chain("dve", lambda e, o=CI, i=CI: e.tensor_scalar_mul(o, i, 2.0))
    chain("dve", lambda e, o=P64R, i=LR: e.tensor_copy(o, i))
    chain("dve", lambda e, o=P64I, i=LI: e.tensor_copy(o, i))
    for _ in range(6):
        chain("dve", lambda e, o=SQT, a=P64R, b=P64I: e.tensor_tensor(o, a, b, ALU.mult))
        chain("dve", lambda e, o=P64R, a=P64R, b=P64R: e.tensor_tensor(o, a, b, ALU.mult))
        chain("dve", lambda e, o=T1, a=P64I, b=P64I: e.tensor_tensor(o, a, b, ALU.mult))
        chain("dve", lambda e, o=P64R, a=P64R, b=T1: e.tensor_tensor(o, a, b, ALU.subtract))
        chain("dve", lambda e, o=P64I, i=SQT: e.tensor_scalar_mul(o, i, 2.0))
    if W.get("DBG2") is not None:
        chain("dve", lambda e: e.engine_nop())
        for cri in range(4):
            P.dma("sp", W["DBG2"][cri * 128:cri * 128 + 96, :],
                  E.SCRF[0:96, cri * 1024:(cri + 1) * 1024],
                  waits=[("s_dve", P.sems["s_dve"][1])])
    LAM2 = E.SCRB[:, 12288:20480]
    PPK = E.SCRB[:, 20480:24576]
    KEV = E.SCRF.bitcast(BF16)[:, 10240:10752]

    def build_powers(dst_packed, L, hoff, seed_re, seed_im, fac_re, fac_im):
        DV = dst_packed.rearrange("p (h l) -> p h l", l=L)
        for hs in range(4):
            ha = hoff + hs * 32
            FR = E.SCRF[0:32, 0:32 * L].rearrange("p (h l) -> p h l", l=L)
            FI = E.SCRF[0:32, 2048:2048 + 32 * L].rearrange("p (h l) -> p h l", l=L)
            if seed_re is None:
                chain("dve", lambda e, o=FR[:, :, 0:1], i=FR[:, :, 0:1]: e.memset(o, 1.0))
                chain("dve", lambda e, o=FI[:, :, 0:1], i=FI[:, :, 0:1]: e.memset(o, 0.0))
            else:
                chain("dve", lambda e, o=FR[:, :, 0:1],
                      i=seed_re[:, ha:ha + 32].unsqueeze(2): e.tensor_copy(o, i))
                chain("dve", lambda e, o=FI[:, :, 0:1],
                      i=seed_im[:, ha:ha + 32].unsqueeze(2): e.tensor_copy(o, i))
            pre, pim = fac_re[:, ha:ha + 32], fac_im[:, ha:ha + 32]
            m = 1
            while m < L:
                for s0 in range(0, m, 16):
                    w = min(16, m - s0)
                    br = pre.unsqueeze(2).broadcast_to([32, 32, w])
                    bi_ = pim.unsqueeze(2).broadcast_to([32, 32, w])
                    sq = E.SCRF[0:32, 4224:4224 + 32 * w].rearrange("p (h l) -> p h l", l=w)
                    chain("dve", lambda e, o=FR[:, :, m + s0:m + s0 + w],
                          a=FR[:, :, s0:s0 + w], b=br: e.tensor_tensor(o, a, b, ALU.mult))
                    chain("dve", lambda e, o=sq, a=FI[:, :, s0:s0 + w], b=bi_:
                          e.tensor_tensor(o, a, b, ALU.mult))
                    chain("dve", lambda e, o=FR[:, :, m + s0:m + s0 + w],
                          a=FR[:, :, m + s0:m + s0 + w], b=sq:
                          e.tensor_tensor(o, a, b, ALU.subtract))
                    chain("dve", lambda e, o=FI[:, :, m + s0:m + s0 + w],
                          a=FR[:, :, s0:s0 + w], b=bi_: e.tensor_tensor(o, a, b, ALU.mult))
                    chain("dve", lambda e, o=sq, a=FI[:, :, s0:s0 + w], b=br:
                          e.tensor_tensor(o, a, b, ALU.mult))
                    chain("dve", lambda e, o=FI[:, :, m + s0:m + s0 + w],
                          a=FI[:, :, m + s0:m + s0 + w], b=sq:
                          e.tensor_tensor(o, a, b, ALU.add))
                if 2 * m < L:
                    PT = E.SCRF[0:32, 4096:4224]
                    pa_, pb_, pc_, pd_ = (PT[:, 0:32], PT[:, 32:64],
                                          PT[:, 64:96], PT[:, 96:128])
                    chain("dve", lambda e, o=pc_, a=pre, b=pim:
                          e.tensor_tensor(o, a, b, ALU.mult))
                    chain("dve", lambda e, o=pa_, a=pre, b=pre:
                          e.tensor_tensor(o, a, b, ALU.mult))
                    chain("dve", lambda e, o=pd_, a=pim, b=pim:
                          e.tensor_tensor(o, a, b, ALU.mult))
                    chain("dve", lambda e, o=pa_, a=pa_, b=pd_:
                          e.tensor_tensor(o, a, b, ALU.subtract))
                    chain("dve", lambda e, o=pb_, i=pc_:
                          e.tensor_scalar_mul(o, i, 2.0))
                    pre, pim = pa_, pb_
                m *= 2
            chain("dve", lambda e, o=DV[0:32, hs * 32:(hs + 1) * 32, :], i=FR:
                  e.tensor_copy(o, i))
            chain("dve", lambda e, o=DV[32:64, hs * 32:(hs + 1) * 32, :], i=FI:
                  e.tensor_copy(o, i))

    for hc in range(8):
        h0 = hc * 128
        build_powers(LAM2, 64, h0, None, None, LR, LI)
        build_powers(PPK, 32, h0, CR, CI, P64R, P64I)
        chain("dve", lambda e, o=PPK[32:64, :], i=PPK[32:64, :]:
              e.tensor_scalar_mul(o, i, -1.0))
        for hb in range(8):
            bi, ps, ft = E.bank()
            mt = None
            for hl in range(16):
                h = hb * 16 + hl
                w0 = [tk[0]] if hl == 0 else []
                if hl == 0 and ft is not None:
                    w0.append(ft)
                mt = P.op("pe", lambda e, o=ps[0:64, hl * 32:(hl + 1) * 32],
                          l=LAM2[0:64, h * 64:(h + 1) * 64], r=PPK[0:64, h * 32:(h + 1) * 32]:
                          e.matmul(o, l, r, start=True, stop=True),
                          waits=w0, sig=(hl == 15))
            ctok = chain("dve", lambda e,
                         o=KEV[0:64, :].rearrange("p (l h) -> p l h", h=16),
                         i=ps[0:64, 0:512].rearrange("p (h l) -> p l h", l=32):
                         e.tensor_copy(o, i), extra=[mt])
            dtok = P.dma("sp", KTM.rearrange("(l1 l2) h -> l2 l1 h", l2=64)[
                :, :, h0 + hb * 16: h0 + hb * 16 + 16],
                KEV[0:64, :].rearrange("p (l h) -> p l h", h=16), waits=[ctok])
            E.bank_free(bi, ctok)
            tk[0] = dtok
    ktm_done = tk[0]

    E.scr_reset()
    TMC = E.scrb(8192)
    FEV = E.scrb(512)
    fMs = [(m, 128) for m in range(0, 4352, 128)]
    for src_sp, dst_sp, wtok in ((UTM, UF, utm_done), (KTM, KF, ktm_done)):
        for cc in range(2):
            ch0 = cc * 512
            ltok = P.dma("pool", TMC.rearrange("p (lg c) -> p lg c", lg=16),
                         src_sp[:, ch0:ch0 + 512].rearrange("(lg p) c -> p lg c", p=128),
                         waits=[wtok, tk[0]])

            def f_cb(mg, m0, nm, cc0, ncx, pa, tok):
                ct = chain("dve", lambda e, o=FEV[0:nm, 0:512], i=pa:
                           e.tensor_copy(o, i), extra=[tok])
                dt_ = P.dma("sp", dst_sp[m0:m0 + nm, ch0:ch0 + 512],
                            FEV[0:nm, 0:512], waits=[ct])
                tk[0] = dt_
                return ct
            mm_fm(P, E, [(W["FWD"], lambda kg, a, n: TMC[:, kg * 512:kg * 512 + 512],
                          ltok)], f_cb, M=4352, Ms=fMs, col_tiles=[(0, 512)])
    fwd_done = tk[0]

    E.scr_reset()
    UFS = E.scrb(4352)
    KFS = E.scrb(4352)
    TPA = E.scrb(128)
    TPB = E.scrb(128)
    UFMC = E.scrb(2048)
    YGC = E.scrb(2048)
    RH2 = [E.scrb(2048), E.scrb(2048)]
    rh_tok = [None, None]
    for hc in range(8):
        h0 = hc * 128
        u1 = P.dma("pool", UFS.rearrange("p (g h) -> p g h", g=34),
                   UF[:, h0:h0 + 128].rearrange("(g p) h -> p g h", p=128),
                   waits=[fwd_done, tk[0]])
        u2_ = P.dma("pool", KFS.rearrange("p (g h) -> p g h", g=34),
                    KF[:, h0:h0 + 128].rearrange("(g p) h -> p g h", p=128))
        for fg in range(17):
            uR = UFS[:, fg * 128:(fg + 1) * 128]
            uI = UFS[:, (17 + fg) * 128:(18 + fg) * 128]
            kR = KFS[:, fg * 128:(fg + 1) * 128]
            kI = KFS[:, (17 + fg) * 128:(18 + fg) * 128]
            w0 = [u1, u2_] if fg == 0 else []
            chain("dve", lambda e, o=TPA[:], a=uR, b=kR: e.tensor_tensor(o, a, b, ALU.mult),
                  extra=w0)
            chain("dve", lambda e, o=TPB[:], a=uI, b=kI: e.tensor_tensor(o, a, b, ALU.mult))
            chain("dve", lambda e, o=TPA[:], a=TPA[:], b=TPB[:]:
                  e.tensor_tensor(o, a, b, ALU.subtract))
            chain("dve", lambda e, o=TPB[:], a=uR, b=kI: e.tensor_tensor(o, a, b, ALU.mult))
            chain("dve", lambda e, o=uI, a=uI, b=kR: e.tensor_tensor(o, a, b, ALU.mult))
            chain("dve", lambda e, o=uI, a=uI, b=TPB[:]: e.tensor_tensor(o, a, b, ALU.add))
            chain("dve", lambda e, o=uR, i=TPA[:]: e.tensor_copy(o, i))

        def ufm_cb(mg, m0, nm, cc0, ncx, pa, tok):
            return chain("act", lambda e, o=UFMC[:, cc0:cc0 + ncx], i=pa,
                         b=W["in_b_col"][:, hc:hc + 1]:
                         e.activation(o, i, AF.Identity, bias=b), extra=[tok])
        mm_fm(P, E, [(W["in_W"][:, h0:h0 + 128], lambda kg, a, n: E.xn(kg, a, n),
                      None)], ufm_cb, M=128, Ms=[(0, 128)])
        held = [E.bank(hold=True) for _ in range(4)]
        x_rd = tk[0]
        for fg in range(34):
            rbuf = RH2[fg % 2]
            rw = [rh_tok[fg % 2]] if rh_tok[fg % 2] else []
            rtok = P.dma("pool", rbuf[:], W["INV"][fg * 128:(fg + 1) * 128, :],
                         waits=rw)
            mt = None
            for lt in range(4):
                bi, ps, ft = held[lt]
                w0 = [rtok]
                if fg == 0:
                    w0.append(x_rd)
                    if ft is not None:
                        w0.append(ft)
                mt = P.op("pe", lambda e, o=ps[:], l=UFS[:, fg * 128:(fg + 1) * 128],
                          r=rbuf[:, lt * 512:(lt + 1) * 512], st=(fg == 0), sp=(fg == 33):
                          e.matmul(o, l, r, start=st, stop=sp, skip_group_check=True),
                          waits=w0, sig=(fg == 33 or lt == 3))
            rh_tok[fg % 2] = mt
        for lt in range(4):
            bi, ps, ft = held[lt]
            chain("dve", lambda e, o=ps[:], i=UFMC[:, lt * 512:(lt + 1) * 512],
                  s=W["D_col"][:, hc:hc + 1], b=ps[:]:
                  e.scalar_tensor_tensor(o, i, s, b, ALU.mult, ALU.add),
                  extra=[mt] if lt == 0 else [])
            ctok = chain("act", lambda e, o=YGC[:, lt * 512:(lt + 1) * 512], i=ps[:]:
                         e.activation(o, i, AF.Gelu))
            E.bank_free(bi, ctok)
        dtok = P.dma("sp", YSP[h0:h0 + 128, :], YGC[:], waits=[tk[0]])
        tk[0] = dtok
    inv_done = tk[0]
    inv_done = tk[0]

    E.scr_reset()
    YGB = E.scrb(G * 512)
    ASB = E.scrb(512)
    SSB = E.scrf(512)
    Y2B = E.scrb(G * 512)
    ygv = lambda g: YGB[:, g * 512:(g + 1) * 512]
    y2v = lambda g: Y2B[:, g * 512:(g + 1) * 512]
    for c0 in range(0, T, 512):
        ltoks = []
        for g in range(G):
            ltoks.append(P.dma("pool", ygv(g), YSP[g * 128:(g + 1) * 128, c0:c0 + 512],
                               waits=[inv_done, tk[0]] if g == 0 else ()))
        x_tok = ltoks[-1]

        def glu_cb(mg, m0, nm, cc0, ncx, pa, tok):
            if mg < 8:
                return chain("dve", lambda e, o=y2v(mg), i=pa, s=W["glu_b"][:, mg:mg + 1]:
                             e.tensor_scalar(o, i, s, None, ALU.add), extra=[tok])
            g2 = mg - 8
            chain("act", lambda e, o=SSB[:, 0:512], i=pa, b=W["glu_b"][:, mg:mg + 1]:
                  e.activation(o, i, AF.Sigmoid, bias=b), extra=[tok])
            return chain("dve", lambda e, o=y2v(g2), a=y2v(g2), b=SSB[:, 0:512]:
                         e.tensor_tensor(o, a, b, ALU.mult))
        mm_fm(P, E, [(W["glu_WT"], lambda kg, a, n: ygv(kg), x_tok)], glu_cb,
              M=2048, col_tiles=[(c0, 512)])

        def out_cb(mg, m0, nm, cc0, ncx, pa, tok):
            chain("act", lambda e, o=pa, i=pa, b=W["out_b"][:, mg:mg + 1]:
                  e.activation(o, i, AF.Identity, bias=b), extra=[tok])
            chain("dve", lambda e, o=pa, a=pa, b=E.xn(mg, cc0, ncx):
                  e.tensor_tensor(o, a, b, ALU.add))
            return chain("dve", lambda e, o=E.xres(mg, cc0, ncx),
                         a=E.xres(mg, cc0, ncx), b=pa:
                         e.tensor_tensor(o, a, b, ALU.add))
        mm_fm(P, E, [(W["out_WT"], lambda kg, a, n: y2v(kg), tk[0])], out_cb,
              M=D, col_tiles=[(c0, 512)])
    P.barrier()



def t_(a):
    return np.ascontiguousarray(np.asarray(a).T)


def colv(a):
    return np.ascontiguousarray(np.asarray(a).reshape(-1, 128).T)


def host_prepare(inputs):
    H = {}
    H["x_fm"] = np.ascontiguousarray(np.asarray(inputs["L0_attn_x"]).transpose(0, 2, 1))
    H["L0f_A_R"] = np.ascontiguousarray(np.asarray(inputs["L0_ffn_G1_R"])[0].transpose(1, 0, 2).reshape(4, 1024))
    H["L0f_A_I"] = np.ascontiguousarray(np.asarray(inputs["L0_ffn_G1_I"])[0].transpose(1, 0, 2).reshape(4, 1024))
    H["L0f_B_R"] = np.ascontiguousarray(np.asarray(inputs["L0_ffn_G2_R"])[:, :, 0, :].transpose(0, 2, 1).reshape(4, 256))
    H["L0f_B_I"] = np.ascontiguousarray(np.asarray(inputs["L0_ffn_G2_I"])[:, :, 0, :].transpose(0, 2, 1).reshape(4, 256))
    H["L0f_bias_real"] = colv(inputs["L0_ffn_bias_real"])
    H["L0f_bias_imag"] = colv(inputs["L0_ffn_bias_imag"])
    H["L3a_W_omT"] = t_(inputs["L3_attn_W_omega"])
    H["L3a_W_zT"] = t_(inputs["L3_attn_W_zeta"])
    H["L3a_W_BT"] = t_(inputs["L3_attn_W_B"])
    H["L3a_C_T"] = t_(inputs["L3_attn_C"])
    H["L3a_DmatT"] = t_(inputs["L3_attn_D_mat"])
    H["L3a_b_omega"] = np.ascontiguousarray(np.asarray(inputs["L3_attn_b_omega"]).reshape(-1, 1))
    H["L3a_b_zeta"] = np.ascontiguousarray(np.asarray(inputs["L3_attn_b_zeta"]).reshape(-1, 1))
    H["L3a_b_B"] = np.ascontiguousarray(np.asarray(inputs["L3_attn_b_B"]).reshape(-1, 1))
    H["L0a_proj_wT"] = t_(inputs["L0_attn_proj_w"])
    H["L0a_gp_wT"] = t_(inputs["L0_attn_gp_w"])
    H["L0a_ip_wT"] = t_(inputs["L0_attn_ip_w"])
    H["L0a_out_wT"] = t_(inputs["L0_attn_out_w"])
    cw = np.asarray(inputs["L0_attn_conv_w"]).reshape(1536, 4)
    H["L0a_convw"] = np.ascontiguousarray(cw.reshape(12, 128, 4).transpose(1, 0, 2).reshape(128, 48))
    H["L0a_lam"] = colv(inputs["L0_attn_forget_lambda"])
    H["L0a_gp_b"] = colv(inputs["L0_attn_gp_b"])
    H["L0a_ip_b"] = colv(inputs["L0_attn_ip_b"])
    H["L0a_conv_b"] = colv(inputs["L0_attn_conv_b"])
    H["L1a_W_in"] = np.ascontiguousarray(np.asarray(inputs["L1_attn_W_in"]))
    xp = np.asarray(inputs["L1_attn_W_xp"])
    H["L1a_W_xp_dt"] = np.ascontiguousarray(xp[:, 0:64])
    H["L1a_W_xp_B"] = np.ascontiguousarray(xp[:, 64:80])
    H["L1a_W_xp_C"] = np.ascontiguousarray(xp[:, 80:96])
    H["L1a_W_dt"] = np.ascontiguousarray(np.asarray(inputs["L1_attn_W_dt"]))
    H["L1a_W_out"] = np.ascontiguousarray(np.asarray(inputs["L1_attn_W_out"]))
    cw1 = np.asarray(inputs["L1_attn_conv_w"])
    H["L1a_convw"] = np.ascontiguousarray(cw1.reshape(16, 128, 4).transpose(1, 0, 2).reshape(128, 64))
    H["L1a_conv_b"] = colv(inputs["L1_attn_conv_b"])
    H["L1a_b_dt"] = colv(inputs["L1_attn_b_dt"])
    H["L1a_D_param"] = colv(inputs["L1_attn_D_param"])
    H["L1a_ln_g"] = colv(inputs["L1_attn_ln_g"])
    H["L1a_ln_b"] = colv(inputs["L1_attn_ln_b"])
    H["L1a_ALOGC"] = np.ascontiguousarray(np.asarray(inputs["L1_attn_A_log"]).reshape(16, 128, 16).transpose(1, 2, 0).reshape(128, 256))
    H["L1a_rs"] = np.ascontiguousarray(np.broadcast_to(np.asarray(inputs["L1_attn_res_scale"]).reshape(1, 1), (128, 1)))
    H["L1a_SELB"] = np.kron(np.eye(16, dtype=np.float32), np.ones((1, 128), np.float32))
    H["identf"] = np.eye(128, dtype=np.float32)
    H["IDB8"] = np.eye(8, dtype=np.float32)
    ltm = np.zeros((8, 8), np.float32)
    for e in range(8):
        for e2 in range(e):
            ltm[e, e2] = 1.0
    H["LTM"] = np.ascontiguousarray(np.broadcast_to(ltm.reshape(1, 64), (128, 64)))
    H["L1f_router_wT"] = t_(inputs["L1_ffn_router_w"])
    H["L1f_w1T"] = np.ascontiguousarray(np.asarray(inputs["L1_ffn_w1"]).transpose(0, 2, 1).reshape(8192, 4096))
    H["L1f_w2T"] = np.ascontiguousarray(np.asarray(inputs["L1_ffn_w2"]).transpose(0, 2, 1).reshape(32768, 1024))
    H["L2f_enc_W"] = np.ascontiguousarray(np.asarray(inputs["L2_ffn_enc_W"]))
    H["L2f_enc_b"] = colv(inputs["L2_ffn_enc_b"])
    H["L2f_s2c_W"] = np.ascontiguousarray(np.asarray(inputs["L2_ffn_s2c_W"]))
    H["L2f_s2c_b"] = np.ascontiguousarray(np.asarray(inputs["L2_ffn_s2c_b"]).reshape(-1, 1))
    H["L2f_rW1"] = np.ascontiguousarray(np.asarray(inputs["L2_ffn_rW1"]))
    H["L2f_rb1"] = np.ascontiguousarray(np.asarray(inputs["L2_ffn_rb1"]).reshape(-1, 1))
    H["L2f_rW2"] = np.ascontiguousarray(np.asarray(inputs["L2_ffn_rW2"]))
    H["L2f_rb2"] = np.ascontiguousarray(np.asarray(inputs["L2_ffn_rb2"]).reshape(-1, 1))
    H["L2f_We1"] = np.ascontiguousarray(np.asarray(inputs["L2_ffn_We1"]).reshape(512, 1024))
    H["L2f_be1"] = np.ascontiguousarray(np.asarray(inputs["L2_ffn_be1"]).reshape(8, 8, 128).transpose(2, 0, 1).reshape(128, 64))
    H["L2f_We2"] = np.ascontiguousarray(np.asarray(inputs["L2_ffn_We2"]).reshape(8192, 64))
    H["L2f_BE2"] = np.ascontiguousarray(np.asarray(inputs["L2_ffn_be2"]).reshape(1, 512))
    H["L2f_c2s_W"] = np.ascontiguousarray(np.asarray(inputs["L2_ffn_c2s_W"]))
    H["L2f_c2s_b"] = colv(inputs["L2_ffn_c2s_b"])
    H["L2f_dec_W"] = np.ascontiguousarray(np.asarray(inputs["L2_ffn_dec_W"]))
    H["L2f_dec_b"] = colv(inputs["L2_ffn_dec_b"])
    H["L2f_ln_g"] = colv(inputs["L2_ffn_ln_g"])
    H["L2f_ln_b"] = colv(inputs["L2_ffn_ln_b"])
    import ml_dtypes
    H["L2a_in_W"] = t_(inputs["L2_attn_in_W"])
    H["L2a_in_b_row"] = np.ascontiguousarray(np.asarray(inputs["L2_attn_in_b"]).reshape(1, 1024))
    H["L2a_in_b_col"] = colv(inputs["L2_attn_in_b"])
    H["L2a_logA_T"] = t_(inputs["L2_attn_log_A_real"])
    H["L2a_Aim_T"] = t_(inputs["L2_attn_A_imag"])
    H["L2a_log_dt_row"] = np.ascontiguousarray(np.asarray(inputs["L2_attn_log_dt"]).reshape(1, 1024))
    H["L2a_C_reT"] = t_(np.asarray(inputs["L2_attn_C"])[:, :, 0])
    H["L2a_C_imT"] = t_(np.asarray(inputs["L2_attn_C"])[:, :, 1])
    H["L2a_D_col"] = colv(inputs["L2_attn_D_skip"])
    H["L2a_glu_WT"] = t_(inputs["L2_attn_glu_W"])
    H["L2a_glu_b"] = colv(inputs["L2_attn_glu_b"])
    H["L2a_out_WT"] = t_(inputs["L2_attn_out_W"])
    H["L2a_out_b"] = colv(inputs["L2_attn_out_b"])
    lv = np.arange(2048, dtype=np.float64)[:, None]
    fv = np.arange(2049, dtype=np.float64)[None, :]
    ang = 2.0 * np.pi * lv * fv / 4096.0
    fwd = np.zeros((2048, 4352), np.float32)
    fwd[:, 0:2049] = np.cos(ang)
    fwd[:, 2176:4225] = -np.sin(ang)
    H["L2a_FWD"] = fwd.astype(ml_dtypes.bfloat16)
    wgt = np.full((1, 2049), 2.0); wgt[0, 0] = 1.0; wgt[0, 2048] = 1.0
    inv = np.zeros((4352, 2048), np.float32)
    inv[0:2049, :] = (wgt.T * np.cos(ang.T) / 4096.0)
    inv[2176:4225, :] = (wgt.T * -np.sin(ang.T) / 4096.0)
    H["L2a_INV"] = inv.astype(ml_dtypes.bfloat16)
    H["L3f_W_rT"] = t_(inputs["L3_ffn_W_r"])
    H["L3f_W_kT"] = t_(inputs["L3_ffn_W_k"])
    H["L3f_W_vT"] = t_(inputs["L3_ffn_W_v"])
    H["L3f_mu_r"] = colv(inputs["L3_ffn_mu_r"])
    H["L3f_mu_k"] = colv(inputs["L3_ffn_mu_k"])
    return H



def build_single(sub, weight_specs=None, emit_fn=None, debug=False):
    nc = bass.Bass("TRN2", target_bir_lowering=False, debug=False)
    es = ExitStack()
    x_in = nc.dram_tensor("x_in", [D, T], F32, kind="ExternalInput").ap()
    x_out = nc.dram_tensor("x_out", [D, T], F32, kind="ExternalOutput").ap()
    drams = {}
    for name, shape in weight_specs:
        base = name.split("@")[0]
        dt_ = BF16 if base in ("FWD", "INV") else F32
        drams[name] = nc.dram_tensor(base, list(shape), dt_, kind="ExternalInput").ap()
    P = Prog(nc, es)
    E = Env(nc, es, P)
    ltoks = []
    for g in range(G):
        ltoks.append(P.dma("sp", E.xres(g), x_in[g * 128:(g + 1) * 128, :]))
    W = {}
    for name, shape in weight_specs:
        base = name.split("@")[0]
        if name.endswith("@cons"):
            tile = E.cons(shape[1])
            ltoks.append(P.dma("sp", tile[0:shape[0], :], drams[name]))
            W[base] = tile
        elif name.endswith("@sbufbf") or base == "SELB":
            tile = E.es.enter_context(
                E.nc.sbuf_tensor(base + "_t", list(shape), BF16))
            ltoks.append(P.dma("pool", tile[:], drams[name]))
            W[base] = tile
        elif name.endswith("@sbuff32"):
            tile = E.es.enter_context(
                E.nc.sbuf_tensor(base + "_t", list(shape), F32))
            ltoks.append(P.dma("sp", tile[:], drams[name]))
            W[base] = tile
        else:
            W[base] = drams[name]
    for e in ENGS:
        for tk in ltoks:
            P._wait(e, tk)
    emit_fn(P, E, W)
    stoks = []
    for g in range(G):
        stoks.append(P.dma("sp", x_out[g * 128:(g + 1) * 128, :], E.xres(g)))
    if debug:
        dbg = nc.dram_tensor("dbg_scrf", [128, SCRF_COLS], F32, kind="ExternalOutput").ap()
        stoks.append(P.dma("sp", dbg, E.SCRF[:]))
    for tk in stoks:
        P._wait("sp", tk)
    P.finalize()
    es.close()
    return nc


def build_l3_ffn_single():
    return build_single("l3_ffn", [
        ("W_rT", (D, D)), ("W_kT", (D, D)), ("W_vT", (D, D)),
        ("mu_r@cons", (128, G)), ("mu_k@cons", (128, G)),
    ], emit_l3_ffn)


def build_l0_ffn_single():
    def emit(P, E, W):
        nc = E.nc
        W = dict(W)
        W["W0R"] = nc.dram_tensor("W0R", [512, 512], BF16, kind="Internal").ap()
        W["W0I"] = nc.dram_tensor("W0I", [512, 512], BF16, kind="Internal").ap()
        W["W0In"] = nc.dram_tensor("W0In", [512, 512], BF16, kind="Internal").ap()
        emit_l0_ffn(P, E, W, build=True)
    return build_single("l0_ffn", [
        ("A_R", (4, 1024)), ("A_I", (4, 1024)), ("B_R", (4, 256)), ("B_I", (4, 256)),
        ("bias_real@cons", (128, 4)), ("bias_imag@cons", (128, 4)),
    ], emit)


def build_l3_attn_single():
    return build_single("l3_attn", debug=True, weight_specs=[
        ("W_omT", (D, 64)), ("W_zT", (D, 64)), ("W_BT", (D, 128)),
        ("C_T", (128, D)), ("DmatT", (D, D)),
        ("b_omega@cons", (64, 1)), ("b_zeta@cons", (64, 1)), ("b_B@cons", (128, 1)),
    ], emit_fn=emit_l3_attn)


def build_l0_attn_single():
    return build_single("l0_attn", weight_specs=[
        ("proj_wT", (D, 3072)), ("gp_wT", (1536, 1536)), ("ip_wT", (1536, 1536)),
        ("out_wT", (1536, D)),
        ("convw@cons", (128, 48)), ("lam@cons", (128, 12)),
        ("gp_b@cons", (128, 12)), ("ip_b@cons", (128, 12)), ("conv_b@cons", (128, 12)),
    ], emit_fn=emit_l0_attn)


def build_l1_attn_single():
    return build_single("l1_attn", weight_specs=[
        ("W_in", (D, 4096)), ("W_xp_dt", (2048, 64)), ("W_xp_B", (2048, 16)),
        ("W_xp_C", (2048, 16)), ("W_dt", (64, 2048)), ("W_out", (2048, D)),
        ("convw@cons", (128, 64)), ("conv_b@cons", (128, 16)), ("b_dt@cons", (128, 16)),
        ("D_param@cons", (128, 16)), ("ln_g@cons", (128, 8)), ("ln_b@cons", (128, 8)),
        ("ALOGC@cons", (128, 256)), ("rs@cons", (128, 1)), ("SELB", (16, 2048)),
    ], emit_fn=emit_l1_attn)


def build_l1_ffn_single():
    return build_single("l1_ffn", weight_specs=[
        ("router_wT", (D, 8)), ("w1T", (8192, 4096)), ("w2T", (32768, 1024)),
        ("identf@sbuff32", (128, 128)), ("IDB8@sbufbf", (8, 8)),
        ("LTM@cons", (128, 64)),
    ], emit_fn=lambda P, E, W: emit_moe(P, E, W, 1))


def build_l2_ffn_single():
    return build_single("l2_ffn", weight_specs=[
        ("enc_W", (D, 2048)), ("s2c_W", (2048, 64)), ("rW1", (64, 64)),
        ("rW2", (64, 8)), ("We1", (512, 1024)), ("We2", (8192, 64)),
        ("c2s_W", (64, 2048)), ("dec_W", (2048, D)),
        ("identf@sbuff32", (128, 128)), ("IDB8@sbufbf", (8, 8)),
        ("BE2@sbufbf", (1, 512)),
        ("LTM@cons", (128, 64)), ("enc_b@cons", (128, 16)), ("s2c_b@cons", (64, 1)),
        ("rb1@cons", (64, 1)), ("rb2@cons", (8, 1)), ("be1@cons", (128, 64)),
        ("c2s_b@cons", (128, 16)), ("dec_b@cons", (128, 8)),
        ("ln_g@cons", (128, 8)), ("ln_b@cons", (128, 8)),
    ], emit_fn=lambda P, E, W: emit_moe(P, E, W, 2))


def build_l2_attn_single():
    def emit(P, E, W):
        nc = E.nc
        W = dict(W)
        for nm, shp in (("UTM", [2048, 1024]), ("KTM", [2048, 1024]),
                        ("UF", [4352, 1024]), ("KF", [4352, 1024]),
                        ("YSP", [1024, 2048])):
            W[nm] = nc.dram_tensor(nm, shp, BF16, kind="Internal").ap()
        emit_l2_attn(P, E, W)
    return build_single("l2_attn", weight_specs=[
        ("in_W", (D, D)), ("in_b_row", (1, D)), ("log_dt_row", (1, D)),
        ("logA_T", (32, D)), ("Aim_T", (32, D)), ("C_reT", (32, D)), ("C_imT", (32, D)),
        ("glu_WT", (D, 2048)), ("out_WT", (D, D)),
        ("FWD", (2048, 4352)), ("INV", (4352, 2048)),
        ("in_b_col@cons", (128, 8)), ("D_col@cons", (128, 8)),
        ("glu_b@cons", (128, 16)), ("out_b@cons", (128, 8)),
    ], emit_fn=emit)


_SUB_BUILDERS = None


def _get_builders():
    global _SUB_BUILDERS
    if _SUB_BUILDERS is None:
        _SUB_BUILDERS = {
            "l0_ffn": build_l0_ffn_single(),
            "l0_attn": build_l0_attn_single(),
            "l1_ffn": build_l1_ffn_single(),
            "l1_attn": build_l1_attn_single(),
            "l2_ffn": build_l2_ffn_single(),
            "l2_attn": build_l2_attn_single(),
            "l3_ffn": build_l3_ffn_single(),
            "l3_attn": build_l3_attn_single(),
        }
    return _SUB_BUILDERS


_LAUNCH_PLAN = [
    ("l0_ffn", "L0f_", ["A_R", "A_I", "B_R", "B_I", "bias_real", "bias_imag"]),
    ("l0_attn", "L0a_", ["proj_wT", "gp_wT", "ip_wT", "out_wT", "convw", "lam",
                         "gp_b", "ip_b", "conv_b"]),
    ("l0_ffn", "L0f_", None),
    ("l1_ffn", "L1f_", ["router_wT", "w1T", "w2T", "identf", "IDB8", "LTM"]),
    ("l1_attn", "L1a_", ["W_in", "W_xp_dt", "W_xp_B", "W_xp_C", "W_dt", "W_out",
                         "convw", "conv_b", "b_dt", "D_param", "ln_g", "ln_b",
                         "ALOGC", "rs", "SELB"]),
    ("l1_ffn", "L1f_", None),
    ("l2_ffn", "L2f_", ["enc_W", "s2c_W", "rW1", "rW2", "We1", "We2", "c2s_W",
                        "dec_W", "BE2", "identf", "IDB8", "LTM", "enc_b", "s2c_b",
                        "rb1", "rb2", "be1", "c2s_b", "dec_b", "ln_g", "ln_b"]),
    ("l2_attn", "L2a_", ["in_W", "in_b_row", "log_dt_row", "logA_T", "Aim_T",
                         "C_reT", "C_imT", "glu_WT", "out_WT", "FWD", "INV",
                         "in_b_col", "D_col", "glu_b", "out_b"]),
    ("l2_ffn", "L2f_", None),
    ("l3_ffn", "L3f_", ["W_rT", "W_kT", "W_vT", "mu_r", "mu_k"]),
    ("l3_attn", "L3a_", ["W_omT", "W_zT", "W_BT", "C_T", "DmatT", "b_omega",
                         "b_zeta", "b_B"]),
    ("l3_ffn", "L3f_", None),
]


def kernel(**inputs):
    H = host_prepare(inputs)
    builders = _get_builders()
    ncores = 8
    xs = [np.ascontiguousarray(H["x_fm"][b]) for b in range(ncores)]
    plan_keys = {}
    for sub, pref, keys in _LAUNCH_PLAN:
        if keys is not None:
            plan_keys[sub] = (pref, keys)
    for sub, pref, keys in _LAUNCH_PLAN:
        pref, keys = plan_keys[sub]
        nc = builders[sub]
        wmap = {}
        for k in keys:
            wmap[k] = H.get(pref + k, H.get(k))
        in_maps = [dict(wmap, x_in=xs[b]) for b in range(ncores)]
        res = run_bass_kernel_spmd(nc, in_maps, core_ids=list(range(ncores)))
        xs = [np.asarray(res.results[b]["x_out"]) for b in range(ncores)]
    out = np.stack([x.T for x in xs], axis=0)
    return np.ascontiguousarray(out.astype(np.float32))


# version 11
# speedup vs baseline: 1.4637x; 1.4637x over previous
import numpy as np
from contextlib import ExitStack

import concourse.bass as bass
import concourse.mybir as mybir
from concourse.bass_utils import run_bass_kernel_spmd

F32 = mybir.dt.float32
BF16 = mybir.dt.bfloat16
AF = mybir.ActivationFunctionType
ALU = mybir.AluOpType

T = 2048
D = 1024
G = D // 128
PAD = 8
TW = T + PAD
TC = 512

MAGIC = 12582912.0

ENGS = ("pe", "dve", "act", "pool", "sp")


class Prog:

    QRING = {"sp": 6, "pool": 8, "act": 4}

    def __init__(self, nc, es):
        self.nc = nc
        self.items = {e: [] for e in ENGS}
        self.pending = {e: [] for e in ENGS}
        self.sems = {}
        for s in ("s_pe", "s_dve", "s_act", "s_pool", "s_sp"):
            self.sems[s] = [es.enter_context(nc.semaphore(s)), 0]
        self.qrings = {}
        self.qcount = {}
        for q, n in self.QRING.items():
            names = []
            for i in range(n):
                s = "q_%s_%d" % (q, i)
                self.sems[s] = [es.enter_context(nc.semaphore(s)), 0]
                names.append(s)
            self.qrings[q] = names
            self.qcount[q] = 0

    SKIP_SAME = True
    STRICT_SAME = True
    EAGER = False

    def _wait(self, eng, tok):
        if tok is None:
            return
        if isinstance(tok, tuple) and tok[0] == "lz":
            entry = tok[1]
            if entry["eng"] == eng and (eng == "pe" or not self.STRICT_SAME):
                return
            tok = entry.get("tok") or self._materialize(entry)
        sem, val = self.sems[tok[0]][0], tok[1]
        if val <= 0:
            return
        self.items[eng].append({"fn": (lambda e, s=sem, v=val: e.wait_ge(s, v)),
                                "sig": None})

    def _materialize(self, entry):
        eng = entry["eng"]
        if entry.get("tok") is not None:
            return entry["tok"]
        pend = self.pending[eng]
        rec = self.sems["s_" + eng]
        while pend:
            e2 = pend.pop(0)
            rec[1] += 1
            e2["tok"] = ("s_" + eng, rec[1])
            e2["sig"] = (rec[0], 1)
            if e2 is entry:
                break
        return entry["tok"]

    def op(self, eng, fn, waits=(), sig=False):
        for t in waits:
            self._wait(eng, t)
        entry = {"fn": fn, "sig": None, "eng": eng}
        self.items[eng].append(entry)
        self.pending[eng].append(entry)
        tok = ("lz", entry)
        if sig or self.EAGER:
            tok = self._materialize(entry)
        return tok

    def dma(self, q, out, in_, waits=()):
        ring = self.qrings[q]
        k = self.qcount[q]
        self.qcount[q] += 1
        sname = ring[k % len(ring)]
        rec = self.sems[sname]
        if rec[1] > 0:
            self._wait(q, (sname, rec[1]))
        for t in waits:
            self._wait(q, t)
        rec[1] += 16
        tok = (sname, rec[1])
        self.items[q].append({"fn": (lambda e, o=out, i=in_: e.dma_start(out=o, in_=i)),
                              "sig": (rec[0], 16)})
        return tok

    def all_tokens(self):
        return [(name, rec[1]) for name, rec in self.sems.items() if rec[1] > 0]

    def barrier(self):
        for e in ENGS:
            if self.pending[e]:
                self._materialize(self.pending[e][-1])
        toks = self.all_tokens()
        for e in ENGS:
            for t in toks:
                self._wait(e, t)

    def _run(self, eng, e):
        for entry in self.items[eng]:
            if entry["sig"] is not None:
                entry["fn"](e).then_inc(entry["sig"][0], entry["sig"][1])
            else:
                entry["fn"](e)

    def finalize(self):
        nc = self.nc
        with nc.Block() as block:
            @block.tensor
            def _(e):
                self._run("pe", e)

            @block.vector
            def _(e):
                self._run("dve", e)

            @block.scalar
            def _(e):
                self._run("act", e)

            @block.gpsimd
            def _(e):
                self._run("pool", e)

            @block.sync
            def _(e):
                self._run("sp", e)


SCRF_COLS = 6144
SCRB_COLS = 24576


class Env:

    def __init__(self, nc, es, P):
        self.nc = nc
        self.es = es
        self.P = P
        sb = lambda n, shp, dt: es.enter_context(nc.sbuf_tensor(n, shp, dt))
        self.XRES = sb("XRES", [128, G * T], F32)
        self.XNBF = sb("XNBF", [128, G * TW], BF16)
        self.SCRF = sb("SCRF", [128, SCRF_COLS], F32)
        self.SCRB = sb("SCRB", [128, SCRB_COLS], BF16)
        self.WB = [sb("WB%d" % i, [128, 2176], BF16) for i in range(3)]
        self.wb_tok = [None, None, None]
        self.wb_idx = 0
        self.ROWS = sb("ROWS", [1, 5 * 512], F32)
        self.ROWSB = sb("ROWSB", [1, 2 * 512], BF16)
        self.ONE_L = sb("ONE_L", [128, 128], BF16)
        self.ONE_R = sb("ONE_R", [1, 128], BF16)
        self.CONS = sb("CONS", [128, 1024], F32)
        self.cons_off = 0
        self.banks = []
        for i in range(8):
            ap = es.enter_context(nc.psum_tensor("pb%d" % i, [128, 512], F32))
            self.banks.append([ap, None, False])
        self.bank_rr = 0
        self.scrf_off = 0
        self.scrb_off = 0
        P.op("dve", lambda e: e.memset(self.ONE_L[:], 1.0))
        P.op("dve", lambda e: e.memset(self.ONE_R[:], 1.0))
        self.init_tok = P.op("dve", lambda e: e.memset(self.XNBF[:], 0.0), sig=True)

    def cons(self, ncol):
        v = self.CONS[:, self.cons_off:self.cons_off + ncol]
        self.cons_off += ncol
        assert self.cons_off <= 1024
        return v

    def constf(self, val):
        if not hasattr(self, "_constf"):
            self._constf = {}
        if val in self._constf:
            return self._constf[val]
        ap = self.cons(1)
        tok = self.P.op("dve", lambda e, o=ap, v=val: e.memset(o, v), sig=True)
        self._constf[val] = (ap, tok)
        return ap, tok

    def scr_reset(self):
        self.scrf_off = 0
        self.scrb_off = 0

    def scrf(self, cols):
        v = self.SCRF[:, self.scrf_off:self.scrf_off + cols]
        self.scrf_off += cols
        assert self.scrf_off <= SCRF_COLS, "SCRF overflow"
        return v

    def scrb(self, cols):
        v = self.SCRB[:, self.scrb_off:self.scrb_off + cols]
        self.scrb_off += cols
        assert self.scrb_off <= SCRB_COLS, "SCRB overflow"
        return v

    def bank(self, hold=False):
        for _ in range(8):
            i = self.bank_rr
            self.bank_rr = (self.bank_rr + 1) % 8
            if not self.banks[i][2]:
                break
        else:
            raise RuntimeError("all psum banks held")
        ap, tok, _ = self.banks[i]
        self.banks[i][2] = hold
        return i, ap, tok

    def bank_free(self, i, tok):
        self.banks[i][1] = tok
        self.banks[i][2] = False

    def xres(self, g, t0=0, n=T):
        return self.XRES[:, g * T + t0: g * T + t0 + n]

    def xn(self, g, t0=0, n=T):
        return self.XNBF[:, g * TW + PAD + t0: g * TW + PAD + t0 + n]

    def xn_shift(self, g, shift, t0=0, n=T):
        c = g * TW + PAD + t0 - shift
        return self.XNBF[:, c: c + n]


def load_w_tile(P, E, w_dram, m0, nm, waits=()):
    idx = E.wb_idx
    E.wb_idx = (E.wb_idx + 1) % 3
    buf = E.WB[idx]
    K, M = w_dram.shape
    nkg = (K + 127) // 128
    assert nkg * nm <= 2176, (nkg, nm)
    src = w_dram[:, m0:m0 + nm]
    if nkg > 1:
        assert K % 128 == 0, K
        src = src.rearrange("(kg p) m -> p kg m", p=128)
        dst = buf[:, 0:nkg * nm].rearrange("p (kg m) -> p kg m", kg=nkg)
    else:
        src = src.unsqueeze(1)
        dst = buf[0:K, 0:nm].unsqueeze(1)
    w8 = list(waits)
    if E.wb_tok[idx] is not None:
        w8.append(E.wb_tok[idx])
    q = "pool" if w_dram.dtype == F32 else "sp"
    tok = P.dma(q, dst, src, waits=w8)
    return idx, buf, tok


def mm_fm(P, E, segments, out_cb, M, Ms=None, col_tiles=None,
          extra_rows=None):
    if col_tiles is None:
        col_tiles = [(c, min(512, T - c)) for c in range(0, T, 512)]
    if Ms is None:
        Ms = [(m, min(128, M - m)) for m in range(0, M, 128)]
    assert sum(1 for s in segments
               if not (isinstance(s[0], tuple) and s[0][0] == "sbuf")) <= 2, \
        "at most 2 streamed weight segments (2 weight buffers)"
    paired = []
    i = 0
    while i < len(Ms):
        if (i + 1 < len(Ms) and Ms[i][1] == 128 and Ms[i + 1][1] == 128
                and Ms[i + 1][0] == Ms[i][0] + 128
                and all((isinstance(s[0], tuple) and s[0][0] == "sbuf")
                        or (s[0].shape[0] + 127) // 128 * 256 <= 2176
                        for s in segments)):
            paired.append((Ms[i][0], 256, [Ms[i], Ms[i + 1]]))
            i += 2
        else:
            paired.append((Ms[i][0], Ms[i][1], [Ms[i]]))
            i += 1
    for (mb0, mbw, subms) in paired:
        loaded = []
        for seg in segments:
            w_src, xcols, x_tok = seg[0], seg[1], seg[2]
            if isinstance(w_src, tuple) and w_src[0] == "sbuf":
                loaded.append((None, None, w_src[1], w_src[2], xcols, x_tok))
            else:
                K = w_src.shape[0]
                bidx, buf, wtok = load_w_tile(P, E, w_src, mb0, mbw)
                tile_fn = (lambda kg, mm0, mnm, b=buf, n=mbw, KK=K, base=mb0:
                           b[0:min(128, KK - kg * 128),
                             kg * n + (mm0 - base): kg * n + (mm0 - base) + mnm])
                loaded.append((bidx, wtok, tile_fn, K, xcols, x_tok))

        first = True
        last_pe = None
        for (m0, nm) in subms:
          for (c0, nc) in col_tiles:
              bi, ps, ftok = E.bank()
              pa = ps[0:nm, 0:nc]
              tok = None
              total = sum((K + 127) // 128 for (_, _, _, K, _, _) in loaded)
              idx = 0
              for (bidx, wtok, tile_fn, K, xcols, x_tok) in loaded:
                  nkg = (K + 127) // 128
                  for kg in range(nkg):
                      kp = min(128, K - kg * 128)
                      lhs = tile_fn(kg, m0, nm)
                      rhs = xcols(kg, c0, nc)
                      waits = []
                      if first:
                          for (b2, w2, t2, K2, xc2, xt2) in loaded:
                              if w2 is not None:
                                  waits.append(w2)
                              if xt2 is not None:
                                  waits.append(xt2)
                          first = False
                      if idx == 0 and ftok is not None:
                          waits.append(ftok)
                      stop = (idx == total - 1) and extra_rows is None
                      tok = P.op("pe", lambda e, o=pa, l=lhs, r=rhs, st=(idx == 0), sp=stop:
                                 e.matmul(o, l, r, start=st, stop=sp),
                                 waits=waits, sig=stop)
                      idx += 1
              if extra_rows is not None:
                  lrow = extra_rows[0](m0, nm)
                  rrow = extra_rows[1](c0, nc)
                  w = [extra_rows[2]] if (len(extra_rows) > 2 and extra_rows[2]) else []
                  tok = P.op("pe", lambda e, o=pa, l=lrow, r=rrow:
                             e.matmul(o, l, r, start=False, stop=True),
                             waits=w, sig=True)
              last_pe = tok
              free_tok = out_cb(m0 // 128, m0, nm, c0, nc, pa, tok)
              E.bank_free(bi, free_tok)
        for (bidx, wtok, tile_fn, K, xcols, x_tok) in loaded:
            if bidx is not None:
                E.wb_tok[bidx] = last_pe


def bcast_row(P, E, row_bf_ap, ncols, waits=()):
    bi, ps, ftok = E.bank()
    w = list(waits)
    if ftok is not None:
        w.append(ftok)
    tok = P.op("pe", lambda e, o=ps[:, 0:ncols], r=row_bf_ap:
               e.matmul(o, E.ONE_R[:, 0:128], r, start=True, stop=True),
               waits=w, sig=True)
    return bi, ps[:, 0:ncols], tok


def emit_ln(P, E, xb, dst, ncols=T, affine=None, src=None, src_tok=None,
            eps=1e-5, ngroups=G, sq_scr=None, apply_src=None):
    nD = ngroups * 128
    ctoks = [src_tok] if src_tok else []
    if src is not None:
        ctoks = []
        for g in range(ngroups):
            w0 = [src_tok] if (g == 0 and src_tok) else []
            if g == 0 and getattr(E, "init_tok", None):
                w0.append(E.init_tok)
            tk = P.op("dve", lambda e, o=xb(g), i=src(g): e.tensor_copy(o, i),
                      waits=w0, sig=True)
        ctoks = [tk]
    sq = sq_scr if sq_scr is not None else E.scrb(512)
    lasttk = None
    for c0 in range(0, ncols, 512):
        nc = min(512, ncols - c0)
        bi1, ps1, ft1 = E.bank()
        bi2, ps2, ft2 = E.bank()
        mtok2 = None
        for g in range(ngroups):
            aw = list(ctoks if g == 0 else [])
            if g == 0 and ft2 is not None:
                aw.append(ft2)
            if mtok2 is not None:
                aw.append(mtok2)
            atok = P.op("act", lambda e, o=sq[:, 0:nc], i=xb(g)[:, c0:c0 + nc]:
                        e.activation(o, i, AF.Square), waits=aw, sig=True)
            w1 = list(ctoks if g == 0 else [])
            if g == 0 and ft1 is not None:
                w1.append(ft1)
            P.op("pe", lambda e, o=ps1[:, 0:nc], r=xb(g)[:, c0:c0 + nc],
                 st=(g == 0), sp=(g == ngroups - 1):
                 e.matmul(o, E.ONE_L[:], r, start=st, stop=sp),
                 waits=w1, sig=(g == ngroups - 1))
            mtok2 = P.op("pe", lambda e, o=ps2[:, 0:nc], r=sq[:, 0:nc],
                         st=(g == 0), sp=(g == ngroups - 1):
                         e.matmul(o, E.ONE_L[:], r, start=st, stop=sp),
                         waits=[atok], sig=True)
        sum_tok = mtok2
        mu = E.ROWS[:, 0:nc]
        var = E.ROWS[:, 512:512 + nc]
        rstd = E.ROWS[:, 1024:1024 + nc]
        m2 = E.ROWS[:, 1536:1536 + nc]
        tmp = E.ROWS[:, 2048:2048 + nc]
        P.op("dve", lambda e, o=mu, i=ps1[0:1, 0:nc]:
             e.tensor_scalar_mul(o, i, 1.0 / nD), waits=[sum_tok])
        P.op("dve", lambda e, o=var, i=ps2[0:1, 0:nc]:
             e.tensor_scalar_mul(o, i, 1.0 / nD))
        P.op("dve", lambda e, o=tmp, a=mu, b=mu: e.tensor_tensor(o, a, b, ALU.mult))
        P.op("dve", lambda e, o=var, a=var, b=tmp:
             e.tensor_tensor(o, a, b, ALU.subtract), sig=True)
        eps_ap, eps_tok = E.constf(eps)
        P.op("act", lambda e, o=rstd, i=var:
             e.activation(o, i, AF.Ln, bias=eps_ap[0:1, :]),
             waits=[("s_dve", P.sems["s_dve"][1]), eps_tok], sig=True)
        atk = P.op("act", lambda e, o=rstd, i=rstd:
                   e.activation(o, i, AF.Exp, scale=-0.5), sig=True)
        P.op("dve", lambda e, o=m2, a=mu, b=rstd: e.tensor_tensor(o, a, b, ALU.mult),
             waits=[atk])
        P.op("dve", lambda e, o=m2, i=m2: e.tensor_scalar_mul(o, i, -1.0))
        rb = E.ROWSB[:, 0:nc]
        m2b = E.ROWSB[:, 512:512 + nc]
        P.op("dve", lambda e, o=rb, i=rstd: e.tensor_copy(o, i))
        rowtok = P.op("dve", lambda e, o=m2b, i=m2: e.tensor_copy(o, i), sig=True)
        b1, prb, t1 = bcast_row(P, E, rb, nc, waits=[rowtok])
        b2, pm2, t2 = bcast_row(P, E, m2b, nc, waits=[rowtok])
        for g in range(ngroups):
            dv = dst(g)[:, c0:c0 + nc]
            sv = (apply_src or xb)(g)[:, c0:c0 + nc]
            P.op("dve", lambda e, o=dv, a=sv, b=prb: e.tensor_tensor(o, a, b, ALU.mult),
                 waits=[t1, t2] if g == 0 else ())
            sig_here = affine is None
            lasttk = P.op("dve", lambda e, o=dv, a=dv, b=pm2:
                          e.tensor_tensor(o, a, b, ALU.add), sig=sig_here)
            if affine is not None:
                ga, ba = affine
                lasttk = P.op("dve", lambda e, o=dv, i=dv, s1=ga[:, g:g + 1], s2=ba[:, g:g + 1]:
                              e.tensor_scalar(o, i, s1, s2, ALU.mult, ALU.add), sig=True)
        E.bank_free(b1, lasttk)
        E.bank_free(b2, lasttk)
        E.bank_free(bi1, lasttk)
        E.bank_free(bi2, lasttk)
    return lasttk



def emit_l3_ffn(P, E, W):
    E.scr_reset()
    ln_tok = emit_ln(P, E, lambda g: E.xn(g), lambda g: E.xn(g),
                     src=lambda g: E.xres(g))
    xr = E.scrb(G * TC)
    xk = E.scrb(G * TC)
    rr = E.scrb(G * TC)
    xrv = lambda g: xr[:, g * TC:(g + 1) * TC]
    xkv = lambda g: xk[:, g * TC:(g + 1) * TC]
    rv = lambda g: rr[:, g * TC:(g + 1) * TC]
    mu_r, mu_k = W["mu_r"], W["mu_k"]
    for c0 in range(0, T, TC):
        x_tok = None
        for g in range(G):
            P.op("dve", lambda e, o=xkv(g), a=E.xn(g, c0, TC), b=E.xn_shift(g, 1, c0, TC):
                 e.tensor_tensor(o, a, b, ALU.subtract),
                 waits=[ln_tok] if (g == 0 and c0 == 0) else ())
            P.op("dve", lambda e, o=xrv(g), i=xkv(g), s=mu_r[:, g:g + 1],
                 b=E.xn_shift(g, 1, c0, TC):
                 e.scalar_tensor_tensor(o, i, s, b, ALU.mult, ALU.add))
            x_tok = P.op("dve", lambda e, o=xkv(g), i=xkv(g), s=mu_k[:, g:g + 1],
                         b=E.xn_shift(g, 1, c0, TC):
                         e.scalar_tensor_tensor(o, i, s, b, ALU.mult, ALU.add), sig=True)
        ct = [(0, TC)]
        last = {}

        def out_r(mg, m0, nm, cc0, ncc, pa, tok):
            tk = P.op("act", lambda e, o=rv(mg)[:, cc0:cc0 + ncc], i=pa:
                      e.activation(o, i, AF.Sigmoid), waits=[tok], sig=True)
            last["r"] = tk
            return tk
        mm_fm(P, E, [(W["W_rT"], lambda kg, cc0, ncc: xrv(kg)[:, cc0:cc0 + ncc], x_tok)],
              out_r, M=D, col_tiles=ct)

        def out_k(mg, m0, nm, cc0, ncc, pa, tok):
            w = [tok] + ([last.pop("r")] if "r" in last else [])
            t1 = P.op("dve", lambda e, o=pa, i=pa: e.tensor_scalar_max(o, i, 0.0),
                      waits=w, sig=True)
            tk = P.op("act", lambda e, o=xrv(mg)[:, cc0:cc0 + ncc], i=pa:
                      e.activation(o, i, AF.Square), waits=[t1], sig=True)
            last["k"] = tk
            return tk
        mm_fm(P, E, [(W["W_kT"], lambda kg, cc0, ncc: xkv(kg)[:, cc0:cc0 + ncc], None)],
              out_k, M=D, col_tiles=ct)

        def out_v(mg, m0, nm, cc0, ncc, pa, tok):
            P.op("dve", lambda e, o=pa, a=pa, b=rv(mg)[:, cc0:cc0 + ncc]:
                 e.tensor_tensor(o, a, b, ALU.mult), waits=[tok])
            return P.op("dve", lambda e, o=E.xres(mg, c0 + cc0, ncc), i=pa,
                        b=E.xres(mg, c0 + cc0, ncc):
                        e.scalar_tensor_tensor(o, i, 0.5, b, ALU.mult, ALU.add),
                        sig=True)
        mm_fm(P, E, [(W["W_vT"], lambda kg, cc0, ncc: xrv(kg)[:, cc0:cc0 + ncc], last.pop("k"))],
              out_v, M=D, col_tiles=ct)
    P.barrier()


def emit_l0ffn_build(P, E, W):
    a_t = E.SCRF[0:4, 0:2048]
    b_t = E.SCRF[0:4, 2048:2560]
    t1 = P.dma("sp", a_t[:, 0:1024], W["A_R"])
    t2 = P.dma("sp", a_t[:, 1024:2048], W["A_I"])
    t3 = P.dma("sp", b_t[:, 0:256], W["B_R"])
    t4 = P.dma("sp", b_t[:, 256:512], W["B_I"])
    wblk = E.SCRB[:, 0:256]
    wneg = E.SCRB[:, 256:512]
    dtoks = [t1, t2, t3, t4]
    last_dma = []
    for mi, mat in enumerate(("R", "I")):
        ao = 1024 * mi
        bo = 256 * mi
        for ch in range(8):
            bi, ps, ftok = E.bank()
            pa = ps[:, 0:256]
            w = list(dtoks) if (mi == 0 and ch == 0) else []
            if ftok is not None:
                w.append(ftok)
            mtok = P.op("pe", lambda e, o=pa, l=a_t[:, ao + ch * 128: ao + (ch + 1) * 128],
                        r=b_t[:, bo:bo + 256]:
                        e.matmul(o, l, r, start=True, stop=True), waits=w, sig=True)
            ctok = P.op("dve", lambda e, o=wblk, i=pa: e.tensor_copy(o, i),
                        waits=[mtok] + last_dma, sig=True)
            E.bank_free(bi, ctok)
            ntok = None
            if mat == "I":
                ntok = P.op("dve", lambda e, o=wneg, i=wblk:
                            e.tensor_single_scalar(o, i, -1.0, ALU.mult), sig=True)
            last_dma = []
            targets = [("W0R" if mat == "R" else "W0I", wblk, ctok)]
            if mat == "I":
                targets.append(("W0In", wneg, ntok))
            for name, srctile, tok in targets:
                dst_full = W[name]
                for b_lo in range(4):
                    b = ch * 4 + b_lo
                    dst = dst_full.rearrange("(d g) (b e) -> d g b e", g=16, e=16)
                    dst = dst[:, :, b, :]
                    src = srctile[b_lo * 32:(b_lo + 1) * 32, :].rearrange(
                        "d (g e) -> d g e", g=16)
                    last_dma.append(P.dma("sp", dst, src, waits=[tok]))
    return last_dma


def emit_l0_ffn(P, E, W, build=False):
    E.scr_reset()
    btoks = []
    if build:
        btoks = emit_l0ffn_build(P, E, W)
        E.scr_reset()
    ln_tok = emit_ln(P, E, lambda g: E.xn(g), lambda g: E.xn(g),
                     src=lambda g: E.xres(g))
    xcr = lambda kg, c0, nc: E.xn(kg, c0, nc)
    xci = lambda kg, c0, nc: E.xn(4 + kg, c0, nc)
    br = W["bias_real"]
    bi_ = W["bias_imag"]

    def mk_cb(goff, bias_col):
        def cb(mg, m0, nm, c0, nc, pa, tok):
            atok = P.op("act", lambda e, o=pa, i=pa, b=bias_col[:, mg:mg + 1]:
                        e.activation(o, i, AF.Identity, bias=b), waits=[tok], sig=True)
            return P.op("dve", lambda e, o=E.xres(goff + mg, c0, nc), i=pa,
                        b=E.xres(goff + mg, c0, nc):
                        e.scalar_tensor_tensor(o, i, 0.5, b, ALU.mult, ALU.add),
                        waits=[atok], sig=True)
        return cb
    wt = btoks[-1] if btoks else None
    mm_fm(P, E, [(W["W0R"], xcr, ln_tok), (W["W0In"], xci, wt)],
          mk_cb(0, br), M=512)
    mm_fm(P, E, [(W["W0I"], xcr, None), (W["W0R"], xci, None)],
          mk_cb(4, bi_), M=512)
    P.barrier()


def emit_l3_attn(P, E, W):
    DT = 0.01
    E.scr_reset()
    ln_tok = emit_ln(P, E, lambda g: E.xn(g), lambda g: E.xn(g),
                     src=lambda g: E.xres(g))
    Tc = 256
    n64 = lambda: E.scrf(Tc)
    OM, AS, SS, DS = n64(), n64(), n64(), n64()
    A0, B0, C0, DD0 = n64(), n64(), n64(), n64()
    A1, B1, C1, DD1 = n64(), n64(), n64(), n64()
    FZ0, FY0, FZ1, FY1 = n64(), n64(), n64(), n64()
    TMP = n64()
    ST = E.scrf(2)
    FZB = E.scrb(Tc)
    FYB = E.scrb(Tc)
    h = lambda a: a[0:64, :]
    one_ap, one_tok = E.constf(1.0)
    P.op("dve", lambda e: e.memset(ST[0:64, :], 0.0), sig=True)
    b_om, b_z, b_B = W["b_omega"], W["b_zeta"], W["b_B"]
    tk = [None]

    def chain(eng, fn, extra=()):
        tk[0] = P.op(eng, fn, waits=[tk[0]] + list(extra))
        return tk[0]

    for c0 in range(0, T, Tc):
        ctl = [(c0, Tc)]

        def om_cb(mg, m0, nm, cc0, nc, pa, tok):
            chain("act", lambda e, o=h(TMP), i=pa[0:64, :], b=b_om[0:64, :]:
                  e.activation(o, i, AF.Abs, bias=b), extra=[tok])
            chain("act", lambda e, o=h(TMP), i=h(TMP):
                  e.activation(o, i, AF.Exp, scale=-1.0))
            chain("act", lambda e, o=h(TMP), i=h(TMP), b=one_ap[0:64, :]:
                  e.activation(o, i, AF.Ln, bias=b), extra=[one_tok])
            chain("dve", lambda e, o=h(OM), i=pa[0:64, :], b=b_om[0:64, :]:
                  e.tensor_scalar(o, i, b, 0.0, ALU.add, ALU.max))
            chain("dve", lambda e, o=h(OM), a=h(OM), b2=h(TMP):
                  e.tensor_tensor(o, a, b2, ALU.add))
            return chain("dve", lambda e, o=h(OM), i=h(OM):
                         e.tensor_scalar(o, i, 1e-4, 100.0, ALU.max, ALU.min))
        mm_fm(P, E, [(W["W_omT"], lambda kg, a, n: E.xn(kg, a, n),
                      ln_tok if c0 == 0 else None)], om_cb, M=64, col_tiles=ctl)

        def z_cb(mg, m0, nm, cc0, nc, pa, tok):
            chain("act", lambda e, o=h(TMP), i=pa[0:64, :], b=b_z[0:64, :]:
                  e.activation(o, i, AF.Sigmoid, bias=b), extra=[tok])
            return chain("dve", lambda e, o=h(DS), i=h(TMP):
                         e.tensor_scalar(o, i, -1.0, 1.0, ALU.mult, ALU.add))
        mm_fm(P, E, [(W["W_zT"], lambda kg, a, n: E.xn(kg, a, n), None)],
              z_cb, M=64, col_tiles=ctl)

        chain("act", lambda e, o=h(AS), i=h(OM): e.activation(o, i, AF.Square))
        chain("dve", lambda e, o=h(TMP), i=h(AS):
              e.tensor_scalar(o, i, DT * DT, 1.0, ALU.mult, ALU.add))
        chain("dve", lambda e, o=h(SS), i=h(TMP): e.reciprocal(o, i))
        chain("dve", lambda e, o=h(A0), a=h(DS), b2=h(SS):
              e.tensor_tensor(o, a, b2, ALU.mult))
        chain("dve", lambda e, o=h(DD0), i=h(A0): e.tensor_copy(o, i))
        chain("dve", lambda e, o=h(B0), i=h(AS), b2=h(A0):
              e.scalar_tensor_tensor(o, i, -DT, b2, ALU.mult, ALU.mult))
        chain("dve", lambda e, o=h(C0), i=h(A0): e.tensor_scalar_mul(o, i, DT))

        def f_cb(mg, m0, nm, cc0, nc, pa, tok):
            dst = FZ0 if m0 == 0 else FY0
            return chain("dve", lambda e, o=h(dst), i=pa[0:64, :],
                         b=b_B[m0:m0 + 64, :]:
                         e.tensor_scalar(o, i, b, None, ALU.add), extra=[tok])
        mm_fm(P, E, [(W["W_BT"], lambda kg, a, n: E.xn(kg, a, n), None)],
              f_cb, M=128, Ms=[(0, 64), (64, 64)], col_tiles=ctl)

        cur = (A0, B0, C0, DD0, FZ0, FY0)
        nxt = (A1, B1, C1, DD1, FZ1, FY1)
        for k in range(8):
            s = 1 << k
            (a, b, c, dd, fz, fy) = cur
            (an, bn, cn, ddn, fzn, fyn) = nxt
            n = Tc - s
            sh = lambda p: p[0:64, 0:n]
            up = lambda p: p[0:64, s:Tc]
            for (src, dst) in ((a, an), (b, bn), (c, cn), (dd, ddn),
                               (fz, fzn), (fy, fyn)):
                chain("dve", lambda e, o=dst[0:64, 0:s], i=src[0:64, 0:s]:
                      e.tensor_copy(o, i))
            for (dst, u1, v1, u2, v2) in (
                    (an, a, a, b, c), (bn, a, b, b, dd),
                    (cn, c, a, dd, c), (ddn, c, b, dd, dd)):
                chain("dve", lambda e, o=up(dst), x1=up(u1), x2=sh(v1):
                      e.tensor_tensor(o, x1, x2, ALU.mult))
                chain("dve", lambda e, o=up(TMP), x1=up(u2), x2=sh(v2):
                      e.tensor_tensor(o, x1, x2, ALU.mult))
                chain("dve", lambda e, o=up(dst), x1=up(dst), x2=up(TMP):
                      e.tensor_tensor(o, x1, x2, ALU.add))
            for (dst, u1, u2, f2) in ((fzn, a, b, fz), (fyn, c, dd, fy)):
                chain("dve", lambda e, o=up(dst), x1=up(u1), x2=sh(fz):
                      e.tensor_tensor(o, x1, x2, ALU.mult))
                chain("dve", lambda e, o=up(TMP), x1=up(u2), x2=sh(fy):
                      e.tensor_tensor(o, x1, x2, ALU.mult))
                chain("dve", lambda e, o=up(dst), x1=up(dst), x2=up(TMP):
                      e.tensor_tensor(o, x1, x2, ALU.add))
                chain("dve", lambda e, o=up(dst), x1=up(dst), x2=up(f2):
                      e.tensor_tensor(o, x1, x2, ALU.add))
            cur, nxt = nxt, cur
        (a, b, c, dd, fz, fy) = cur
        chain("dve", lambda e, o=h(fz), i=h(a), s1=ST[0:64, 0:1], b2=h(fz):
              e.scalar_tensor_tensor(o, i, s1, b2, ALU.mult, ALU.add))
        chain("dve", lambda e, o=h(fz), i=h(b), s1=ST[0:64, 1:2], b2=h(fz):
              e.scalar_tensor_tensor(o, i, s1, b2, ALU.mult, ALU.add))
        chain("dve", lambda e, o=h(fy), i=h(c), s1=ST[0:64, 0:1], b2=h(fy):
              e.scalar_tensor_tensor(o, i, s1, b2, ALU.mult, ALU.add))
        chain("dve", lambda e, o=h(fy), i=h(dd), s1=ST[0:64, 1:2], b2=h(fy):
              e.scalar_tensor_tensor(o, i, s1, b2, ALU.mult, ALU.add))
        chain("dve", lambda e, o=ST[0:64, 0:1], i=fz[0:64, Tc - 1:Tc]:
              e.tensor_copy(o, i))
        chain("dve", lambda e, o=ST[0:64, 1:2], i=fy[0:64, Tc - 1:Tc]:
              e.tensor_copy(o, i))
        chain("dve", lambda e, o=FZB[0:64, :], i=h(fz): e.tensor_copy(o, i))
        x_tok = chain("dve", lambda e, o=FYB[0:64, :], i=h(fy): e.tensor_copy(o, i))

        def out_cb(mg, m0, nm, cc0, nc, pa, tok):
            return P.op("dve", lambda e, o=E.xres(mg, cc0, nc), a=E.xres(mg, cc0, nc), b2=pa:
                        e.tensor_tensor(o, a, b2, ALU.add), waits=[tok], sig=True)
        mm_fm(P, E, [(W["C_T"][0:64, :], lambda kg, cc0, n: FZB[0:64, 0:n], x_tok),
                     (W["C_T"][64:128, :], lambda kg, cc0, n: FYB[0:64, 0:n], None)],
              out_cb, M=D, col_tiles=ctl)
        mm_fm(P, E, [(W["DmatT"], lambda kg, cc0, n: E.xn(kg, cc0, n), None)],
              out_cb, M=D, col_tiles=ctl)
    P.barrier()


def emit_l0_attn(P, E, W):
    E.scr_reset()
    ln_tok = emit_ln(P, E, lambda g: E.xn(g), lambda g: E.xn(g),
                     src=lambda g: E.xres(g))
    Tc = 256
    HG = 12
    ALPHA = E.scrf(HG * Tc)
    XBETA = E.scrf(HG * Tc)
    GG = E.scrb(HG * Tc)
    XH = E.scrb(HG * (Tc + 3))
    XC = E.scrb(HG * Tc)
    GHB = E.scrb(HG * Tc)
    al = lambda g: ALPHA[:, g * Tc:(g + 1) * Tc]
    xb = lambda g: XBETA[:, g * Tc:(g + 1) * Tc]
    gg = lambda g: GG[:, g * Tc:(g + 1) * Tc]
    xh = lambda g, j=3, n=Tc: XH[:, g * (Tc + 3) + j: g * (Tc + 3) + j + n]
    xc = lambda g: XC[:, g * Tc:(g + 1) * Tc]
    gh = lambda g: GHB[:, g * Tc:(g + 1) * Tc]
    lam, convw, gpb, ipb = W["lam"], W["convw"], W["gp_b"], W["ip_b"]
    c8 = E.cons(HG)
    HC = E.cons(HG)
    one_ap, one_tok = E.constf(1.0)
    tk = [None]

    def chain(eng, fn, extra=()):
        tk[0] = P.op(eng, fn, waits=[tk[0]] + list(extra))
        return tk[0]

    chain("act", lambda e, o=c8, i=lam: e.activation(o, i, AF.Abs), extra=[one_tok])
    chain("act", lambda e, o=c8, i=c8: e.activation(o, i, AF.Exp, scale=-1.0))
    chain("act", lambda e, o=c8, i=c8, b=one_ap: e.activation(o, i, AF.Ln, bias=b))
    chain("dve", lambda e, o=HC, i=lam: e.tensor_scalar_max(o, i, 0.0))
    chain("dve", lambda e, o=c8, a=c8, b=HC: e.tensor_tensor(o, a, b, ALU.add))
    chain("dve", lambda e, o=c8, i=c8: e.tensor_scalar_mul(o, i, -8.0))
    chain("dve", lambda e, o=HC, i=HC: e.memset(o, 0.0))
    chain("dve", lambda e, o=XH[:], i=XH[:]: e.memset(o, 0.0))

    for c0 in range(0, T, Tc):
        ctl = [(c0, Tc)]
        if c0 > 0:
            for g in range(HG):
                chain("dve", lambda e, o=xh(g, 0, 3), i=xh(g, Tc, 3):
                      e.tensor_copy(o, i))

        def proj_cb(mg, m0, nm, cc0, nc, pa, tok):
            if mg < HG:
                return chain("act", lambda e, o=gg(mg), i=pa:
                             e.activation(o, i, AF.Gelu), extra=[tok])
            return chain("dve", lambda e, o=xh(mg - HG), i=pa:
                         e.tensor_copy(o, i), extra=[tok])
        mm_fm(P, E, [(W["proj_wT"], lambda kg, a, n: E.xn(kg, a, n),
                      ln_tok if c0 == 0 else None)], proj_cb, M=3072,
              col_tiles=ctl)
        for g in range(HG):
            chain("dve", lambda e, o=xc(g), i=xh(g, 0), s=convw[:, g * 4:g * 4 + 1]:
                  e.tensor_single_scalar(o, i, s, ALU.mult))
            for j in range(1, 4):
                chain("dve", lambda e, o=xc(g), i=xh(g, j), s=convw[:, g * 4 + j:g * 4 + j + 1],
                      b=xc(g): e.scalar_tensor_tensor(o, i, s, b, ALU.mult, ALU.add))
            chain("dve", lambda e, o=xc(g), i=xc(g), s=W["conv_b"][:, g:g + 1]:
                  e.tensor_single_scalar(o, i, s, ALU.add))

        def rec_cb(mg, m0, nm, cc0, nc, pa, tok):
            bi2, ps2, ft2 = E.bank()
            chain("act", lambda e, o=ps2[:, 0:nc], i=pa, b=gpb[:, mg:mg + 1]:
                  e.activation(o, i, AF.Sigmoid, bias=b),
                  extra=[tok] + ([ft2] if ft2 else []))
            chain("dve", lambda e, o=al(mg), i=ps2[:, 0:nc], s=c8[:, mg:mg + 1]:
                  e.tensor_single_scalar(o, i, s, ALU.mult))
            E.bank_free(bi2, tk[0])
            tka = chain("act", lambda e, o=al(mg), i=al(mg): e.activation(o, i, AF.Exp))
            return tka
        mm_fm(P, E, [(W["gp_wT"], lambda kg, a, n: xc(kg), tk[0])], rec_cb,
              M=1536, col_tiles=ctl)

        def inp_cb(mg, m0, nm, cc0, nc, pa, tok):
            bi2, ps2, ft2 = E.bank()
            chain("act", lambda e, o=xb(mg), i=al(mg): e.activation(o, i, AF.Square),
                  extra=[tok] + ([ft2] if ft2 else []))
            chain("dve", lambda e, o=xb(mg), i=xb(mg):
                  e.tensor_scalar(o, i, -1.0, 1.0 + 1e-6, ALU.mult, ALU.add))
            chain("act", lambda e, o=xb(mg), i=xb(mg): e.activation(o, i, AF.Ln))
            chain("act", lambda e, o=xb(mg), i=xb(mg):
                  e.activation(o, i, AF.Exp, scale=0.5))
            chain("act", lambda e, o=ps2[:, 0:nc], i=pa, b=ipb[:, mg:mg + 1]:
                  e.activation(o, i, AF.Sigmoid, bias=b))
            chain("dve", lambda e, o=xb(mg), a=xb(mg), b=ps2[:, 0:nc]:
                  e.tensor_tensor(o, a, b, ALU.mult))
            chain("dve", lambda e, o=ps2[:, 0:nc], i=xc(mg): e.tensor_copy(o, i))
            tkd = chain("dve", lambda e, o=xb(mg), a=xb(mg), b=ps2[:, 0:nc]:
                        e.tensor_tensor(o, a, b, ALU.mult))
            E.bank_free(bi2, tkd)
            return tkd
        mm_fm(P, E, [(W["ip_wT"], lambda kg, a, n: xc(kg), None)], inp_cb,
              M=1536, col_tiles=ctl)

        for g in range(HG):
            bi2, ps2, ft2 = E.bank()
            chain("dve", lambda e, o=ps2[:, 0:Tc], a=al(g), d1=xb(g), init=HC[:, g:g + 1]:
                  e.tensor_tensor_scan(o, a, d1, init, ALU.mult, ALU.add),
                  extra=[ft2] if ft2 else [])
            chain("dve", lambda e, o=HC[:, g:g + 1], i=ps2[:, Tc - 1:Tc]:
                  e.tensor_copy(o, i))
            tkd = chain("dve", lambda e, o=gh(g), a=gg(g), b=ps2[:, 0:Tc]:
                        e.tensor_tensor(o, a, b, ALU.mult))
            E.bank_free(bi2, tkd)

        def out_cb(mg, m0, nm, cc0, nc, pa, tok):
            return P.op("dve", lambda e, o=E.xres(mg, cc0, nc), a=E.xres(mg, cc0, nc),
                        b=pa: e.tensor_tensor(o, a, b, ALU.add), waits=[tok], sig=True)
        mm_fm(P, E, [(W["out_wT"], lambda kg, a, n: gh(kg), tk[0])], out_cb,
              M=D, col_tiles=ctl)
    P.barrier()


def emit_l1_attn(P, E, W):
    E.scr_reset()
    ln_tok = emit_ln(P, E, lambda g: E.xn(g), lambda g: E.xn(g),
                     src=lambda g: E.xres(g))
    for g in range(G):
        ln_tok = P.op("dve", lambda e, o=E.xres(g), a=E.xres(g), b=E.xn(g):
                      e.tensor_tensor(o, a, b, ALU.add),
                      waits=[ln_tok] if g == 0 else (), sig=(g == G - 1))
    ln2 = emit_ln(P, E, lambda g: E.xn(g), lambda g: E.xn(g),
                  affine=(W["ln_g"], W["ln_b"]), src_tok=ln_tok)
    Tc = 256
    EG = 16
    A_ = E.scrf(256)
    HCM = E.scrf(256)
    Y = E.scrf(EG * Tc)
    DA4 = E.scrf(4 * Tc)
    DAs = DA4[:, 0:Tc]
    DTBs = E.scrf(Tc)
    Hs = E.scrf(Tc)
    XU = E.scrb(EG * (Tc + 3))
    Z = E.scrb(EG * Tc)
    U2 = E.scrb(EG * Tc)
    DT_ = E.scrb(EG * Tc)
    DTU = E.scrb(EG * Tc)
    XDT = E.scrb(Tc)
    XB_ = E.scrb(Tc)
    XC_ = E.scrb(Tc)
    yv = lambda g: Y[:, g * Tc:(g + 1) * Tc]
    xu = lambda g, j=3, n=Tc: XU[:, g * (Tc + 3) + j: g * (Tc + 3) + j + n]
    zv = lambda g: Z[:, g * Tc:(g + 1) * Tc]
    u2v = lambda g: U2[:, g * Tc:(g + 1) * Tc]
    dtv = lambda g: DT_[:, g * Tc:(g + 1) * Tc]
    dtuv = lambda g: DTU[:, g * Tc:(g + 1) * Tc]
    one_ap, one_tok = E.constf(1.0)
    convw, conv_b, b_dt, Dp, rs = (W["convw"], W["conv_b"], W["b_dt"],
                                   W["D_param"], W["rs"])
    tk = [None]

    def chain(eng, fn, extra=()):
        tk[0] = P.op(eng, fn, waits=[tk[0]] + list(extra))
        return tk[0]

    chain("act", lambda e, o=A_, i=W["ALOGC"]: e.activation(o, i, AF.Exp),
          extra=[one_tok])
    chain("dve", lambda e, o=A_, i=A_: e.tensor_scalar_mul(o, i, -1.0))
    chain("dve", lambda e, o=HCM, i=HCM: e.memset(o, 0.0))
    chain("dve", lambda e, o=XU[:], i=XU[:]: e.memset(o, 0.0))

    for c0 in range(0, T, Tc):
        ctl = [(c0, Tc)]
        if c0 > 0:
            for g in range(EG):
                chain("dve", lambda e, o=xu(g, 0, 3), i=xu(g, Tc, 3):
                      e.tensor_copy(o, i))

        def in_cb(mg, m0, nm, cc0, nc, pa, tok):
            if mg < EG:
                return chain("dve", lambda e, o=xu(mg), i=pa: e.tensor_copy(o, i),
                             extra=[tok])
            chain("act", lambda e, o=zv(mg - EG), i=pa:
                  e.activation(o, i, AF.Sigmoid), extra=[tok])
            return chain("dve", lambda e, o=zv(mg - EG), a=zv(mg - EG), b=pa:
                         e.tensor_tensor(o, a, b, ALU.mult))
        mm_fm(P, E, [(W["W_in"], lambda kg, a, n: E.xn(kg, a, n),
                      ln2 if c0 == 0 else None)], in_cb, M=4096, col_tiles=ctl)
        for g in range(EG):
            bi2, ps2, ft2 = E.bank()
            chain("dve", lambda e, o=ps2[:, 0:Tc], i=xu(g, 0), s=convw[:, g * 4:g * 4 + 1]:
                  e.tensor_single_scalar(o, i, s, ALU.mult),
                  extra=[ft2] if ft2 else [])
            for j in range(1, 4):
                chain("dve", lambda e, o=ps2[:, 0:Tc], i=xu(g, j),
                      s=convw[:, g * 4 + j:g * 4 + j + 1], b=ps2[:, 0:Tc]:
                      e.scalar_tensor_tensor(o, i, s, b, ALU.mult, ALU.add))
            chain("dve", lambda e, o=DAs, i=ps2[:, 0:Tc], s=conv_b[:, g:g + 1]:
                  e.tensor_single_scalar(o, i, s, ALU.add))
            chain("act", lambda e, o=u2v(g), i=DAs: e.activation(o, i, AF.Sigmoid))
            tka = chain("dve", lambda e, o=u2v(g), a=u2v(g), b=DAs:
                        e.tensor_tensor(o, a, b, ALU.mult))
            E.bank_free(bi2, tka)

        def dbl_cb(dst):
            def cb(mg, m0, nm, cc0, nc, pa, tok):
                return chain("dve", lambda e, o=dst[0:nm, 0:nc], i=pa:
                             e.tensor_copy(o, i), extra=[tok])
            return cb
        mm_fm(P, E, [(W["W_xp_dt"], lambda kg, a, n: u2v(kg), tk[0])],
              dbl_cb(XDT), M=64, col_tiles=ctl)
        mm_fm(P, E, [(W["W_xp_B"], lambda kg, a, n: u2v(kg), None)],
              dbl_cb(XB_), M=16, col_tiles=ctl)
        mm_fm(P, E, [(W["W_xp_C"], lambda kg, a, n: u2v(kg), None)],
              dbl_cb(XC_), M=16, col_tiles=ctl)

        def dt_cb(mg, m0, nm, cc0, nc, pa, tok):
            chain("act", lambda e, o=DAs, i=pa, b=b_dt[:, mg:mg + 1]:
                  e.activation(o, i, AF.Abs, bias=b), extra=[tok])
            chain("act", lambda e, o=DAs, i=DAs: e.activation(o, i, AF.Exp, scale=-1.0))
            chain("act", lambda e, o=DAs, i=DAs, b=one_ap:
                  e.activation(o, i, AF.Ln, bias=b))
            chain("dve", lambda e, o=dtv(mg), i=pa, b=b_dt[:, mg:mg + 1]:
                  e.tensor_scalar(o, i, b, 0.0, ALU.add, ALU.max))
            tkd = chain("dve", lambda e, o=dtv(mg), a=dtv(mg), b=DAs:
                        e.tensor_tensor(o, a, b, ALU.add))
            return tkd
        mm_fm(P, E, [(W["W_dt"], lambda kg, a, n: XDT[0:64, 0:n], tk[0])],
              dt_cb, M=2048, col_tiles=ctl)
        for g in range(EG):
            chain("dve", lambda e, o=dtuv(g), a=dtv(g), b=u2v(g):
                  e.tensor_tensor(o, a, b, ALU.mult))
        chain("dve", lambda e, o=Y[:], i=Y[:]: e.memset(o, 0.0))
        for n in range(16):
            biB, psB, ftB = E.bank()
            biC, psC, ftC = E.bank()
            chain("pe", lambda e, o=psB[:, 0:Tc], l=W["SELB"][:, n * 128:(n + 1) * 128],
                  r=XB_[0:16, 0:Tc]: e.matmul(o, l, r, start=True, stop=True),
                  extra=[ftB] if ftB else [])
            chain("pe", lambda e, o=psC[:, 0:Tc], l=W["SELB"][:, n * 128:(n + 1) * 128],
                  r=XC_[0:16, 0:Tc]: e.matmul(o, l, r, start=True, stop=True),
                  extra=[ftC] if ftC else [])
            for g4 in range(0, EG, 4):
                for j in range(4):
                    g = g4 + j
                    col = n * 16 + g
                    chain("dve", lambda e, o=DA4[:, j * Tc:(j + 1) * Tc], i=dtv(g),
                          s=A_[:, col:col + 1]:
                          e.tensor_single_scalar(o, i, s, ALU.mult))
                chain("act", lambda e, o=DA4, i=DA4: e.activation(o, i, AF.Exp))
                for j in range(4):
                    g = g4 + j
                    col = n * 16 + g
                    das = DA4[:, j * Tc:(j + 1) * Tc]
                    chain("dve", lambda e, o=DTBs, a=dtuv(g), b=psB[:, 0:Tc]:
                          e.tensor_tensor(o, a, b, ALU.mult))
                    chain("dve", lambda e, o=Hs, a=das, d1=DTBs, init=HCM[:, col:col + 1]:
                          e.tensor_tensor_scan(o, a, d1, init, ALU.mult, ALU.add))
                    chain("dve", lambda e, o=HCM[:, col:col + 1], i=Hs[:, Tc - 1:Tc]:
                          e.tensor_copy(o, i))
                    chain("dve", lambda e, o=Hs, a=Hs, b=psC[:, 0:Tc]:
                          e.tensor_tensor(o, a, b, ALU.mult))
                    chain("dve", lambda e, o=yv(g), a=yv(g), b=Hs:
                          e.tensor_tensor(o, a, b, ALU.add))
            E.bank_free(biB, tk[0])
            E.bank_free(biC, tk[0])
        for g in range(EG):
            chain("dve", lambda e, o=yv(g), i=u2v(g), s=Dp[:, g:g + 1], b=yv(g):
                  e.scalar_tensor_tensor(o, i, s, b, ALU.mult, ALU.add))
            chain("dve", lambda e, o=yv(g), a=yv(g), b=zv(g):
                  e.tensor_tensor(o, a, b, ALU.mult))
            chain("dve", lambda e, o=dtuv(g), i=yv(g): e.tensor_copy(o, i))

        def out_cb(mg, m0, nm, cc0, nc, pa, tok):
            return P.op("dve", lambda e, o=E.xres(mg, cc0, nc), i=pa, s=rs[:, 0:1],
                        b=E.xres(mg, cc0, nc):
                        e.scalar_tensor_tensor(o, i, s, b, ALU.mult, ALU.add),
                        waits=[tok], sig=True)
        mm_fm(P, E, [(W["W_out"], lambda kg, a, n: dtuv(kg), tk[0])], out_cb,
              M=D, col_tiles=ctl)
    P.barrier()


def emit_top2(P, E, W, chain, LG, GT, nc_cols, renorm):
    idf, ltm, idb8 = W["identf"], W["LTM"], W["IDB8"]
    rt = E.rt_area
    RT = rt[:, 0:8]
    PW = rt[:, 8:72]
    PW2 = rt[:, 72:136]
    RANK = rt[:, 136:144]
    ME = rt[:, 144:152]
    M1 = rt[:, 152:153]
    DEN = rt[:, 153:154]
    R4 = rt[:, 154:158]
    R2 = rt[:, 158:160]
    pw3 = lambda a: a.rearrange("p (e f) -> p e f", e=8)
    last = [None]
    for tt in range(nc_cols // 128):
        bi, ps, ft = E.bank()
        chain("pe", lambda e, o=ps[:, 0:8], i=LG[0:8, tt * 128:(tt + 1) * 128],
              d=idf[0:8, 0:8]: e.transpose(o, i, d), extra=[ft] if ft else [])
        t_rt = chain("dve", lambda e, o=RT, i=ps[:, 0:8]: e.tensor_copy(o, i))
        E.bank_free(bi, t_rt)
        b1 = RT.unsqueeze(2).broadcast_to([128, 8, 8])
        b2 = RT.unsqueeze(1).broadcast_to([128, 8, 8])
        chain("dve", lambda e, o=pw3(PW), a=b1, b=b2:
              e.tensor_tensor(o, a, b, ALU.is_lt))
        chain("dve", lambda e, o=pw3(PW2), a=b1, b=b2:
              e.tensor_tensor(o, a, b, ALU.is_equal))
        chain("dve", lambda e, o=PW2, a=PW2, b=ltm:
              e.tensor_tensor(o, a, b, ALU.mult))
        chain("dve", lambda e, o=PW, a=PW, b=PW2:
              e.tensor_tensor(o, a, b, ALU.add))
        p3 = pw3(PW)
        chain("dve", lambda e, o=p3[:, :, 0:4], a=p3[:, :, 0:4], b=p3[:, :, 4:8]:
              e.tensor_tensor(o, a, b, ALU.add))
        chain("dve", lambda e, o=p3[:, :, 0:2], a=p3[:, :, 0:2], b=p3[:, :, 2:4]:
              e.tensor_tensor(o, a, b, ALU.add))
        chain("dve", lambda e, o=RANK.unsqueeze(2), a=p3[:, :, 0:1], b=p3[:, :, 1:2]:
              e.tensor_tensor(o, a, b, ALU.add))
        chain("dve", lambda e, o=R4, a=RT[:, 0:4], b=RT[:, 4:8]:
              e.tensor_tensor(o, a, b, ALU.max))
        chain("dve", lambda e, o=R2, a=R4[:, 0:2], b=R4[:, 2:4]:
              e.tensor_tensor(o, a, b, ALU.max))
        chain("dve", lambda e, o=M1, a=R2[:, 0:1], b=R2[:, 1:2]:
              e.tensor_tensor(o, a, b, ALU.max))
        chain("dve", lambda e, o=M1, i=M1: e.tensor_scalar_mul(o, i, -1.0))
        chain("act", lambda e, o=RT, i=RT, b=M1: e.activation(o, i, AF.Exp, bias=b))
        chain("dve", lambda e, o=RANK, i=RANK:
              e.tensor_single_scalar(o, i, 1.5, ALU.is_lt))
        chain("dve", lambda e, o=ME, a=RT, b=RANK:
              e.tensor_tensor(o, a, b, ALU.mult))
        S = ME if renorm else RT
        chain("dve", lambda e, o=R4, a=S[:, 0:4], b=S[:, 4:8]:
              e.tensor_tensor(o, a, b, ALU.add))
        chain("dve", lambda e, o=R2, a=R4[:, 0:2], b=R4[:, 2:4]:
              e.tensor_tensor(o, a, b, ALU.add))
        chain("dve", lambda e, o=DEN, a=R2[:, 0:1], b=R2[:, 1:2]:
              e.tensor_tensor(o, a, b, ALU.add))
        chain("dve", lambda e, o=DEN, i=DEN: e.reciprocal(o, i))
        chain("dve", lambda e, o=ME, i=ME, s=DEN:
              e.tensor_single_scalar(o, i, s, ALU.mult))
        bi2, ps2, ft2 = E.bank()
        chain("pe", lambda e, o=ps2[0:8, 0:128], i=ME, d=idf[:]:
              e.transpose(o, i, d), extra=[ft2] if ft2 else [])
        tkd = chain("dve", lambda e, o=GT[0:8, tt * 128:(tt + 1) * 128],
                    i=ps2[0:8, 0:128]: e.tensor_copy(o, i))
        E.bank_free(bi2, tkd)
        last[0] = tkd
    return last[0]


def tk_cur(P):
    for nm in ("s_dve",):
        pass
    return ("s_dve", P.sems["s_dve"][1])


L2_STAGE = 99


def emit_moe(P, E, W, layer):
    E.scr_reset()
    ln_tok = emit_ln(P, E, lambda g: E.xn(g), lambda g: E.xn(g),
                     src=lambda g: E.xres(g))
    Tc = 512
    E.rt_area = E.scrf(176)
    LG = E.scrf(Tc)
    GT = E.scrb(Tc)
    GROW = E.scrb(Tc)
    tk = [None]

    def chain(eng, fn, extra=()):
        tk[0] = P.op(eng, fn, waits=[tk[0]] + list(extra))
        return tk[0]
    if layer == 1:
        H_ = E.scrb(32 * Tc)
        ACC = E.scrf(G * Tc)
        hv = lambda g: H_[:, g * Tc:(g + 1) * Tc]
        av = lambda g: ACC[:, g * Tc:(g + 1) * Tc]
        for c0 in range(0, T, Tc):
            ctl = [(c0, Tc)]

            def rout_cb(mg, m0, nm, cc0, ncx, pa, tok):
                return chain("dve", lambda e, o=LG[0:8, 0:Tc], i=pa:
                             e.tensor_copy(o, i), extra=[tok])
            mm_fm(P, E, [(W["router_wT"], lambda kg, a, n: E.xn(kg, a, n),
                          ln_tok if c0 == 0 else None)], rout_cb, M=8,
                  col_tiles=ctl)
            gt_tok = emit_top2(P, E, W, chain, LG, GT, Tc, renorm=False)
            for g in range(G):
                chain("dve", lambda e, o=av(g), i=av(g): e.memset(o, 0.0))
            for ei in range(8):
                bi, ps, ft = E.bank()
                stok = P.op("pe", lambda e, o=ps[0:1, 0:Tc], l=W["IDB8"][:, ei:ei + 1],
                            r=GT[0:8, 0:Tc]: e.matmul(o, l, r, start=True, stop=True),
                            waits=[gt_tok] + ([ft] if ft else []))
                gtok = P.op("dve", lambda e, o=GROW[0:1, 0:Tc], i=ps[0:1, 0:Tc]:
                            e.tensor_copy(o, i), waits=[stok])
                biw, psw, ftw = E.bank(hold=True)
                wtok = P.op("pe", lambda e, o=psw[:, 0:Tc], r=GROW[0:1, 0:Tc]:
                            e.matmul(o, E.ONE_R[0:1, 0:128], r, start=True, stop=True),
                            waits=[gtok] + ([ftw] if ftw else []))
                E.bank_free(bi, wtok)

                def h_cb(mg, m0, nm, cc0, ncx, pa, tok):
                    chain("act", lambda e, o=hv(mg), i=pa:
                          e.activation(o, i, AF.Gelu), extra=[tok])
                    return chain("dve", lambda e, o=hv(mg), a=hv(mg), b=psw[:, 0:Tc]:
                                 e.tensor_tensor(o, a, b, ALU.mult), extra=[wtok])
                mm_fm(P, E, [(W["w1T"][ei * 1024:(ei + 1) * 1024, :],
                              lambda kg, a, n: E.xn(kg, a, n),
                              ln_tok if (c0 == 0 and ei == 0) else wtok)],
                      h_cb, M=4096, col_tiles=ctl)
                E.bank_free(biw, tk[0])

                def y_cb(mg, m0, nm, cc0, ncx, pa, tok):
                    return chain("dve", lambda e, o=av(mg), a=av(mg), b=pa:
                                 e.tensor_tensor(o, a, b, ALU.add), extra=[tok])
                mm_fm(P, E, [(W["w2T"][ei * 4096:ei * 4096 + 2048, :],
                              lambda kg, a, n: hv(kg), tk[0]),
                             (W["w2T"][ei * 4096 + 2048:(ei + 1) * 4096, :],
                              lambda kg, a, n: hv(16 + kg), None)],
                      y_cb, M=D, col_tiles=ctl)
            for g in range(G):
                chain("dve", lambda e, o=E.xres(g, c0, Tc), i=av(g),
                      b=E.xres(g, c0, Tc):
                      e.scalar_tensor_tensor(o, i, 0.5, b, ALU.mult, ALU.add))
    else:
        W["rb1x2"] = E.cons(1)
        P.op("dve", lambda e, o=W["rb1x2"][0:64, :], i=W["rb1"][0:64, :]:
             e.tensor_scalar_mul(o, i, 2.0), sig=True)
        SP = E.scrb(16 * Tc)
        CONTB = E.scrb(Tc)
        T1B = E.scrb(Tc)
        H_ = E.scrb(G * Tc)
        ACCB = E.scrb(Tc)
        LNO = E.scrb(G * Tc)
        ACC = E.scrf(Tc)
        DEC = E.scrf(G * Tc)
        spv = lambda g: SP[:, g * Tc:(g + 1) * Tc]
        hv = lambda g: H_[:, g * Tc:(g + 1) * Tc]
        dcv = lambda g: DEC[:, g * Tc:(g + 1) * Tc]
        lnv = lambda g: LNO[:, g * Tc:(g + 1) * Tc]
        for c0 in range(0, T, Tc):
            ctl = [(c0, Tc)]

            def enc_cb(mg, m0, nm, cc0, ncx, pa, tok):
                return chain("act", lambda e, o=spv(mg), i=pa,
                             b=W["enc_b"][:, mg:mg + 1]:
                             e.activation(o, i, AF.Sigmoid, bias=b), extra=[tok])
            mm_fm(P, E, [(W["enc_W"], lambda kg, a, n: E.xn(kg, a, n),
                          ln_tok if c0 == 0 else None)], enc_cb, M=2048,
                  col_tiles=ctl)

            if L2_STAGE < 1:
                continue

            def s2c_cb(mg, m0, nm, cc0, ncx, pa, tok):
                chain("dve", lambda e, o=pa, i=pa, s=W["s2c_b"][0:64, 0:1]:
                      e.tensor_single_scalar(o, i, s, ALU.add), extra=[tok])
                return chain("dve", lambda e, o=CONTB[0:64, 0:Tc], i=pa:
                             e.tensor_copy(o, i))
            mm_fm(P, E, [(W["s2c_W"], lambda kg, a, n: spv(kg), tk[0])],
                  s2c_cb, M=64, col_tiles=ctl)

            def r1_cb(mg, m0, nm, cc0, ncx, pa, tok):
                chain("act", lambda e, o=T1B[0:64, 0:Tc], i=pa,
                      b=W["rb1x2"][0:64, 0:1]:
                      e.activation(o, i, AF.Sigmoid, bias=b, scale=2.0), extra=[tok])
                return chain("dve", lambda e, o=T1B[0:64, 0:Tc], i=T1B[0:64, 0:Tc]:
                             e.tensor_scalar(o, i, 2.0, -1.0, ALU.mult, ALU.add))
            mm_fm(P, E, [(W["rW1"], lambda kg, a, n: CONTB[0:64, 0:n], tk[0])],
                  r1_cb, M=64, col_tiles=ctl)

            def r2_cb(mg, m0, nm, cc0, ncx, pa, tok):
                chain("dve", lambda e, o=pa, i=pa, s=W["rb2"][0:8, 0:1]:
                      e.tensor_single_scalar(o, i, s, ALU.add), extra=[tok])
                return chain("dve", lambda e, o=LG[0:8, 0:Tc], i=pa:
                             e.tensor_copy(o, i))
            mm_fm(P, E, [(W["rW2"], lambda kg, a, n: T1B[0:64, 0:n], tk[0])],
                  r2_cb, M=8, col_tiles=ctl)
            if L2_STAGE < 2:
                continue
            gt_tok = emit_top2(P, E, W, chain, LG, GT, Tc, renorm=True)
            chain("dve", lambda e, o=ACC[0:64, 0:Tc], i=ACC[0:64, 0:Tc]:
                  e.memset(o, 0.0))
            cont_tok = tk[0]
            for ei in range(8 if L2_STAGE >= 3 else 0):
                bi, ps, ft = E.bank()
                stok = P.op("pe", lambda e, o=ps[0:1, 0:Tc], l=W["IDB8"][:, ei:ei + 1],
                            r=GT[0:8, 0:Tc]: e.matmul(o, l, r, start=True, stop=True),
                            waits=[gt_tok] + ([ft] if ft else []))
                gtok = P.op("dve", lambda e, o=GROW[0:1, 0:Tc], i=ps[0:1, 0:Tc]:
                            e.tensor_copy(o, i), waits=[stok])
                biw, psw, ftw = E.bank(hold=True)
                wtok = P.op("pe", lambda e, o=psw[:, 0:Tc], r=GROW[0:1, 0:Tc]:
                            e.matmul(o, E.ONE_R[0:1, 0:128], r, start=True, stop=True),
                            waits=[gtok] + ([ftw] if ftw else []))
                E.bank_free(bi, wtok)

                def h_cb(mg, m0, nm, cc0, ncx, pa, tok):
                    chain("act", lambda e, o=hv(mg), i=pa,
                          b=W["be1"][:, ei * 8 + mg:ei * 8 + mg + 1]:
                          e.activation(o, i, AF.Relu, bias=b), extra=[tok])
                    return chain("dve", lambda e, o=hv(mg), a=hv(mg), b=psw[:, 0:Tc]:
                                 e.tensor_tensor(o, a, b, ALU.mult), extra=[wtok])
                mm_fm(P, E, [(W["We1"][ei * 64:(ei + 1) * 64, :],
                              lambda kg, a, n: CONTB[0:64, 0:n], cont_tok)],
                      h_cb, M=1024, col_tiles=ctl)
                E.bank_free(biw, tk[0])
                if L2_STAGE < 4:
                    continue

                def eo_cb(mg, m0, nm, cc0, ncx, pa, tok):
                    return chain("dve", lambda e, o=ACC[0:64, 0:Tc], a=ACC[0:64, 0:Tc],
                                 b=pa: e.tensor_tensor(o, a, b, ALU.add), extra=[tok])
                mm_fm(P, E, [(W["We2"][ei * 1024:(ei + 1) * 1024, :],
                              lambda kg, a, n: hv(kg), tk[0])], eo_cb, M=64,
                      col_tiles=ctl,
                      extra_rows=(lambda m0, nm: W["BE2"][0:1, ei * 64 + m0:ei * 64 + m0 + nm],
                                  lambda a, n: GROW[0:1, 0:n], tk[0]))
            if L2_STAGE < 5:
                continue
            chain("dve", lambda e, o=ACCB[0:64, 0:Tc], i=ACC[0:64, 0:Tc]:
                  e.tensor_copy(o, i))

            def c2s_cb(mg, m0, nm, cc0, ncx, pa, tok):
                return chain("act", lambda e, o=spv(mg), i=pa,
                             b=W["c2s_b"][:, mg:mg + 1]:
                             e.activation(o, i, AF.Sigmoid, bias=b), extra=[tok])
            mm_fm(P, E, [(W["c2s_W"], lambda kg, a, n: ACCB[0:64, 0:n], tk[0])],
                  c2s_cb, M=2048, col_tiles=ctl)

            def dec_cb(mg, m0, nm, cc0, ncx, pa, tok):
                return chain("act", lambda e, o=dcv(mg), i=pa,
                             b=W["dec_b"][:, mg:mg + 1]:
                             e.activation(o, i, AF.Sigmoid, bias=b), extra=[tok])
            mm_fm(P, E, [(W["dec_W"], lambda kg, a, n: spv(kg), tk[0])],
                  dec_cb, M=D, col_tiles=ctl)
            if L2_STAGE < 6:
                continue
            lt = emit_ln(P, E, lambda g: lnv(g), lambda g: lnv(g), ncols=Tc,
                         affine=(W["ln_g"], W["ln_b"]),
                         src=lambda g: dcv(g), src_tok=tk[0], sq_scr=GROW,
                         apply_src=lambda g: dcv(g))
            tk[0] = lt
            for g in range(G):
                chain("dve", lambda e, o=E.xres(g, c0, Tc), i=lnv(g),
                      b=E.xres(g, c0, Tc):
                      e.scalar_tensor_tensor(o, i, 0.5, b, ALU.mult, ALU.add))
    P.barrier()


def emit_l2_attn(P, E, W):
    nc = E.nc
    E.scr_reset()
    ln_tok = emit_ln(P, E, lambda g: E.xn(g), lambda g: E.xn(g),
                     src=lambda g: E.xres(g))
    tk = [None]

    def chain(eng, fn, extra=()):
        tk[0] = P.op(eng, fn, waits=[tk[0]] + list(extra))
        return tk[0]
    UTM, KTM, UF, KF, YSP = (W["UTM"], W["KTM"], W["UF"], W["KF"], W["YSP"])

    WINB = E.scrb(8192)
    BROW = E.scrb(1024)
    ESTRIP = E.scrb(512)
    t1 = P.dma("pool", WINB.rearrange("p (kg h) -> p kg h", kg=8),
               W["in_W"].rearrange("(kg p) h -> p kg h", p=128))
    t2 = P.dma("pool", BROW[0:1, :], W["in_b_row"])
    for tc in range(16):
        for hh in range(2):
            bi, ps, ft = E.bank()
            w0 = ([t1, t2, ln_tok] if (tc == 0 and hh == 0) else [])
            if ft is not None:
                w0.append(ft)
            for kg in range(8):
                P.op("pe", lambda e, o=ps[:], l=E.xn(kg, tc * 128, 128),
                     r=WINB[:, kg * 1024 + hh * 512: kg * 1024 + hh * 512 + 512],
                     st=(kg == 0): e.matmul(o, l, r, start=st, stop=False),
                     waits=w0 if kg == 0 else ())
            mtok = P.op("pe", lambda e, o=ps[:], l=E.ONE_R[0:1, 0:128],
                        r=BROW[0:1, hh * 512:hh * 512 + 512]:
                        e.matmul(o, l, r, start=False, stop=True), sig=True)
            ctok = chain("dve", lambda e, o=ESTRIP[:], i=ps[:]: e.tensor_copy(o, i),
                         extra=[mtok])
            dtok = P.dma("sp", UTM[tc * 128:(tc + 1) * 128, hh * 512:hh * 512 + 512],
                         ESTRIP[:], waits=[ctok])
            E.bank_free(bi, ctok)
            tk[0] = dtok
    utm_done = tk[0]

    E.scr_reset()
    SBF = E.SCRB.bitcast(F32)
    sf = lambda i: E.SCRF[0:32, i * 1024:(i + 1) * 1024]
    sb_ = lambda i: SBF[0:32, i * 1024:(i + 1) * 1024]
    DT_B, AR, AI, T1, T2, SQT = sf(0), sf(1), sf(2), sf(3), sf(4), sf(5)
    LR, LI, CR, CI, P64R, P64I = sb_(0), sb_(1), sb_(2), sb_(3), sb_(4), sb_(5)
    lt1 = P.dma("sp", LR[0:1, 0:1024], W["log_dt_row"])
    chain("act", lambda e, o=LR[0:1, :], i=LR[0:1, :]: e.activation(o, i, AF.Exp),
          extra=[lt1])
    chain("dve", lambda e, o=P64R[0:1, 0:32], i=P64R[0:1, 0:32]: e.memset(o, 1.0))
    bi, ps, ft = E.bank()
    for hh in range(2):
        chain("pe", lambda e, o=ps[0:32, 0:512], l=P64R[0:1, 0:32],
              r=LR[0:1, hh * 512:hh * 512 + 512]:
              e.matmul(o, l, r, start=True, stop=True),
              extra=[ft] if (hh == 0 and ft) else [])
        chain("dve", lambda e, o=DT_B[:, hh * 512:hh * 512 + 512], i=ps[0:32, 0:512]:
              e.tensor_copy(o, i))
    E.bank_free(bi, tk[0])
    la1 = P.dma("sp", AR, W["logA_T"])
    la2 = P.dma("sp", AI, W["Aim_T"])
    chain("act", lambda e, o=AR, i=AR: e.activation(o, i, AF.Exp), extra=[la1, la2])
    chain("dve", lambda e, o=AR, a=AR, b=DT_B: e.tensor_tensor(o, a, b, ALU.mult))
    chain("dve", lambda e, o=AR, i=AR: e.tensor_scalar_mul(o, i, -1.0))
    chain("dve", lambda e, o=AI, a=AI, b=DT_B: e.tensor_tensor(o, a, b, ALU.mult))
    chain("act", lambda e, o=T2, i=AR: e.activation(o, i, AF.Exp))

    def sincos(dst, phase):
        chain("dve", lambda e, o=T1, i=AI: e.tensor_scalar(
            o, i, 1.0 / (2 * np.pi), MAGIC, ALU.mult, ALU.add))
        chain("dve", lambda e, o=T1, i=T1:
              e.tensor_scalar(o, i, MAGIC, None, ALU.subtract))
        chain("dve", lambda e, o=dst, i=T1, b=AI:
              e.scalar_tensor_tensor(o, i, -6.28125, b, ALU.mult, ALU.add))
        chain("dve", lambda e, o=dst, i=T1, b=dst:
              e.scalar_tensor_tensor(o, i, -0.0019302368, b, ALU.mult, ALU.add))
        chain("dve", lambda e, o=dst, i=T1, b=dst:
              e.scalar_tensor_tensor(o, i, -6.287769e-06, b, ALU.mult, ALU.add))
        if phase != 0.0:
            chain("dve", lambda e, o=dst, i=dst:
                  e.tensor_scalar(o, i, phase, None, ALU.add))
            chain("dve", lambda e, o=SQT, i=dst:
                  e.tensor_scalar(o, i, np.pi, None, ALU.is_gt))
            chain("dve", lambda e, o=dst, i=SQT, b=dst:
                  e.scalar_tensor_tensor(o, i, -2 * np.pi, b, ALU.mult, ALU.add))
        chain("act", lambda e, o=dst, i=dst: e.activation(o, i, AF.Sin))
    sincos(LI, 0.0)
    sincos(LR, np.pi / 2)
    chain("dve", lambda e, o=LI, a=LI, b=T2: e.tensor_tensor(o, a, b, ALU.mult))
    chain("dve", lambda e, o=LR, a=LR, b=T2: e.tensor_tensor(o, a, b, ALU.mult))
    chain("dve", lambda e, o=T1, i=LR: e.tensor_scalar(o, i, -1.0, None, ALU.add))
    chain("dve", lambda e, o=T2, a=AR, b=AR: e.tensor_tensor(o, a, b, ALU.mult))
    chain("dve", lambda e, o=SQT, a=AI, b=AI: e.tensor_tensor(o, a, b, ALU.mult))
    chain("dve", lambda e, o=T2, a=T2, b=SQT: e.tensor_tensor(o, a, b, ALU.add))
    chain("dve", lambda e, o=T2, i=T2: e.reciprocal(o, i))
    chain("dve", lambda e, o=CR, a=T1, b=AR: e.tensor_tensor(o, a, b, ALU.mult))
    chain("dve", lambda e, o=SQT, a=LI, b=AI: e.tensor_tensor(o, a, b, ALU.mult))
    chain("dve", lambda e, o=CR, a=CR, b=SQT: e.tensor_tensor(o, a, b, ALU.add))
    chain("dve", lambda e, o=CR, a=CR, b=T2: e.tensor_tensor(o, a, b, ALU.mult))
    chain("dve", lambda e, o=CI, a=LI, b=AR: e.tensor_tensor(o, a, b, ALU.mult))
    chain("dve", lambda e, o=SQT, a=T1, b=AI: e.tensor_tensor(o, a, b, ALU.mult))
    chain("dve", lambda e, o=CI, a=CI, b=SQT: e.tensor_tensor(o, a, b, ALU.subtract))
    chain("dve", lambda e, o=CI, a=CI, b=T2: e.tensor_tensor(o, a, b, ALU.mult))
    c1 = P.dma("sp", T1, W["C_reT"], waits=[tk[0]])
    c2 = P.dma("sp", T2, W["C_imT"])
    chain("dve", lambda e, o=AR, a=T1, b=CR: e.tensor_tensor(o, a, b, ALU.mult),
          extra=[c1, c2])
    chain("dve", lambda e, o=SQT, a=T2, b=CI: e.tensor_tensor(o, a, b, ALU.mult))
    chain("dve", lambda e, o=AR, a=AR, b=SQT: e.tensor_tensor(o, a, b, ALU.subtract))
    chain("dve", lambda e, o=AI, a=T1, b=CI: e.tensor_tensor(o, a, b, ALU.mult))
    chain("dve", lambda e, o=SQT, a=T2, b=CR: e.tensor_tensor(o, a, b, ALU.mult))
    chain("dve", lambda e, o=AI, a=AI, b=SQT: e.tensor_tensor(o, a, b, ALU.add))
    chain("dve", lambda e, o=CR, a=AR, b=DT_B: e.tensor_tensor(o, a, b, ALU.mult))
    chain("dve", lambda e, o=CR, i=CR: e.tensor_scalar_mul(o, i, 2.0))
    chain("dve", lambda e, o=CI, a=AI, b=DT_B: e.tensor_tensor(o, a, b, ALU.mult))
    chain("dve", lambda e, o=CI, i=CI: e.tensor_scalar_mul(o, i, 2.0))
    chain("dve", lambda e, o=P64R, i=LR: e.tensor_copy(o, i))
    chain("dve", lambda e, o=P64I, i=LI: e.tensor_copy(o, i))
    for _ in range(6):
        chain("dve", lambda e, o=SQT, a=P64R, b=P64I: e.tensor_tensor(o, a, b, ALU.mult))
        chain("dve", lambda e, o=P64R, a=P64R, b=P64R: e.tensor_tensor(o, a, b, ALU.mult))
        chain("dve", lambda e, o=T1, a=P64I, b=P64I: e.tensor_tensor(o, a, b, ALU.mult))
        chain("dve", lambda e, o=P64R, a=P64R, b=T1: e.tensor_tensor(o, a, b, ALU.subtract))
        chain("dve", lambda e, o=P64I, i=SQT: e.tensor_scalar_mul(o, i, 2.0))
    if W.get("DBG2") is not None:
        chain("dve", lambda e: e.engine_nop())
        for cri in range(4):
            P.dma("sp", W["DBG2"][cri * 128:cri * 128 + 96, :],
                  E.SCRF[0:96, cri * 1024:(cri + 1) * 1024],
                  waits=[("s_dve", P.sems["s_dve"][1])])
    LAM2 = E.SCRB[:, 12288:20480]
    PPK = E.SCRB[:, 20480:24576]
    KEV = E.SCRF.bitcast(BF16)[:, 10240:10752]

    def build_powers(dst_packed, L, hoff, seed_re, seed_im, fac_re, fac_im):
        DV = dst_packed.rearrange("p (h l) -> p h l", l=L)
        for hs in range(4):
            ha = hoff + hs * 32
            FR = E.SCRF[0:32, 0:32 * L].rearrange("p (h l) -> p h l", l=L)
            FI = E.SCRF[0:32, 2048:2048 + 32 * L].rearrange("p (h l) -> p h l", l=L)
            if seed_re is None:
                chain("dve", lambda e, o=FR[:, :, 0:1], i=FR[:, :, 0:1]: e.memset(o, 1.0))
                chain("dve", lambda e, o=FI[:, :, 0:1], i=FI[:, :, 0:1]: e.memset(o, 0.0))
            else:
                chain("dve", lambda e, o=FR[:, :, 0:1],
                      i=seed_re[:, ha:ha + 32].unsqueeze(2): e.tensor_copy(o, i))
                chain("dve", lambda e, o=FI[:, :, 0:1],
                      i=seed_im[:, ha:ha + 32].unsqueeze(2): e.tensor_copy(o, i))
            pre, pim = fac_re[:, ha:ha + 32], fac_im[:, ha:ha + 32]
            m = 1
            while m < L:
                for s0 in range(0, m, 16):
                    w = min(16, m - s0)
                    br = pre.unsqueeze(2).broadcast_to([32, 32, w])
                    bi_ = pim.unsqueeze(2).broadcast_to([32, 32, w])
                    sq = E.SCRF[0:32, 4224:4224 + 32 * w].rearrange("p (h l) -> p h l", l=w)
                    chain("dve", lambda e, o=FR[:, :, m + s0:m + s0 + w],
                          a=FR[:, :, s0:s0 + w], b=br: e.tensor_tensor(o, a, b, ALU.mult))
                    chain("dve", lambda e, o=sq, a=FI[:, :, s0:s0 + w], b=bi_:
                          e.tensor_tensor(o, a, b, ALU.mult))
                    chain("dve", lambda e, o=FR[:, :, m + s0:m + s0 + w],
                          a=FR[:, :, m + s0:m + s0 + w], b=sq:
                          e.tensor_tensor(o, a, b, ALU.subtract))
                    chain("dve", lambda e, o=FI[:, :, m + s0:m + s0 + w],
                          a=FR[:, :, s0:s0 + w], b=bi_: e.tensor_tensor(o, a, b, ALU.mult))
                    chain("dve", lambda e, o=sq, a=FI[:, :, s0:s0 + w], b=br:
                          e.tensor_tensor(o, a, b, ALU.mult))
                    chain("dve", lambda e, o=FI[:, :, m + s0:m + s0 + w],
                          a=FI[:, :, m + s0:m + s0 + w], b=sq:
                          e.tensor_tensor(o, a, b, ALU.add))
                if 2 * m < L:
                    PT = E.SCRF[0:32, 4096:4224]
                    pa_, pb_, pc_, pd_ = (PT[:, 0:32], PT[:, 32:64],
                                          PT[:, 64:96], PT[:, 96:128])
                    chain("dve", lambda e, o=pc_, a=pre, b=pim:
                          e.tensor_tensor(o, a, b, ALU.mult))
                    chain("dve", lambda e, o=pa_, a=pre, b=pre:
                          e.tensor_tensor(o, a, b, ALU.mult))
                    chain("dve", lambda e, o=pd_, a=pim, b=pim:
                          e.tensor_tensor(o, a, b, ALU.mult))
                    chain("dve", lambda e, o=pa_, a=pa_, b=pd_:
                          e.tensor_tensor(o, a, b, ALU.subtract))
                    chain("dve", lambda e, o=pb_, i=pc_:
                          e.tensor_scalar_mul(o, i, 2.0))
                    pre, pim = pa_, pb_
                m *= 2
            chain("dve", lambda e, o=DV[0:32, hs * 32:(hs + 1) * 32, :], i=FR:
                  e.tensor_copy(o, i))
            chain("dve", lambda e, o=DV[32:64, hs * 32:(hs + 1) * 32, :], i=FI:
                  e.tensor_copy(o, i))

    for hc in range(8):
        h0 = hc * 128
        build_powers(LAM2, 64, h0, None, None, LR, LI)
        build_powers(PPK, 32, h0, CR, CI, P64R, P64I)
        chain("dve", lambda e, o=PPK[32:64, :], i=PPK[32:64, :]:
              e.tensor_scalar_mul(o, i, -1.0))
        for hb in range(8):
            bi, ps, ft = E.bank()
            mt = None
            for hl in range(16):
                h = hb * 16 + hl
                w0 = [tk[0]] if hl == 0 else []
                if hl == 0 and ft is not None:
                    w0.append(ft)
                mt = P.op("pe", lambda e, o=ps[0:64, hl * 32:(hl + 1) * 32],
                          l=LAM2[0:64, h * 64:(h + 1) * 64], r=PPK[0:64, h * 32:(h + 1) * 32]:
                          e.matmul(o, l, r, start=True, stop=True),
                          waits=w0, sig=(hl == 15))
            ctok = chain("dve", lambda e,
                         o=KEV[0:64, :].rearrange("p (l h) -> p l h", h=16),
                         i=ps[0:64, 0:512].rearrange("p (h l) -> p l h", l=32):
                         e.tensor_copy(o, i), extra=[mt])
            dtok = P.dma("sp", KTM.rearrange("(l1 l2) h -> l2 l1 h", l2=64)[
                :, :, h0 + hb * 16: h0 + hb * 16 + 16],
                KEV[0:64, :].rearrange("p (l h) -> p l h", h=16), waits=[ctok])
            E.bank_free(bi, ctok)
            tk[0] = dtok
    ktm_done = tk[0]

    E.scr_reset()
    TMC = E.scrb(8192)
    FEV = E.scrb(512)
    fMs = [(m, 128) for m in range(0, 4352, 128)]
    for src_sp, dst_sp, wtok in ((UTM, UF, utm_done), (KTM, KF, ktm_done)):
        for cc in range(2):
            ch0 = cc * 512
            ltok = P.dma("pool", TMC.rearrange("p (lg c) -> p lg c", lg=16),
                         src_sp[:, ch0:ch0 + 512].rearrange("(lg p) c -> p lg c", p=128),
                         waits=[wtok, tk[0]])

            def f_cb(mg, m0, nm, cc0, ncx, pa, tok):
                ct = chain("dve", lambda e, o=FEV[0:nm, 0:512], i=pa:
                           e.tensor_copy(o, i), extra=[tok])
                dt_ = P.dma("sp", dst_sp[m0:m0 + nm, ch0:ch0 + 512],
                            FEV[0:nm, 0:512], waits=[ct])
                tk[0] = dt_
                return ct
            mm_fm(P, E, [(W["FWD"], lambda kg, a, n: TMC[:, kg * 512:kg * 512 + 512],
                          ltok)], f_cb, M=4352, Ms=fMs, col_tiles=[(0, 512)])
    fwd_done = tk[0]

    E.scr_reset()
    UFS = E.scrb(4352)
    KFS = E.scrb(4352)
    TPA = E.scrb(128)
    TPB = E.scrb(128)
    UFMC = E.scrb(2048)
    YGC = E.scrb(2048)
    RH2 = [E.scrb(2048), E.scrb(2048)]
    rh_tok = [None, None]
    for hc in range(8):
        h0 = hc * 128
        u1 = P.dma("pool", UFS.rearrange("p (g h) -> p g h", g=34),
                   UF[:, h0:h0 + 128].rearrange("(g p) h -> p g h", p=128),
                   waits=[fwd_done, tk[0]])
        u2_ = P.dma("pool", KFS.rearrange("p (g h) -> p g h", g=34),
                    KF[:, h0:h0 + 128].rearrange("(g p) h -> p g h", p=128))
        for fg in range(17):
            uR = UFS[:, fg * 128:(fg + 1) * 128]
            uI = UFS[:, (17 + fg) * 128:(18 + fg) * 128]
            kR = KFS[:, fg * 128:(fg + 1) * 128]
            kI = KFS[:, (17 + fg) * 128:(18 + fg) * 128]
            w0 = [u1, u2_] if fg == 0 else []
            chain("dve", lambda e, o=TPA[:], a=uR, b=kR: e.tensor_tensor(o, a, b, ALU.mult),
                  extra=w0)
            chain("dve", lambda e, o=TPB[:], a=uI, b=kI: e.tensor_tensor(o, a, b, ALU.mult))
            chain("dve", lambda e, o=TPA[:], a=TPA[:], b=TPB[:]:
                  e.tensor_tensor(o, a, b, ALU.subtract))
            chain("dve", lambda e, o=TPB[:], a=uR, b=kI: e.tensor_tensor(o, a, b, ALU.mult))
            chain("dve", lambda e, o=uI, a=uI, b=kR: e.tensor_tensor(o, a, b, ALU.mult))
            chain("dve", lambda e, o=uI, a=uI, b=TPB[:]: e.tensor_tensor(o, a, b, ALU.add))
            chain("dve", lambda e, o=uR, i=TPA[:]: e.tensor_copy(o, i))

        def ufm_cb(mg, m0, nm, cc0, ncx, pa, tok):
            return chain("act", lambda e, o=UFMC[:, cc0:cc0 + ncx], i=pa,
                         b=W["in_b_col"][:, hc:hc + 1]:
                         e.activation(o, i, AF.Identity, bias=b), extra=[tok])
        mm_fm(P, E, [(W["in_W"][:, h0:h0 + 128], lambda kg, a, n: E.xn(kg, a, n),
                      None)], ufm_cb, M=128, Ms=[(0, 128)])
        held = [E.bank(hold=True) for _ in range(4)]
        x_rd = tk[0]
        for fg in range(34):
            rbuf = RH2[fg % 2]
            rw = [rh_tok[fg % 2]] if rh_tok[fg % 2] else []
            rtok = P.dma("pool", rbuf[:], W["INV"][fg * 128:(fg + 1) * 128, :],
                         waits=rw)
            mt = None
            for lt in range(4):
                bi, ps, ft = held[lt]
                w0 = [rtok]
                if fg == 0:
                    w0.append(x_rd)
                    if ft is not None:
                        w0.append(ft)
                mt = P.op("pe", lambda e, o=ps[:], l=UFS[:, fg * 128:(fg + 1) * 128],
                          r=rbuf[:, lt * 512:(lt + 1) * 512], st=(fg == 0), sp=(fg == 33):
                          e.matmul(o, l, r, start=st, stop=sp, skip_group_check=True),
                          waits=w0, sig=(fg == 33 or lt == 3))
            rh_tok[fg % 2] = mt
        for lt in range(4):
            bi, ps, ft = held[lt]
            chain("dve", lambda e, o=ps[:], i=UFMC[:, lt * 512:(lt + 1) * 512],
                  s=W["D_col"][:, hc:hc + 1], b=ps[:]:
                  e.scalar_tensor_tensor(o, i, s, b, ALU.mult, ALU.add),
                  extra=[mt] if lt == 0 else [])
            ctok = chain("act", lambda e, o=YGC[:, lt * 512:(lt + 1) * 512], i=ps[:]:
                         e.activation(o, i, AF.Gelu))
            E.bank_free(bi, ctok)
        dtok = P.dma("sp", YSP[h0:h0 + 128, :], YGC[:], waits=[tk[0]])
        tk[0] = dtok
    inv_done = tk[0]
    inv_done = tk[0]

    E.scr_reset()
    YGB = E.scrb(G * 512)
    ASB = E.scrb(512)
    SSB = E.scrf(512)
    Y2B = E.scrb(G * 512)
    ygv = lambda g: YGB[:, g * 512:(g + 1) * 512]
    y2v = lambda g: Y2B[:, g * 512:(g + 1) * 512]
    for c0 in range(0, T, 512):
        ltoks = []
        for g in range(G):
            ltoks.append(P.dma("pool", ygv(g), YSP[g * 128:(g + 1) * 128, c0:c0 + 512],
                               waits=[inv_done, tk[0]] if g == 0 else ()))
        x_tok = ltoks[-1]

        def glu_cb(mg, m0, nm, cc0, ncx, pa, tok):
            if mg < 8:
                return chain("dve", lambda e, o=y2v(mg), i=pa, s=W["glu_b"][:, mg:mg + 1]:
                             e.tensor_scalar(o, i, s, None, ALU.add), extra=[tok])
            g2 = mg - 8
            chain("act", lambda e, o=SSB[:, 0:512], i=pa, b=W["glu_b"][:, mg:mg + 1]:
                  e.activation(o, i, AF.Sigmoid, bias=b), extra=[tok])
            return chain("dve", lambda e, o=y2v(g2), a=y2v(g2), b=SSB[:, 0:512]:
                         e.tensor_tensor(o, a, b, ALU.mult))
        mm_fm(P, E, [(W["glu_WT"], lambda kg, a, n: ygv(kg), x_tok)], glu_cb,
              M=2048, col_tiles=[(c0, 512)])

        def out_cb(mg, m0, nm, cc0, ncx, pa, tok):
            chain("act", lambda e, o=pa, i=pa, b=W["out_b"][:, mg:mg + 1]:
                  e.activation(o, i, AF.Identity, bias=b), extra=[tok])
            chain("dve", lambda e, o=pa, a=pa, b=E.xn(mg, cc0, ncx):
                  e.tensor_tensor(o, a, b, ALU.add))
            return chain("dve", lambda e, o=E.xres(mg, cc0, ncx),
                         a=E.xres(mg, cc0, ncx), b=pa:
                         e.tensor_tensor(o, a, b, ALU.add))
        mm_fm(P, E, [(W["out_WT"], lambda kg, a, n: y2v(kg), tk[0])], out_cb,
              M=D, col_tiles=[(c0, 512)])
    P.barrier()



def t_(a):
    return np.ascontiguousarray(np.asarray(a).T)


def colv(a):
    return np.ascontiguousarray(np.asarray(a).reshape(-1, 128).T)


def host_prepare(inputs):
    H = {}
    H["x_fm"] = np.ascontiguousarray(np.asarray(inputs["L0_attn_x"]).transpose(0, 2, 1))
    H["L0f_A_R"] = np.ascontiguousarray(np.asarray(inputs["L0_ffn_G1_R"])[0].transpose(1, 0, 2).reshape(4, 1024))
    H["L0f_A_I"] = np.ascontiguousarray(np.asarray(inputs["L0_ffn_G1_I"])[0].transpose(1, 0, 2).reshape(4, 1024))
    H["L0f_B_R"] = np.ascontiguousarray(np.asarray(inputs["L0_ffn_G2_R"])[:, :, 0, :].transpose(0, 2, 1).reshape(4, 256))
    H["L0f_B_I"] = np.ascontiguousarray(np.asarray(inputs["L0_ffn_G2_I"])[:, :, 0, :].transpose(0, 2, 1).reshape(4, 256))
    H["L0f_bias_real"] = colv(inputs["L0_ffn_bias_real"])
    H["L0f_bias_imag"] = colv(inputs["L0_ffn_bias_imag"])
    H["L3a_W_omT"] = t_(inputs["L3_attn_W_omega"])
    H["L3a_W_zT"] = t_(inputs["L3_attn_W_zeta"])
    H["L3a_W_BT"] = t_(inputs["L3_attn_W_B"])
    H["L3a_C_T"] = t_(inputs["L3_attn_C"])
    H["L3a_DmatT"] = t_(inputs["L3_attn_D_mat"])
    H["L3a_b_omega"] = np.ascontiguousarray(np.asarray(inputs["L3_attn_b_omega"]).reshape(-1, 1))
    H["L3a_b_zeta"] = np.ascontiguousarray(np.asarray(inputs["L3_attn_b_zeta"]).reshape(-1, 1))
    H["L3a_b_B"] = np.ascontiguousarray(np.asarray(inputs["L3_attn_b_B"]).reshape(-1, 1))
    H["L0a_proj_wT"] = t_(inputs["L0_attn_proj_w"])
    H["L0a_gp_wT"] = t_(inputs["L0_attn_gp_w"])
    H["L0a_ip_wT"] = t_(inputs["L0_attn_ip_w"])
    H["L0a_out_wT"] = t_(inputs["L0_attn_out_w"])
    cw = np.asarray(inputs["L0_attn_conv_w"]).reshape(1536, 4)
    H["L0a_convw"] = np.ascontiguousarray(cw.reshape(12, 128, 4).transpose(1, 0, 2).reshape(128, 48))
    H["L0a_lam"] = colv(inputs["L0_attn_forget_lambda"])
    H["L0a_gp_b"] = colv(inputs["L0_attn_gp_b"])
    H["L0a_ip_b"] = colv(inputs["L0_attn_ip_b"])
    H["L0a_conv_b"] = colv(inputs["L0_attn_conv_b"])
    H["L1a_W_in"] = np.ascontiguousarray(np.asarray(inputs["L1_attn_W_in"]))
    xp = np.asarray(inputs["L1_attn_W_xp"])
    H["L1a_W_xp_dt"] = np.ascontiguousarray(xp[:, 0:64])
    H["L1a_W_xp_B"] = np.ascontiguousarray(xp[:, 64:80])
    H["L1a_W_xp_C"] = np.ascontiguousarray(xp[:, 80:96])
    H["L1a_W_dt"] = np.ascontiguousarray(np.asarray(inputs["L1_attn_W_dt"]))
    H["L1a_W_out"] = np.ascontiguousarray(np.asarray(inputs["L1_attn_W_out"]))
    cw1 = np.asarray(inputs["L1_attn_conv_w"])
    H["L1a_convw"] = np.ascontiguousarray(cw1.reshape(16, 128, 4).transpose(1, 0, 2).reshape(128, 64))
    H["L1a_conv_b"] = colv(inputs["L1_attn_conv_b"])
    H["L1a_b_dt"] = colv(inputs["L1_attn_b_dt"])
    H["L1a_D_param"] = colv(inputs["L1_attn_D_param"])
    H["L1a_ln_g"] = colv(inputs["L1_attn_ln_g"])
    H["L1a_ln_b"] = colv(inputs["L1_attn_ln_b"])
    H["L1a_ALOGC"] = np.ascontiguousarray(np.asarray(inputs["L1_attn_A_log"]).reshape(16, 128, 16).transpose(1, 2, 0).reshape(128, 256))
    H["L1a_rs"] = np.ascontiguousarray(np.broadcast_to(np.asarray(inputs["L1_attn_res_scale"]).reshape(1, 1), (128, 1)))
    H["L1a_SELB"] = np.kron(np.eye(16, dtype=np.float32), np.ones((1, 128), np.float32))
    H["identf"] = np.eye(128, dtype=np.float32)
    H["IDB8"] = np.eye(8, dtype=np.float32)
    ltm = np.zeros((8, 8), np.float32)
    for e in range(8):
        for e2 in range(e):
            ltm[e, e2] = 1.0
    H["LTM"] = np.ascontiguousarray(np.broadcast_to(ltm.reshape(1, 64), (128, 64)))
    H["L1f_router_wT"] = t_(inputs["L1_ffn_router_w"])
    H["L1f_w1T"] = np.ascontiguousarray(np.asarray(inputs["L1_ffn_w1"]).transpose(0, 2, 1).reshape(8192, 4096))
    H["L1f_w2T"] = np.ascontiguousarray(np.asarray(inputs["L1_ffn_w2"]).transpose(0, 2, 1).reshape(32768, 1024))
    H["L2f_enc_W"] = np.ascontiguousarray(np.asarray(inputs["L2_ffn_enc_W"]))
    H["L2f_enc_b"] = colv(inputs["L2_ffn_enc_b"])
    H["L2f_s2c_W"] = np.ascontiguousarray(np.asarray(inputs["L2_ffn_s2c_W"]))
    H["L2f_s2c_b"] = np.ascontiguousarray(np.asarray(inputs["L2_ffn_s2c_b"]).reshape(-1, 1))
    H["L2f_rW1"] = np.ascontiguousarray(np.asarray(inputs["L2_ffn_rW1"]))
    H["L2f_rb1"] = np.ascontiguousarray(np.asarray(inputs["L2_ffn_rb1"]).reshape(-1, 1))
    H["L2f_rW2"] = np.ascontiguousarray(np.asarray(inputs["L2_ffn_rW2"]))
    H["L2f_rb2"] = np.ascontiguousarray(np.asarray(inputs["L2_ffn_rb2"]).reshape(-1, 1))
    H["L2f_We1"] = np.ascontiguousarray(np.asarray(inputs["L2_ffn_We1"]).reshape(512, 1024))
    H["L2f_be1"] = np.ascontiguousarray(np.asarray(inputs["L2_ffn_be1"]).reshape(8, 8, 128).transpose(2, 0, 1).reshape(128, 64))
    H["L2f_We2"] = np.ascontiguousarray(np.asarray(inputs["L2_ffn_We2"]).reshape(8192, 64))
    H["L2f_BE2"] = np.ascontiguousarray(np.asarray(inputs["L2_ffn_be2"]).reshape(1, 512))
    H["L2f_c2s_W"] = np.ascontiguousarray(np.asarray(inputs["L2_ffn_c2s_W"]))
    H["L2f_c2s_b"] = colv(inputs["L2_ffn_c2s_b"])
    H["L2f_dec_W"] = np.ascontiguousarray(np.asarray(inputs["L2_ffn_dec_W"]))
    H["L2f_dec_b"] = colv(inputs["L2_ffn_dec_b"])
    H["L2f_ln_g"] = colv(inputs["L2_ffn_ln_g"])
    H["L2f_ln_b"] = colv(inputs["L2_ffn_ln_b"])
    import ml_dtypes
    H["L2a_in_W"] = t_(inputs["L2_attn_in_W"])
    H["L2a_in_b_row"] = np.ascontiguousarray(np.asarray(inputs["L2_attn_in_b"]).reshape(1, 1024))
    H["L2a_in_b_col"] = colv(inputs["L2_attn_in_b"])
    H["L2a_logA_T"] = t_(inputs["L2_attn_log_A_real"])
    H["L2a_Aim_T"] = t_(inputs["L2_attn_A_imag"])
    H["L2a_log_dt_row"] = np.ascontiguousarray(np.asarray(inputs["L2_attn_log_dt"]).reshape(1, 1024))
    H["L2a_C_reT"] = t_(np.asarray(inputs["L2_attn_C"])[:, :, 0])
    H["L2a_C_imT"] = t_(np.asarray(inputs["L2_attn_C"])[:, :, 1])
    H["L2a_D_col"] = colv(inputs["L2_attn_D_skip"])
    H["L2a_glu_WT"] = t_(inputs["L2_attn_glu_W"])
    H["L2a_glu_b"] = colv(inputs["L2_attn_glu_b"])
    H["L2a_out_WT"] = t_(inputs["L2_attn_out_W"])
    H["L2a_out_b"] = colv(inputs["L2_attn_out_b"])
    lv = np.arange(2048, dtype=np.float64)[:, None]
    fv = np.arange(2049, dtype=np.float64)[None, :]
    ang = 2.0 * np.pi * lv * fv / 4096.0
    fwd = np.zeros((2048, 4352), np.float32)
    fwd[:, 0:2049] = np.cos(ang)
    fwd[:, 2176:4225] = -np.sin(ang)
    H["L2a_FWD"] = fwd.astype(ml_dtypes.bfloat16)
    wgt = np.full((1, 2049), 2.0); wgt[0, 0] = 1.0; wgt[0, 2048] = 1.0
    inv = np.zeros((4352, 2048), np.float32)
    inv[0:2049, :] = (wgt.T * np.cos(ang.T) / 4096.0)
    inv[2176:4225, :] = (wgt.T * -np.sin(ang.T) / 4096.0)
    H["L2a_INV"] = inv.astype(ml_dtypes.bfloat16)
    H["L3f_W_rT"] = t_(inputs["L3_ffn_W_r"])
    H["L3f_W_kT"] = t_(inputs["L3_ffn_W_k"])
    H["L3f_W_vT"] = t_(inputs["L3_ffn_W_v"])
    H["L3f_mu_r"] = colv(inputs["L3_ffn_mu_r"])
    H["L3f_mu_k"] = colv(inputs["L3_ffn_mu_k"])
    return H



def setup_weights(nc, P, E, specs, loaded, prefix):
    W = {}
    ltoks = []
    for name, shape in specs:
        base = name.split("@")[0]
        key = base if base in GLOBAL_KEYS else prefix + base
        if key in loaded:
            W[base] = loaded[key]
            continue
        dt_ = BF16 if base in ("FWD", "INV") else F32
        dr = nc.dram_tensor(key, list(shape), dt_, kind="ExternalInput").ap()
        if name.endswith("@cons"):
            tile = E.cons(shape[1])
            ltoks.append(P.dma("sp", tile[0:shape[0], :], dr))
            W[base] = tile
        elif name.endswith("@sbufbf") or base == "SELB":
            tile = E.es.enter_context(nc.sbuf_tensor(key + "_t", list(shape), BF16))
            ltoks.append(P.dma("pool", tile[:], dr))
            W[base] = tile
        elif name.endswith("@sbuff32"):
            tile = E.es.enter_context(nc.sbuf_tensor(key + "_t", list(shape), F32))
            ltoks.append(P.dma("sp", tile[:], dr))
            W[base] = tile
        else:
            W[base] = dr
        loaded[key] = W[base]
    return W, ltoks


def build_single(sub, weight_specs=None, emit_fn=None, debug=False):
    nc = bass.Bass("TRN2", target_bir_lowering=False, debug=False)
    es = ExitStack()
    x_in = nc.dram_tensor("x_in", [D, T], F32, kind="ExternalInput").ap()
    x_out = nc.dram_tensor("x_out", [D, T], F32, kind="ExternalOutput").ap()
    P = Prog(nc, es)
    E = Env(nc, es, P)
    ltoks = []
    for g in range(G):
        ltoks.append(P.dma("sp", E.xres(g), x_in[g * 128:(g + 1) * 128, :]))
    W, wl = setup_weights(nc, P, E, weight_specs, {}, "")
    ltoks += wl
    for e in ENGS:
        for tk in ltoks:
            P._wait(e, tk)
    emit_fn(P, E, W)
    stoks = []
    for g in range(G):
        stoks.append(P.dma("sp", x_out[g * 128:(g + 1) * 128, :], E.xres(g)))
    if debug:
        dbg = nc.dram_tensor("dbg_scrf", [128, SCRF_COLS], F32, kind="ExternalOutput").ap()
        stoks.append(P.dma("sp", dbg, E.SCRF[:]))
    for tk in stoks:
        P._wait("sp", tk)
    P.finalize()
    es.close()
    return nc


SUB_SPECS = {
    "l3_ffn": [("W_rT", (D, D)), ("W_kT", (D, D)), ("W_vT", (D, D)),
               ("mu_r@cons", (128, G)), ("mu_k@cons", (128, G))],
    "l0_ffn": [("A_R", (4, 1024)), ("A_I", (4, 1024)), ("B_R", (4, 256)),
               ("B_I", (4, 256)), ("bias_real@cons", (128, 4)),
               ("bias_imag@cons", (128, 4))],
    "l3_attn": [("W_omT", (D, 64)), ("W_zT", (D, 64)), ("W_BT", (D, 128)),
                ("C_T", (128, D)), ("DmatT", (D, D)), ("b_omega@cons", (64, 1)),
                ("b_zeta@cons", (64, 1)), ("b_B@cons", (128, 1))],
    "l0_attn": [("proj_wT", (D, 3072)), ("gp_wT", (1536, 1536)),
                ("ip_wT", (1536, 1536)), ("out_wT", (1536, D)),
                ("convw@cons", (128, 48)), ("lam@cons", (128, 12)),
                ("gp_b@cons", (128, 12)), ("ip_b@cons", (128, 12)),
                ("conv_b@cons", (128, 12))],
    "l1_attn": [("W_in", (D, 4096)), ("W_xp_dt", (2048, 64)), ("W_xp_B", (2048, 16)),
                ("W_xp_C", (2048, 16)), ("W_dt", (64, 2048)), ("W_out", (2048, D)),
                ("convw@cons", (128, 64)), ("conv_b@cons", (128, 16)),
                ("b_dt@cons", (128, 16)), ("D_param@cons", (128, 16)),
                ("ln_g@cons", (128, 8)), ("ln_b@cons", (128, 8)),
                ("ALOGC@cons", (128, 256)), ("rs@cons", (128, 1)), ("SELB", (16, 2048))],
    "l1_ffn": [("router_wT", (D, 8)), ("w1T", (8192, 4096)), ("w2T", (32768, 1024)),
               ("identf@sbuff32", (128, 128)), ("IDB8@sbufbf", (8, 8)),
               ("LTM@cons", (128, 64))],
    "l2_ffn": [("enc_W", (D, 2048)), ("s2c_W", (2048, 64)), ("rW1", (64, 64)),
               ("rW2", (64, 8)), ("We1", (512, 1024)), ("We2", (8192, 64)),
               ("c2s_W", (64, 2048)), ("dec_W", (2048, D)),
               ("identf@sbuff32", (128, 128)), ("IDB8@sbufbf", (8, 8)),
               ("BE2@sbufbf", (1, 512)), ("LTM@cons", (128, 64)),
               ("enc_b@cons", (128, 16)), ("s2c_b@cons", (64, 1)),
               ("rb1@cons", (64, 1)), ("rb2@cons", (8, 1)), ("be1@cons", (128, 64)),
               ("c2s_b@cons", (128, 16)), ("dec_b@cons", (128, 8)),
               ("ln_g@cons", (128, 8)), ("ln_b@cons", (128, 8))],
    "l2_attn": [("in_W", (D, D)), ("in_b_row", (1, D)), ("log_dt_row", (1, D)),
                ("logA_T", (32, D)), ("Aim_T", (32, D)), ("C_reT", (32, D)),
                ("C_imT", (32, D)), ("glu_WT", (D, 2048)), ("out_WT", (D, D)),
                ("FWD", (2048, 4352)), ("INV", (4352, 2048)),
                ("in_b_col@cons", (128, 8)), ("D_col@cons", (128, 8)),
                ("glu_b@cons", (128, 16)), ("out_b@cons", (128, 8))],
}

SUB_PREFIX = {"l3_ffn": "L3f_", "l0_ffn": "L0f_", "l3_attn": "L3a_",
              "l0_attn": "L0a_", "l1_attn": "L1a_", "l1_ffn": "L1f_",
              "l2_ffn": "L2f_", "l2_attn": "L2a_"}
GLOBAL_KEYS = {"identf", "IDB8", "LTM", "SELB"}


def prep_internal(nc, sub, W):
    if sub == "l0_ffn" and "W0R" not in W:
        for nm in ("W0R", "W0I", "W0In"):
            W[nm] = nc.dram_tensor(nm, [512, 512], BF16, kind="Internal").ap()
    if sub == "l2_attn" and "UTM" not in W:
        for nm, shp in (("UTM", [2048, 1024]), ("KTM", [2048, 1024]),
                        ("UF", [4352, 1024]), ("KF", [4352, 1024]),
                        ("YSP", [1024, 2048])):
            W[nm] = nc.dram_tensor(nm, shp, BF16, kind="Internal").ap()


SUB_EMIT = {}


def build_l3_ffn_single():
    return build_single("l3_ffn", SUB_SPECS["l3_ffn"], emit_l3_ffn)


def build_l0_ffn_single():
    def emit(P, E, W):
        W = dict(W)
        prep_internal(E.nc, "l0_ffn", W)
        emit_l0_ffn(P, E, W, build=True)
    return build_single("l0_ffn", SUB_SPECS["l0_ffn"], emit)


def build_l3_attn_single():
    return build_single("l3_attn", weight_specs=SUB_SPECS["l3_attn"],
                        emit_fn=emit_l3_attn)


def build_l0_attn_single():
    return build_single("l0_attn", weight_specs=SUB_SPECS["l0_attn"],
                        emit_fn=emit_l0_attn)


def build_l1_attn_single():
    return build_single("l1_attn", weight_specs=SUB_SPECS["l1_attn"],
                        emit_fn=emit_l1_attn)


def build_l1_ffn_single():
    return build_single("l1_ffn", weight_specs=SUB_SPECS["l1_ffn"],
                        emit_fn=lambda P, E, W: emit_moe(P, E, W, 1))


def build_l2_ffn_single():
    return build_single("l2_ffn", weight_specs=SUB_SPECS["l2_ffn"],
                        emit_fn=lambda P, E, W: emit_moe(P, E, W, 2))


def build_l2_attn_single():
    def emit(P, E, W):
        W = dict(W)
        prep_internal(E.nc, "l2_attn", W)
        emit_l2_attn(P, E, W)
    return build_single("l2_attn", weight_specs=SUB_SPECS["l2_attn"], emit_fn=emit)


CONVERT_ENABLE = True


def build_fused():
    nc = bass.Bass("TRN2", target_bir_lowering=False, debug=False)
    es = ExitStack()
    x_in = nc.dram_tensor("x_in", [D, T], F32, kind="ExternalInput").ap()
    x_out = nc.dram_tensor("x_out", [D, T], F32, kind="ExternalOutput").ap()
    P = Prog(nc, es)
    E = Env(nc, es, P)
    loaded = {}
    subW = {}
    ltoks = []
    for g in range(G):
        ltoks.append(P.dma("sp", E.xres(g), x_in[g * 128:(g + 1) * 128, :]))
    for sub in SUB_SPECS:
        W, wl = setup_weights(nc, P, E, SUB_SPECS[sub], loaded, SUB_PREFIX[sub])
        prep_internal(nc, sub, W)
        subW[sub] = W
        ltoks += wl
    for e in ENGS:
        for tk in ltoks:
            P._wait(e, tk)
    if CONVERT_ENABLE:
        emit_weight_convert(nc, P, E, subW)
        P.barrier()
    first_l0f = [True]
    for sub, pref, keys in _LAUNCH_PLAN:
        W = subW[sub]
        if sub == "l0_ffn":
            emit_l0_ffn(P, E, W, build=first_l0f[0])
            first_l0f[0] = False
        elif sub == "l1_ffn":
            emit_moe(P, E, W, 1)
        elif sub == "l2_ffn":
            emit_moe(P, E, W, 2)
        elif sub == "l0_attn":
            emit_l0_attn(P, E, W)
        elif sub == "l1_attn":
            emit_l1_attn(P, E, W)
        elif sub == "l2_attn":
            emit_l2_attn(P, E, W)
        elif sub == "l3_ffn":
            emit_l3_ffn(P, E, W)
        elif sub == "l3_attn":
            emit_l3_attn(P, E, W)
    stoks = []
    for g in range(G):
        stoks.append(P.dma("sp", x_out[g * 128:(g + 1) * 128, :], E.xres(g)))
    for tk in stoks:
        P._wait("sp", tk)
    P.finalize()
    es.close()
    return nc


_FUSED_NC = None


def kernel(**inputs):
    global _FUSED_NC
    H = host_prepare(inputs)
    if _FUSED_NC is None:
        _FUSED_NC = build_fused()
    nc = _FUSED_NC
    ncores = 8
    wmap = {}
    for sub, specs in SUB_SPECS.items():
        pref = SUB_PREFIX[sub]
        for name, shape in specs:
            base = name.split("@")[0]
            key = base if base in GLOBAL_KEYS else pref + base
            v = H.get(key)
            if v is None:
                v = H.get(base)
            if v is None:
                v = H.get(pref + base)
            assert v is not None, key
            wmap[key] = v
    in_maps = [dict(wmap, x_in=np.ascontiguousarray(H["x_fm"][b]))
               for b in range(ncores)]
    res = run_bass_kernel_spmd(nc, in_maps, core_ids=list(range(ncores)))
    out = np.stack([np.asarray(res.results[b]["x_out"]).T for b in range(ncores)],
                   axis=0)
    return np.ascontiguousarray(out.astype(np.float32))


_SUB_BUILDERS = None


def _get_builders():
    global _SUB_BUILDERS
    if _SUB_BUILDERS is None:
        _SUB_BUILDERS = {
            "l0_ffn": build_l0_ffn_single(),
            "l0_attn": build_l0_attn_single(),
            "l1_ffn": build_l1_ffn_single(),
            "l1_attn": build_l1_attn_single(),
            "l2_ffn": build_l2_ffn_single(),
            "l2_attn": build_l2_attn_single(),
            "l3_ffn": build_l3_ffn_single(),
            "l3_attn": build_l3_attn_single(),
        }
    return _SUB_BUILDERS


CONVERT_KEYS = {
    "l3_ffn": ["W_rT", "W_kT", "W_vT"],
    "l3_attn": ["W_omT", "W_zT", "W_BT", "C_T", "DmatT"],
    "l0_attn": ["proj_wT", "gp_wT", "ip_wT", "out_wT"],
    "l1_attn": ["W_in", "W_xp_dt", "W_xp_B", "W_xp_C", "W_dt", "W_out"],
    "l1_ffn": ["router_wT", "w1T", "w2T"],
    "l2_ffn": ["enc_W", "s2c_W", "rW1", "rW2", "We1", "We2", "c2s_W", "dec_W"],
    "l2_attn": ["in_W", "glu_WT", "out_WT"],
}


def emit_weight_convert(nc, P, E, subW):
    done = {}
    state = [None, None]
    idx = [0]
    for sub, keys in CONVERT_KEYS.items():
        for k in keys:
            src_ap = subW[sub][k]
            name = src_ap.tensor.name if hasattr(src_ap, "tensor") else k
            if name in done:
                subW[sub][k] = done[name]
                continue
            Ks, Ms = src_ap.shape
            img = nc.dram_tensor("bfi_" + name, [Ks, Ms], BF16, kind="Internal").ap()
            for r0 in range(0, Ks, 128):
                rp = min(128, Ks - r0)
                for c0 in range(0, Ms, 2176):
                    cw = min(2176, Ms - c0)
                    i = idx[0]
                    idx[0] ^= 1
                    buf = E.WB[i]
                    w0 = [state[i]] if state[i] else []
                    t1 = P.dma("pool", buf[0:rp, 0:cw],
                               src_ap[r0:r0 + rp, c0:c0 + cw], waits=w0)
                    t2 = P.dma("sp", img[r0:r0 + rp, c0:c0 + cw],
                               buf[0:rp, 0:cw], waits=[t1])
                    state[i] = t2
            done[name] = img
            subW[sub][k] = img
    E.wb_tok[0] = state[0]
    E.wb_tok[1] = state[1]


_LAUNCH_PLAN = [
    ("l0_ffn", "L0f_", ["A_R", "A_I", "B_R", "B_I", "bias_real", "bias_imag"]),
    ("l0_attn", "L0a_", ["proj_wT", "gp_wT", "ip_wT", "out_wT", "convw", "lam",
                         "gp_b", "ip_b", "conv_b"]),
    ("l0_ffn", "L0f_", None),
    ("l1_ffn", "L1f_", ["router_wT", "w1T", "w2T", "identf", "IDB8", "LTM"]),
    ("l1_attn", "L1a_", ["W_in", "W_xp_dt", "W_xp_B", "W_xp_C", "W_dt", "W_out",
                         "convw", "conv_b", "b_dt", "D_param", "ln_g", "ln_b",
                         "ALOGC", "rs", "SELB"]),
    ("l1_ffn", "L1f_", None),
    ("l2_ffn", "L2f_", ["enc_W", "s2c_W", "rW1", "rW2", "We1", "We2", "c2s_W",
                        "dec_W", "BE2", "identf", "IDB8", "LTM", "enc_b", "s2c_b",
                        "rb1", "rb2", "be1", "c2s_b", "dec_b", "ln_g", "ln_b"]),
    ("l2_attn", "L2a_", ["in_W", "in_b_row", "log_dt_row", "logA_T", "Aim_T",
                         "C_reT", "C_imT", "glu_WT", "out_WT", "FWD", "INV",
                         "in_b_col", "D_col", "glu_b", "out_b"]),
    ("l2_ffn", "L2f_", None),
    ("l3_ffn", "L3f_", ["W_rT", "W_kT", "W_vT", "mu_r", "mu_k"]),
    ("l3_attn", "L3a_", ["W_omT", "W_zT", "W_BT", "C_T", "DmatT", "b_omega",
                         "b_zeta", "b_B"]),
    ("l3_ffn", "L3f_", None),
]


def kernel_multilaunch(**inputs):
    H = host_prepare(inputs)
    builders = _get_builders()
    ncores = 8
    xs = [np.ascontiguousarray(H["x_fm"][b]) for b in range(ncores)]
    plan_keys = {}
    for sub, pref, keys in _LAUNCH_PLAN:
        if keys is not None:
            plan_keys[sub] = (pref, keys)
    for sub, pref, keys in _LAUNCH_PLAN:
        pref, keys = plan_keys[sub]
        nc = builders[sub]
        wmap = {}
        for k in keys:
            wmap[k] = H.get(pref + k, H.get(k))
        in_maps = [dict(wmap, x_in=xs[b]) for b in range(ncores)]
        res = run_bass_kernel_spmd(nc, in_maps, core_ids=list(range(ncores)))
        xs = [np.asarray(res.results[b]["x_out"]) for b in range(ncores)]
    out = np.stack([x.T for x in xs], axis=0)
    return np.ascontiguousarray(out.astype(np.float32))


# version 12
# speedup vs baseline: 1.5060x; 1.0289x over previous
import numpy as np
from contextlib import ExitStack

import concourse.bass as bass
import concourse.mybir as mybir
from concourse.bass_utils import run_bass_kernel_spmd

F32 = mybir.dt.float32
BF16 = mybir.dt.bfloat16
AF = mybir.ActivationFunctionType
ALU = mybir.AluOpType

T = 2048
D = 1024
G = D // 128
PAD = 8
TW = T + PAD
TC = 512

MAGIC = 12582912.0

ENGS = ("pe", "dve", "act", "pool", "sp")


class Prog:

    QRING = {"sp": 6, "pool": 8, "act": 4}

    def __init__(self, nc, es):
        self.nc = nc
        self.items = {e: [] for e in ENGS}
        self.pending = {e: [] for e in ENGS}
        self.sems = {}
        for s in ("s_pe", "s_dve", "s_act", "s_pool", "s_sp"):
            self.sems[s] = [es.enter_context(nc.semaphore(s)), 0]
        self.qrings = {}
        self.qcount = {}
        for q, n in self.QRING.items():
            names = []
            for i in range(n):
                s = "q_%s_%d" % (q, i)
                self.sems[s] = [es.enter_context(nc.semaphore(s)), 0]
                names.append(s)
            self.qrings[q] = names
            self.qcount[q] = 0

    SKIP_SAME = True
    STRICT_SAME = True
    EAGER = False
    strict_same = True

    def _wait(self, eng, tok):
        if tok is None:
            return
        if isinstance(tok, tuple) and tok[0] == "lz":
            entry = tok[1]
            if entry["eng"] == eng and (eng == "pe" or not self.strict_same):
                return
            tok = entry.get("tok") or self._materialize(entry)
        sem, val = self.sems[tok[0]][0], tok[1]
        if val <= 0:
            return
        self.items[eng].append({"fn": (lambda e, s=sem, v=val: e.wait_ge(s, v)),
                                "sig": None})

    def _materialize(self, entry):
        eng = entry["eng"]
        if entry.get("tok") is not None:
            return entry["tok"]
        pend = self.pending[eng]
        rec = self.sems["s_" + eng]
        while pend:
            e2 = pend.pop(0)
            rec[1] += 1
            e2["tok"] = ("s_" + eng, rec[1])
            e2["sig"] = (rec[0], 1)
            if e2 is entry:
                break
        return entry["tok"]

    def op(self, eng, fn, waits=(), sig=False):
        for t in waits:
            self._wait(eng, t)
        entry = {"fn": fn, "sig": None, "eng": eng}
        self.items[eng].append(entry)
        self.pending[eng].append(entry)
        tok = ("lz", entry)
        if sig or self.EAGER:
            tok = self._materialize(entry)
        return tok

    def dma(self, q, out, in_, waits=()):
        ring = self.qrings[q]
        k = self.qcount[q]
        self.qcount[q] += 1
        sname = ring[k % len(ring)]
        rec = self.sems[sname]
        if rec[1] > 0:
            self._wait(q, (sname, rec[1]))
        for t in waits:
            self._wait(q, t)
        rec[1] += 16
        tok = (sname, rec[1])
        self.items[q].append({"fn": (lambda e, o=out, i=in_: e.dma_start(out=o, in_=i)),
                              "sig": (rec[0], 16)})
        return tok

    def all_tokens(self):
        return [(name, rec[1]) for name, rec in self.sems.items() if rec[1] > 0]

    def barrier(self):
        for e in ENGS:
            if self.pending[e]:
                self._materialize(self.pending[e][-1])
        toks = self.all_tokens()
        for e in ENGS:
            for t in toks:
                self._wait(e, t)

    def _run(self, eng, e):
        for entry in self.items[eng]:
            if entry["sig"] is not None:
                entry["fn"](e).then_inc(entry["sig"][0], entry["sig"][1])
            else:
                entry["fn"](e)

    def finalize(self):
        nc = self.nc
        with nc.Block() as block:
            @block.tensor
            def _(e):
                self._run("pe", e)

            @block.vector
            def _(e):
                self._run("dve", e)

            @block.scalar
            def _(e):
                self._run("act", e)

            @block.gpsimd
            def _(e):
                self._run("pool", e)

            @block.sync
            def _(e):
                self._run("sp", e)


SCRF_COLS = 6144
SCRB_COLS = 24576


class Env:

    def __init__(self, nc, es, P):
        self.nc = nc
        self.es = es
        self.P = P
        sb = lambda n, shp, dt: es.enter_context(nc.sbuf_tensor(n, shp, dt))
        self.XRES = sb("XRES", [128, G * T], F32)
        self.XNBF = sb("XNBF", [128, G * TW], BF16)
        self.SCRF = sb("SCRF", [128, SCRF_COLS], F32)
        self.SCRB = sb("SCRB", [128, SCRB_COLS], BF16)
        self.WB = [sb("WB%d" % i, [128, 2176], BF16) for i in range(3)]
        self.wb_tok = [None, None, None]
        self.wb_idx = 0
        self.ROWS = sb("ROWS", [1, 5 * 512], F32)
        self.ROWSB = sb("ROWSB", [1, 2 * 512], BF16)
        self.ONE_L = sb("ONE_L", [128, 128], BF16)
        self.ONE_R = sb("ONE_R", [1, 128], BF16)
        self.CONS = sb("CONS", [128, 1024], F32)
        self.cons_off = 0
        self.banks = []
        for i in range(8):
            ap = es.enter_context(nc.psum_tensor("pb%d" % i, [128, 512], F32))
            self.banks.append([ap, None, False])
        self.bank_rr = 0
        self.scrf_off = 0
        self.scrb_off = 0
        P.op("dve", lambda e: e.memset(self.ONE_L[:], 1.0))
        P.op("dve", lambda e: e.memset(self.ONE_R[:], 1.0))
        self.init_tok = P.op("dve", lambda e: e.memset(self.XNBF[:], 0.0), sig=True)

    def cons(self, ncol):
        v = self.CONS[:, self.cons_off:self.cons_off + ncol]
        self.cons_off += ncol
        assert self.cons_off <= 1024
        return v

    def constf(self, val):
        if not hasattr(self, "_constf"):
            self._constf = {}
        if val in self._constf:
            return self._constf[val]
        ap = self.cons(1)
        tok = self.P.op("dve", lambda e, o=ap, v=val: e.memset(o, v), sig=True)
        self._constf[val] = (ap, tok)
        return ap, tok

    def scr_reset(self):
        self.scrf_off = 0
        self.scrb_off = 0

    def scrf(self, cols):
        v = self.SCRF[:, self.scrf_off:self.scrf_off + cols]
        self.scrf_off += cols
        assert self.scrf_off <= SCRF_COLS, "SCRF overflow"
        return v

    def scrb(self, cols):
        v = self.SCRB[:, self.scrb_off:self.scrb_off + cols]
        self.scrb_off += cols
        assert self.scrb_off <= SCRB_COLS, "SCRB overflow"
        return v

    def bank(self, hold=False):
        for _ in range(8):
            i = self.bank_rr
            self.bank_rr = (self.bank_rr + 1) % 8
            if not self.banks[i][2]:
                break
        else:
            raise RuntimeError("all psum banks held")
        ap, tok, _ = self.banks[i]
        self.banks[i][2] = hold
        return i, ap, tok

    def bank_free(self, i, tok):
        self.banks[i][1] = tok
        self.banks[i][2] = False

    def xres(self, g, t0=0, n=T):
        return self.XRES[:, g * T + t0: g * T + t0 + n]

    def xn(self, g, t0=0, n=T):
        return self.XNBF[:, g * TW + PAD + t0: g * TW + PAD + t0 + n]

    def xn_shift(self, g, shift, t0=0, n=T):
        c = g * TW + PAD + t0 - shift
        return self.XNBF[:, c: c + n]


def load_w_tile(P, E, w_dram, m0, nm, waits=()):
    idx = E.wb_idx
    E.wb_idx = (E.wb_idx + 1) % 3
    buf = E.WB[idx]
    K, M = w_dram.shape
    nkg = (K + 127) // 128
    assert nkg * nm <= 2176, (nkg, nm)
    src = w_dram[:, m0:m0 + nm]
    if nkg > 1:
        assert K % 128 == 0, K
        src = src.rearrange("(kg p) m -> p kg m", p=128)
        dst = buf[:, 0:nkg * nm].rearrange("p (kg m) -> p kg m", kg=nkg)
    else:
        src = src.unsqueeze(1)
        dst = buf[0:K, 0:nm].unsqueeze(1)
    w8 = list(waits)
    if E.wb_tok[idx] is not None:
        w8.append(E.wb_tok[idx])
    q = "pool" if w_dram.dtype == F32 else "sp"
    tok = P.dma(q, dst, src, waits=w8)
    return idx, buf, tok


def mm_fm(P, E, segments, out_cb, M, Ms=None, col_tiles=None,
          extra_rows=None):
    if col_tiles is None:
        col_tiles = [(c, min(512, T - c)) for c in range(0, T, 512)]
    if Ms is None:
        Ms = [(m, min(128, M - m)) for m in range(0, M, 128)]
    assert sum(1 for s in segments
               if not (isinstance(s[0], tuple) and s[0][0] == "sbuf")) <= 2, \
        "at most 2 streamed weight segments (2 weight buffers)"
    paired = []
    i = 0
    while i < len(Ms):
        if (i + 1 < len(Ms) and Ms[i][1] == 128 and Ms[i + 1][1] == 128
                and Ms[i + 1][0] == Ms[i][0] + 128
                and all((isinstance(s[0], tuple) and s[0][0] == "sbuf")
                        or (s[0].shape[0] + 127) // 128 * 256 <= 2176
                        for s in segments)):
            paired.append((Ms[i][0], 256, [Ms[i], Ms[i + 1]]))
            i += 2
        else:
            paired.append((Ms[i][0], Ms[i][1], [Ms[i]]))
            i += 1
    for (mb0, mbw, subms) in paired:
        loaded = []
        for seg in segments:
            w_src, xcols, x_tok = seg[0], seg[1], seg[2]
            if isinstance(w_src, tuple) and w_src[0] == "sbuf":
                loaded.append((None, None, w_src[1], w_src[2], xcols, x_tok))
            else:
                K = w_src.shape[0]
                bidx, buf, wtok = load_w_tile(P, E, w_src, mb0, mbw)
                tile_fn = (lambda kg, mm0, mnm, b=buf, n=mbw, KK=K, base=mb0:
                           b[0:min(128, KK - kg * 128),
                             kg * n + (mm0 - base): kg * n + (mm0 - base) + mnm])
                loaded.append((bidx, wtok, tile_fn, K, xcols, x_tok))

        first = True
        last_pe = None
        for (m0, nm) in subms:
          for (c0, nc) in col_tiles:
              bi, ps, ftok = E.bank()
              pa = ps[0:nm, 0:nc]
              tok = None
              total = sum((K + 127) // 128 for (_, _, _, K, _, _) in loaded)
              idx = 0
              for (bidx, wtok, tile_fn, K, xcols, x_tok) in loaded:
                  nkg = (K + 127) // 128
                  for kg in range(nkg):
                      kp = min(128, K - kg * 128)
                      lhs = tile_fn(kg, m0, nm)
                      rhs = xcols(kg, c0, nc)
                      waits = []
                      if first:
                          for (b2, w2, t2, K2, xc2, xt2) in loaded:
                              if w2 is not None:
                                  waits.append(w2)
                              if xt2 is not None:
                                  waits.append(xt2)
                          first = False
                      if idx == 0 and ftok is not None:
                          waits.append(ftok)
                      stop = (idx == total - 1) and extra_rows is None
                      tok = P.op("pe", lambda e, o=pa, l=lhs, r=rhs, st=(idx == 0), sp=stop:
                                 e.matmul(o, l, r, start=st, stop=sp),
                                 waits=waits, sig=stop)
                      idx += 1
              if extra_rows is not None:
                  lrow = extra_rows[0](m0, nm)
                  rrow = extra_rows[1](c0, nc)
                  w = [extra_rows[2]] if (len(extra_rows) > 2 and extra_rows[2]) else []
                  tok = P.op("pe", lambda e, o=pa, l=lrow, r=rrow:
                             e.matmul(o, l, r, start=False, stop=True),
                             waits=w, sig=True)
              last_pe = tok
              free_tok = out_cb(m0 // 128, m0, nm, c0, nc, pa, tok)
              E.bank_free(bi, free_tok)
        for (bidx, wtok, tile_fn, K, xcols, x_tok) in loaded:
            if bidx is not None:
                E.wb_tok[bidx] = last_pe


def bcast_row(P, E, row_bf_ap, ncols, waits=()):
    bi, ps, ftok = E.bank()
    w = list(waits)
    if ftok is not None:
        w.append(ftok)
    tok = P.op("pe", lambda e, o=ps[:, 0:ncols], r=row_bf_ap:
               e.matmul(o, E.ONE_R[:, 0:128], r, start=True, stop=True),
               waits=w, sig=True)
    return bi, ps[:, 0:ncols], tok


def emit_ln(P, E, xb, dst, ncols=T, affine=None, src=None, src_tok=None,
            eps=1e-5, ngroups=G, sq_scr=None, apply_src=None):
    nD = ngroups * 128
    ctoks = [src_tok] if src_tok else []
    if src is not None:
        ctoks = []
        for g in range(ngroups):
            w0 = [src_tok] if (g == 0 and src_tok) else []
            if g == 0 and getattr(E, "init_tok", None):
                w0.append(E.init_tok)
            tk = P.op("dve", lambda e, o=xb(g), i=src(g): e.tensor_copy(o, i),
                      waits=w0, sig=True)
        ctoks = [tk]
    sq = sq_scr if sq_scr is not None else E.scrb(512)
    lasttk = None
    for c0 in range(0, ncols, 512):
        nc = min(512, ncols - c0)
        bi1, ps1, ft1 = E.bank()
        bi2, ps2, ft2 = E.bank()
        mtok2 = None
        for g in range(ngroups):
            aw = list(ctoks if g == 0 else [])
            if g == 0 and ft2 is not None:
                aw.append(ft2)
            if mtok2 is not None:
                aw.append(mtok2)
            atok = P.op("act", lambda e, o=sq[:, 0:nc], i=xb(g)[:, c0:c0 + nc]:
                        e.activation(o, i, AF.Square), waits=aw, sig=True)
            w1 = list(ctoks if g == 0 else [])
            if g == 0 and ft1 is not None:
                w1.append(ft1)
            P.op("pe", lambda e, o=ps1[:, 0:nc], r=xb(g)[:, c0:c0 + nc],
                 st=(g == 0), sp=(g == ngroups - 1):
                 e.matmul(o, E.ONE_L[:], r, start=st, stop=sp),
                 waits=w1, sig=(g == ngroups - 1))
            mtok2 = P.op("pe", lambda e, o=ps2[:, 0:nc], r=sq[:, 0:nc],
                         st=(g == 0), sp=(g == ngroups - 1):
                         e.matmul(o, E.ONE_L[:], r, start=st, stop=sp),
                         waits=[atok], sig=True)
        sum_tok = mtok2
        mu = E.ROWS[:, 0:nc]
        var = E.ROWS[:, 512:512 + nc]
        rstd = E.ROWS[:, 1024:1024 + nc]
        m2 = E.ROWS[:, 1536:1536 + nc]
        tmp = E.ROWS[:, 2048:2048 + nc]
        P.op("dve", lambda e, o=mu, i=ps1[0:1, 0:nc]:
             e.tensor_scalar_mul(o, i, 1.0 / nD), waits=[sum_tok])
        P.op("dve", lambda e, o=var, i=ps2[0:1, 0:nc]:
             e.tensor_scalar_mul(o, i, 1.0 / nD))
        P.op("dve", lambda e, o=tmp, a=mu, b=mu: e.tensor_tensor(o, a, b, ALU.mult))
        P.op("dve", lambda e, o=var, a=var, b=tmp:
             e.tensor_tensor(o, a, b, ALU.subtract), sig=True)
        eps_ap, eps_tok = E.constf(eps)
        P.op("act", lambda e, o=rstd, i=var:
             e.activation(o, i, AF.Ln, bias=eps_ap[0:1, :]),
             waits=[("s_dve", P.sems["s_dve"][1]), eps_tok], sig=True)
        atk = P.op("act", lambda e, o=rstd, i=rstd:
                   e.activation(o, i, AF.Exp, scale=-0.5), sig=True)
        P.op("dve", lambda e, o=m2, a=mu, b=rstd: e.tensor_tensor(o, a, b, ALU.mult),
             waits=[atk])
        P.op("dve", lambda e, o=m2, i=m2: e.tensor_scalar_mul(o, i, -1.0))
        rb = E.ROWSB[:, 0:nc]
        m2b = E.ROWSB[:, 512:512 + nc]
        P.op("dve", lambda e, o=rb, i=rstd: e.tensor_copy(o, i))
        rowtok = P.op("dve", lambda e, o=m2b, i=m2: e.tensor_copy(o, i), sig=True)
        b1, prb, t1 = bcast_row(P, E, rb, nc, waits=[rowtok])
        b2, pm2, t2 = bcast_row(P, E, m2b, nc, waits=[rowtok])
        for g in range(ngroups):
            dv = dst(g)[:, c0:c0 + nc]
            sv = (apply_src or xb)(g)[:, c0:c0 + nc]
            P.op("dve", lambda e, o=dv, a=sv, b=prb: e.tensor_tensor(o, a, b, ALU.mult),
                 waits=[t1, t2] if g == 0 else ())
            sig_here = affine is None
            lasttk = P.op("dve", lambda e, o=dv, a=dv, b=pm2:
                          e.tensor_tensor(o, a, b, ALU.add), sig=sig_here)
            if affine is not None:
                ga, ba = affine
                lasttk = P.op("dve", lambda e, o=dv, i=dv, s1=ga[:, g:g + 1], s2=ba[:, g:g + 1]:
                              e.tensor_scalar(o, i, s1, s2, ALU.mult, ALU.add), sig=True)
        E.bank_free(b1, lasttk)
        E.bank_free(b2, lasttk)
        E.bank_free(bi1, lasttk)
        E.bank_free(bi2, lasttk)
    return lasttk



def emit_l3_ffn(P, E, W):
    E.scr_reset()
    P.strict_same = False
    ln_tok = emit_ln(P, E, lambda g: E.xn(g), lambda g: E.xn(g),
                     src=lambda g: E.xres(g))
    xr = E.scrb(G * TC)
    xk = E.scrb(G * TC)
    rr = E.scrb(G * TC)
    xrv = lambda g: xr[:, g * TC:(g + 1) * TC]
    xkv = lambda g: xk[:, g * TC:(g + 1) * TC]
    rv = lambda g: rr[:, g * TC:(g + 1) * TC]
    mu_r, mu_k = W["mu_r"], W["mu_k"]
    for c0 in range(0, T, TC):
        x_tok = None
        for g in range(G):
            P.op("dve", lambda e, o=xkv(g), a=E.xn(g, c0, TC), b=E.xn_shift(g, 1, c0, TC):
                 e.tensor_tensor(o, a, b, ALU.subtract),
                 waits=[ln_tok] if (g == 0 and c0 == 0) else ())
            P.op("dve", lambda e, o=xrv(g), i=xkv(g), s=mu_r[:, g:g + 1],
                 b=E.xn_shift(g, 1, c0, TC):
                 e.scalar_tensor_tensor(o, i, s, b, ALU.mult, ALU.add))
            x_tok = P.op("dve", lambda e, o=xkv(g), i=xkv(g), s=mu_k[:, g:g + 1],
                         b=E.xn_shift(g, 1, c0, TC):
                         e.scalar_tensor_tensor(o, i, s, b, ALU.mult, ALU.add), sig=True)
        ct = [(0, TC)]
        last = {}

        def out_r(mg, m0, nm, cc0, ncc, pa, tok):
            tk = P.op("act", lambda e, o=rv(mg)[:, cc0:cc0 + ncc], i=pa:
                      e.activation(o, i, AF.Sigmoid), waits=[tok], sig=True)
            last["r"] = tk
            return tk
        mm_fm(P, E, [(W["W_rT"], lambda kg, cc0, ncc: xrv(kg)[:, cc0:cc0 + ncc], x_tok)],
              out_r, M=D, col_tiles=ct)

        def out_k(mg, m0, nm, cc0, ncc, pa, tok):
            w = [tok] + ([last.pop("r")] if "r" in last else [])
            t1 = P.op("dve", lambda e, o=pa, i=pa: e.tensor_scalar_max(o, i, 0.0),
                      waits=w, sig=True)
            tk = P.op("act", lambda e, o=xrv(mg)[:, cc0:cc0 + ncc], i=pa:
                      e.activation(o, i, AF.Square), waits=[t1], sig=True)
            last["k"] = tk
            return tk
        mm_fm(P, E, [(W["W_kT"], lambda kg, cc0, ncc: xkv(kg)[:, cc0:cc0 + ncc], None)],
              out_k, M=D, col_tiles=ct)

        def out_v(mg, m0, nm, cc0, ncc, pa, tok):
            P.op("dve", lambda e, o=pa, a=pa, b=rv(mg)[:, cc0:cc0 + ncc]:
                 e.tensor_tensor(o, a, b, ALU.mult), waits=[tok])
            return P.op("dve", lambda e, o=E.xres(mg, c0 + cc0, ncc), i=pa,
                        b=E.xres(mg, c0 + cc0, ncc):
                        e.scalar_tensor_tensor(o, i, 0.5, b, ALU.mult, ALU.add),
                        sig=True)
        mm_fm(P, E, [(W["W_vT"], lambda kg, cc0, ncc: xrv(kg)[:, cc0:cc0 + ncc], last.pop("k"))],
              out_v, M=D, col_tiles=ct)
    P.strict_same = True
    P.barrier()


def emit_l0ffn_build(P, E, W):
    a_t = E.SCRF[0:4, 0:2048]
    b_t = E.SCRF[0:4, 2048:2560]
    t1 = P.dma("sp", a_t[:, 0:1024], W["A_R"])
    t2 = P.dma("sp", a_t[:, 1024:2048], W["A_I"])
    t3 = P.dma("sp", b_t[:, 0:256], W["B_R"])
    t4 = P.dma("sp", b_t[:, 256:512], W["B_I"])
    wblk = E.SCRB[:, 0:256]
    wneg = E.SCRB[:, 256:512]
    dtoks = [t1, t2, t3, t4]
    last_dma = []
    for mi, mat in enumerate(("R", "I")):
        ao = 1024 * mi
        bo = 256 * mi
        for ch in range(8):
            bi, ps, ftok = E.bank()
            pa = ps[:, 0:256]
            w = list(dtoks) if (mi == 0 and ch == 0) else []
            if ftok is not None:
                w.append(ftok)
            mtok = P.op("pe", lambda e, o=pa, l=a_t[:, ao + ch * 128: ao + (ch + 1) * 128],
                        r=b_t[:, bo:bo + 256]:
                        e.matmul(o, l, r, start=True, stop=True), waits=w, sig=True)
            ctok = P.op("dve", lambda e, o=wblk, i=pa: e.tensor_copy(o, i),
                        waits=[mtok] + last_dma, sig=True)
            E.bank_free(bi, ctok)
            ntok = None
            if mat == "I":
                ntok = P.op("dve", lambda e, o=wneg, i=wblk:
                            e.tensor_single_scalar(o, i, -1.0, ALU.mult), sig=True)
            last_dma = []
            targets = [("W0R" if mat == "R" else "W0I", wblk, ctok)]
            if mat == "I":
                targets.append(("W0In", wneg, ntok))
            for name, srctile, tok in targets:
                dst_full = W[name]
                for b_lo in range(4):
                    b = ch * 4 + b_lo
                    dst = dst_full.rearrange("(d g) (b e) -> d g b e", g=16, e=16)
                    dst = dst[:, :, b, :]
                    src = srctile[b_lo * 32:(b_lo + 1) * 32, :].rearrange(
                        "d (g e) -> d g e", g=16)
                    last_dma.append(P.dma("sp", dst, src, waits=[tok]))
    return last_dma


def emit_l0_ffn(P, E, W, build=False):
    E.scr_reset()
    P.strict_same = False
    btoks = []
    if build:
        btoks = emit_l0ffn_build(P, E, W)
        E.scr_reset()
    ln_tok = emit_ln(P, E, lambda g: E.xn(g), lambda g: E.xn(g),
                     src=lambda g: E.xres(g))
    xcr = lambda kg, c0, nc: E.xn(kg, c0, nc)
    xci = lambda kg, c0, nc: E.xn(4 + kg, c0, nc)
    br = W["bias_real"]
    bi_ = W["bias_imag"]

    def mk_cb(goff, bias_col):
        def cb(mg, m0, nm, c0, nc, pa, tok):
            atok = P.op("act", lambda e, o=pa, i=pa, b=bias_col[:, mg:mg + 1]:
                        e.activation(o, i, AF.Identity, bias=b), waits=[tok], sig=True)
            return P.op("dve", lambda e, o=E.xres(goff + mg, c0, nc), i=pa,
                        b=E.xres(goff + mg, c0, nc):
                        e.scalar_tensor_tensor(o, i, 0.5, b, ALU.mult, ALU.add),
                        waits=[atok], sig=True)
        return cb
    wt = btoks[-1] if btoks else None
    mm_fm(P, E, [(W["W0R"], xcr, ln_tok), (W["W0In"], xci, wt)],
          mk_cb(0, br), M=512)
    mm_fm(P, E, [(W["W0I"], xcr, None), (W["W0R"], xci, None)],
          mk_cb(4, bi_), M=512)
    P.strict_same = True
    P.barrier()


def emit_l3_attn(P, E, W):
    DT = 0.01
    E.scr_reset()
    ln_tok = emit_ln(P, E, lambda g: E.xn(g), lambda g: E.xn(g),
                     src=lambda g: E.xres(g))
    Tc = 256
    n64 = lambda: E.scrf(Tc)
    OM, AS, SS, DS = n64(), n64(), n64(), n64()
    A0, B0, C0, DD0 = n64(), n64(), n64(), n64()
    A1, B1, C1, DD1 = n64(), n64(), n64(), n64()
    FZ0, FY0, FZ1, FY1 = n64(), n64(), n64(), n64()
    TMP = n64()
    ST = E.scrf(2)
    FZB = E.scrb(Tc)
    FYB = E.scrb(Tc)
    h = lambda a: a[0:64, :]
    one_ap, one_tok = E.constf(1.0)
    P.op("dve", lambda e: e.memset(ST[0:64, :], 0.0), sig=True)
    b_om, b_z, b_B = W["b_omega"], W["b_zeta"], W["b_B"]
    tk = [None]

    def chain(eng, fn, extra=()):
        tk[0] = P.op(eng, fn, waits=[tk[0]] + list(extra))
        return tk[0]

    for c0 in range(0, T, Tc):
        ctl = [(c0, Tc)]

        def om_cb(mg, m0, nm, cc0, nc, pa, tok):
            chain("act", lambda e, o=h(TMP), i=pa[0:64, :], b=b_om[0:64, :]:
                  e.activation(o, i, AF.Abs, bias=b), extra=[tok])
            chain("act", lambda e, o=h(TMP), i=h(TMP):
                  e.activation(o, i, AF.Exp, scale=-1.0))
            chain("act", lambda e, o=h(TMP), i=h(TMP), b=one_ap[0:64, :]:
                  e.activation(o, i, AF.Ln, bias=b), extra=[one_tok])
            chain("dve", lambda e, o=h(OM), i=pa[0:64, :], b=b_om[0:64, :]:
                  e.tensor_scalar(o, i, b, 0.0, ALU.add, ALU.max))
            chain("dve", lambda e, o=h(OM), a=h(OM), b2=h(TMP):
                  e.tensor_tensor(o, a, b2, ALU.add))
            return chain("dve", lambda e, o=h(OM), i=h(OM):
                         e.tensor_scalar(o, i, 1e-4, 100.0, ALU.max, ALU.min))
        mm_fm(P, E, [(W["W_omT"], lambda kg, a, n: E.xn(kg, a, n),
                      ln_tok if c0 == 0 else None)], om_cb, M=64, col_tiles=ctl)

        def z_cb(mg, m0, nm, cc0, nc, pa, tok):
            chain("act", lambda e, o=h(TMP), i=pa[0:64, :], b=b_z[0:64, :]:
                  e.activation(o, i, AF.Sigmoid, bias=b), extra=[tok])
            return chain("dve", lambda e, o=h(DS), i=h(TMP):
                         e.tensor_scalar(o, i, -1.0, 1.0, ALU.mult, ALU.add))
        mm_fm(P, E, [(W["W_zT"], lambda kg, a, n: E.xn(kg, a, n), None)],
              z_cb, M=64, col_tiles=ctl)

        chain("act", lambda e, o=h(AS), i=h(OM): e.activation(o, i, AF.Square))
        chain("dve", lambda e, o=h(TMP), i=h(AS):
              e.tensor_scalar(o, i, DT * DT, 1.0, ALU.mult, ALU.add))
        chain("dve", lambda e, o=h(SS), i=h(TMP): e.reciprocal(o, i))
        chain("dve", lambda e, o=h(A0), a=h(DS), b2=h(SS):
              e.tensor_tensor(o, a, b2, ALU.mult))
        chain("dve", lambda e, o=h(DD0), i=h(A0): e.tensor_copy(o, i))
        chain("dve", lambda e, o=h(B0), i=h(AS), b2=h(A0):
              e.scalar_tensor_tensor(o, i, -DT, b2, ALU.mult, ALU.mult))
        chain("dve", lambda e, o=h(C0), i=h(A0): e.tensor_scalar_mul(o, i, DT))

        def f_cb(mg, m0, nm, cc0, nc, pa, tok):
            dst = FZ0 if m0 == 0 else FY0
            return chain("dve", lambda e, o=h(dst), i=pa[0:64, :],
                         b=b_B[m0:m0 + 64, :]:
                         e.tensor_scalar(o, i, b, None, ALU.add), extra=[tok])
        mm_fm(P, E, [(W["W_BT"], lambda kg, a, n: E.xn(kg, a, n), None)],
              f_cb, M=128, Ms=[(0, 64), (64, 64)], col_tiles=ctl)

        cur = (A0, B0, C0, DD0, FZ0, FY0)
        nxt = (A1, B1, C1, DD1, FZ1, FY1)
        for k in range(8):
            s = 1 << k
            (a, b, c, dd, fz, fy) = cur
            (an, bn, cn, ddn, fzn, fyn) = nxt
            n = Tc - s
            sh = lambda p: p[0:64, 0:n]
            up = lambda p: p[0:64, s:Tc]
            for (src, dst) in ((a, an), (b, bn), (c, cn), (dd, ddn),
                               (fz, fzn), (fy, fyn)):
                chain("dve", lambda e, o=dst[0:64, 0:s], i=src[0:64, 0:s]:
                      e.tensor_copy(o, i))
            for (dst, u1, v1, u2, v2) in (
                    (an, a, a, b, c), (bn, a, b, b, dd),
                    (cn, c, a, dd, c), (ddn, c, b, dd, dd)):
                chain("dve", lambda e, o=up(dst), x1=up(u1), x2=sh(v1):
                      e.tensor_tensor(o, x1, x2, ALU.mult))
                chain("dve", lambda e, o=up(TMP), x1=up(u2), x2=sh(v2):
                      e.tensor_tensor(o, x1, x2, ALU.mult))
                chain("dve", lambda e, o=up(dst), x1=up(dst), x2=up(TMP):
                      e.tensor_tensor(o, x1, x2, ALU.add))
            for (dst, u1, u2, f2) in ((fzn, a, b, fz), (fyn, c, dd, fy)):
                chain("dve", lambda e, o=up(dst), x1=up(u1), x2=sh(fz):
                      e.tensor_tensor(o, x1, x2, ALU.mult))
                chain("dve", lambda e, o=up(TMP), x1=up(u2), x2=sh(fy):
                      e.tensor_tensor(o, x1, x2, ALU.mult))
                chain("dve", lambda e, o=up(dst), x1=up(dst), x2=up(TMP):
                      e.tensor_tensor(o, x1, x2, ALU.add))
                chain("dve", lambda e, o=up(dst), x1=up(dst), x2=up(f2):
                      e.tensor_tensor(o, x1, x2, ALU.add))
            cur, nxt = nxt, cur
        (a, b, c, dd, fz, fy) = cur
        chain("dve", lambda e, o=h(fz), i=h(a), s1=ST[0:64, 0:1], b2=h(fz):
              e.scalar_tensor_tensor(o, i, s1, b2, ALU.mult, ALU.add))
        chain("dve", lambda e, o=h(fz), i=h(b), s1=ST[0:64, 1:2], b2=h(fz):
              e.scalar_tensor_tensor(o, i, s1, b2, ALU.mult, ALU.add))
        chain("dve", lambda e, o=h(fy), i=h(c), s1=ST[0:64, 0:1], b2=h(fy):
              e.scalar_tensor_tensor(o, i, s1, b2, ALU.mult, ALU.add))
        chain("dve", lambda e, o=h(fy), i=h(dd), s1=ST[0:64, 1:2], b2=h(fy):
              e.scalar_tensor_tensor(o, i, s1, b2, ALU.mult, ALU.add))
        chain("dve", lambda e, o=ST[0:64, 0:1], i=fz[0:64, Tc - 1:Tc]:
              e.tensor_copy(o, i))
        chain("dve", lambda e, o=ST[0:64, 1:2], i=fy[0:64, Tc - 1:Tc]:
              e.tensor_copy(o, i))
        chain("dve", lambda e, o=FZB[0:64, :], i=h(fz): e.tensor_copy(o, i))
        x_tok = chain("dve", lambda e, o=FYB[0:64, :], i=h(fy): e.tensor_copy(o, i))

        def out_cb(mg, m0, nm, cc0, nc, pa, tok):
            return P.op("dve", lambda e, o=E.xres(mg, cc0, nc), a=E.xres(mg, cc0, nc), b2=pa:
                        e.tensor_tensor(o, a, b2, ALU.add), waits=[tok], sig=True)
        mm_fm(P, E, [(W["C_T"][0:64, :], lambda kg, cc0, n: FZB[0:64, 0:n], x_tok),
                     (W["C_T"][64:128, :], lambda kg, cc0, n: FYB[0:64, 0:n], None)],
              out_cb, M=D, col_tiles=ctl)
        mm_fm(P, E, [(W["DmatT"], lambda kg, cc0, n: E.xn(kg, cc0, n), None)],
              out_cb, M=D, col_tiles=ctl)
    P.barrier()


def emit_l0_attn(P, E, W):
    E.scr_reset()
    ln_tok = emit_ln(P, E, lambda g: E.xn(g), lambda g: E.xn(g),
                     src=lambda g: E.xres(g))
    Tc = 256
    HG = 12
    ALPHA = E.scrf(HG * Tc)
    XBETA = E.scrf(HG * Tc)
    GG = E.scrb(HG * Tc)
    XH = E.scrb(HG * (Tc + 3))
    XC = E.scrb(HG * Tc)
    GHB = E.scrb(HG * Tc)
    al = lambda g: ALPHA[:, g * Tc:(g + 1) * Tc]
    xb = lambda g: XBETA[:, g * Tc:(g + 1) * Tc]
    gg = lambda g: GG[:, g * Tc:(g + 1) * Tc]
    xh = lambda g, j=3, n=Tc: XH[:, g * (Tc + 3) + j: g * (Tc + 3) + j + n]
    xc = lambda g: XC[:, g * Tc:(g + 1) * Tc]
    gh = lambda g: GHB[:, g * Tc:(g + 1) * Tc]
    lam, convw, gpb, ipb = W["lam"], W["convw"], W["gp_b"], W["ip_b"]
    c8 = E.cons(HG)
    HC = E.cons(HG)
    one_ap, one_tok = E.constf(1.0)
    tk = [None]

    def chain(eng, fn, extra=()):
        tk[0] = P.op(eng, fn, waits=[tk[0]] + list(extra))
        return tk[0]

    chain("act", lambda e, o=c8, i=lam: e.activation(o, i, AF.Abs), extra=[one_tok])
    chain("act", lambda e, o=c8, i=c8: e.activation(o, i, AF.Exp, scale=-1.0))
    chain("act", lambda e, o=c8, i=c8, b=one_ap: e.activation(o, i, AF.Ln, bias=b))
    chain("dve", lambda e, o=HC, i=lam: e.tensor_scalar_max(o, i, 0.0))
    chain("dve", lambda e, o=c8, a=c8, b=HC: e.tensor_tensor(o, a, b, ALU.add))
    chain("dve", lambda e, o=c8, i=c8: e.tensor_scalar_mul(o, i, -8.0))
    chain("dve", lambda e, o=HC, i=HC: e.memset(o, 0.0))
    chain("dve", lambda e, o=XH[:], i=XH[:]: e.memset(o, 0.0))

    for c0 in range(0, T, Tc):
        ctl = [(c0, Tc)]
        if c0 > 0:
            for g in range(HG):
                chain("dve", lambda e, o=xh(g, 0, 3), i=xh(g, Tc, 3):
                      e.tensor_copy(o, i))

        def proj_cb(mg, m0, nm, cc0, nc, pa, tok):
            if mg < HG:
                return chain("act", lambda e, o=gg(mg), i=pa:
                             e.activation(o, i, AF.Gelu), extra=[tok])
            return chain("dve", lambda e, o=xh(mg - HG), i=pa:
                         e.tensor_copy(o, i), extra=[tok])
        mm_fm(P, E, [(W["proj_wT"], lambda kg, a, n: E.xn(kg, a, n),
                      ln_tok if c0 == 0 else None)], proj_cb, M=3072,
              col_tiles=ctl)
        for g in range(HG):
            chain("dve", lambda e, o=xc(g), i=xh(g, 0), s=convw[:, g * 4:g * 4 + 1]:
                  e.tensor_single_scalar(o, i, s, ALU.mult))
            for j in range(1, 4):
                chain("dve", lambda e, o=xc(g), i=xh(g, j), s=convw[:, g * 4 + j:g * 4 + j + 1],
                      b=xc(g): e.scalar_tensor_tensor(o, i, s, b, ALU.mult, ALU.add))
            chain("dve", lambda e, o=xc(g), i=xc(g), s=W["conv_b"][:, g:g + 1]:
                  e.tensor_single_scalar(o, i, s, ALU.add))

        def rec_cb(mg, m0, nm, cc0, nc, pa, tok):
            bi2, ps2, ft2 = E.bank()
            chain("act", lambda e, o=ps2[:, 0:nc], i=pa, b=gpb[:, mg:mg + 1]:
                  e.activation(o, i, AF.Sigmoid, bias=b),
                  extra=[tok] + ([ft2] if ft2 else []))
            chain("dve", lambda e, o=al(mg), i=ps2[:, 0:nc], s=c8[:, mg:mg + 1]:
                  e.tensor_single_scalar(o, i, s, ALU.mult))
            E.bank_free(bi2, tk[0])
            tka = chain("act", lambda e, o=al(mg), i=al(mg): e.activation(o, i, AF.Exp))
            return tka
        mm_fm(P, E, [(W["gp_wT"], lambda kg, a, n: xc(kg), tk[0])], rec_cb,
              M=1536, col_tiles=ctl)

        def inp_cb(mg, m0, nm, cc0, nc, pa, tok):
            bi2, ps2, ft2 = E.bank()
            chain("act", lambda e, o=xb(mg), i=al(mg): e.activation(o, i, AF.Square),
                  extra=[tok] + ([ft2] if ft2 else []))
            chain("dve", lambda e, o=xb(mg), i=xb(mg):
                  e.tensor_scalar(o, i, -1.0, 1.0 + 1e-6, ALU.mult, ALU.add))
            chain("act", lambda e, o=xb(mg), i=xb(mg): e.activation(o, i, AF.Ln))
            chain("act", lambda e, o=xb(mg), i=xb(mg):
                  e.activation(o, i, AF.Exp, scale=0.5))
            chain("act", lambda e, o=ps2[:, 0:nc], i=pa, b=ipb[:, mg:mg + 1]:
                  e.activation(o, i, AF.Sigmoid, bias=b))
            chain("dve", lambda e, o=xb(mg), a=xb(mg), b=ps2[:, 0:nc]:
                  e.tensor_tensor(o, a, b, ALU.mult))
            chain("dve", lambda e, o=ps2[:, 0:nc], i=xc(mg): e.tensor_copy(o, i))
            tkd = chain("dve", lambda e, o=xb(mg), a=xb(mg), b=ps2[:, 0:nc]:
                        e.tensor_tensor(o, a, b, ALU.mult))
            E.bank_free(bi2, tkd)
            return tkd
        mm_fm(P, E, [(W["ip_wT"], lambda kg, a, n: xc(kg), None)], inp_cb,
              M=1536, col_tiles=ctl)

        for g in range(HG):
            bi2, ps2, ft2 = E.bank()
            chain("dve", lambda e, o=ps2[:, 0:Tc], a=al(g), d1=xb(g), init=HC[:, g:g + 1]:
                  e.tensor_tensor_scan(o, a, d1, init, ALU.mult, ALU.add),
                  extra=[ft2] if ft2 else [])
            chain("dve", lambda e, o=HC[:, g:g + 1], i=ps2[:, Tc - 1:Tc]:
                  e.tensor_copy(o, i))
            tkd = chain("dve", lambda e, o=gh(g), a=gg(g), b=ps2[:, 0:Tc]:
                        e.tensor_tensor(o, a, b, ALU.mult))
            E.bank_free(bi2, tkd)

        def out_cb(mg, m0, nm, cc0, nc, pa, tok):
            return P.op("dve", lambda e, o=E.xres(mg, cc0, nc), a=E.xres(mg, cc0, nc),
                        b=pa: e.tensor_tensor(o, a, b, ALU.add), waits=[tok], sig=True)
        mm_fm(P, E, [(W["out_wT"], lambda kg, a, n: gh(kg), tk[0])], out_cb,
              M=D, col_tiles=ctl)
    P.barrier()


def emit_l1_attn(P, E, W):
    E.scr_reset()
    ln_tok = emit_ln(P, E, lambda g: E.xn(g), lambda g: E.xn(g),
                     src=lambda g: E.xres(g))
    for g in range(G):
        ln_tok = P.op("dve", lambda e, o=E.xres(g), a=E.xres(g), b=E.xn(g):
                      e.tensor_tensor(o, a, b, ALU.add),
                      waits=[ln_tok] if g == 0 else (), sig=(g == G - 1))
    ln2 = emit_ln(P, E, lambda g: E.xn(g), lambda g: E.xn(g),
                  affine=(W["ln_g"], W["ln_b"]), src_tok=ln_tok)
    Tc = 256
    EG = 16
    A_ = E.scrf(256)
    HCM = E.scrf(256)
    Y = E.scrf(EG * Tc)
    DA4 = E.scrf(4 * Tc)
    DAs = DA4[:, 0:Tc]
    DTBs = E.scrf(Tc)
    Hs = E.scrf(Tc)
    XU = E.scrb(EG * (Tc + 3))
    Z = E.scrb(EG * Tc)
    U2 = E.scrb(EG * Tc)
    DT_ = E.scrb(EG * Tc)
    DTU = E.scrb(EG * Tc)
    XDT = E.scrb(Tc)
    XB_ = E.scrb(Tc)
    XC_ = E.scrb(Tc)
    yv = lambda g: Y[:, g * Tc:(g + 1) * Tc]
    xu = lambda g, j=3, n=Tc: XU[:, g * (Tc + 3) + j: g * (Tc + 3) + j + n]
    zv = lambda g: Z[:, g * Tc:(g + 1) * Tc]
    u2v = lambda g: U2[:, g * Tc:(g + 1) * Tc]
    dtv = lambda g: DT_[:, g * Tc:(g + 1) * Tc]
    dtuv = lambda g: DTU[:, g * Tc:(g + 1) * Tc]
    one_ap, one_tok = E.constf(1.0)
    convw, conv_b, b_dt, Dp, rs = (W["convw"], W["conv_b"], W["b_dt"],
                                   W["D_param"], W["rs"])
    tk = [None]

    def chain(eng, fn, extra=()):
        tk[0] = P.op(eng, fn, waits=[tk[0]] + list(extra))
        return tk[0]

    chain("act", lambda e, o=A_, i=W["ALOGC"]: e.activation(o, i, AF.Exp),
          extra=[one_tok])
    chain("dve", lambda e, o=A_, i=A_: e.tensor_scalar_mul(o, i, -1.0))
    chain("dve", lambda e, o=HCM, i=HCM: e.memset(o, 0.0))
    chain("dve", lambda e, o=XU[:], i=XU[:]: e.memset(o, 0.0))

    for c0 in range(0, T, Tc):
        ctl = [(c0, Tc)]
        if c0 > 0:
            for g in range(EG):
                chain("dve", lambda e, o=xu(g, 0, 3), i=xu(g, Tc, 3):
                      e.tensor_copy(o, i))

        def in_cb(mg, m0, nm, cc0, nc, pa, tok):
            if mg < EG:
                return chain("dve", lambda e, o=xu(mg), i=pa: e.tensor_copy(o, i),
                             extra=[tok])
            chain("act", lambda e, o=zv(mg - EG), i=pa:
                  e.activation(o, i, AF.Sigmoid), extra=[tok])
            return chain("dve", lambda e, o=zv(mg - EG), a=zv(mg - EG), b=pa:
                         e.tensor_tensor(o, a, b, ALU.mult))
        mm_fm(P, E, [(W["W_in"], lambda kg, a, n: E.xn(kg, a, n),
                      ln2 if c0 == 0 else None)], in_cb, M=4096, col_tiles=ctl)
        for g in range(EG):
            bi2, ps2, ft2 = E.bank()
            chain("dve", lambda e, o=ps2[:, 0:Tc], i=xu(g, 0), s=convw[:, g * 4:g * 4 + 1]:
                  e.tensor_single_scalar(o, i, s, ALU.mult),
                  extra=[ft2] if ft2 else [])
            for j in range(1, 4):
                chain("dve", lambda e, o=ps2[:, 0:Tc], i=xu(g, j),
                      s=convw[:, g * 4 + j:g * 4 + j + 1], b=ps2[:, 0:Tc]:
                      e.scalar_tensor_tensor(o, i, s, b, ALU.mult, ALU.add))
            chain("dve", lambda e, o=DAs, i=ps2[:, 0:Tc], s=conv_b[:, g:g + 1]:
                  e.tensor_single_scalar(o, i, s, ALU.add))
            chain("act", lambda e, o=u2v(g), i=DAs: e.activation(o, i, AF.Sigmoid))
            tka = chain("dve", lambda e, o=u2v(g), a=u2v(g), b=DAs:
                        e.tensor_tensor(o, a, b, ALU.mult))
            E.bank_free(bi2, tka)

        def dbl_cb(dst):
            def cb(mg, m0, nm, cc0, nc, pa, tok):
                return chain("dve", lambda e, o=dst[0:nm, 0:nc], i=pa:
                             e.tensor_copy(o, i), extra=[tok])
            return cb
        mm_fm(P, E, [(W["W_xp_dt"], lambda kg, a, n: u2v(kg), tk[0])],
              dbl_cb(XDT), M=64, col_tiles=ctl)
        mm_fm(P, E, [(W["W_xp_B"], lambda kg, a, n: u2v(kg), None)],
              dbl_cb(XB_), M=16, col_tiles=ctl)
        mm_fm(P, E, [(W["W_xp_C"], lambda kg, a, n: u2v(kg), None)],
              dbl_cb(XC_), M=16, col_tiles=ctl)

        def dt_cb(mg, m0, nm, cc0, nc, pa, tok):
            chain("act", lambda e, o=DAs, i=pa, b=b_dt[:, mg:mg + 1]:
                  e.activation(o, i, AF.Abs, bias=b), extra=[tok])
            chain("act", lambda e, o=DAs, i=DAs: e.activation(o, i, AF.Exp, scale=-1.0))
            chain("act", lambda e, o=DAs, i=DAs, b=one_ap:
                  e.activation(o, i, AF.Ln, bias=b))
            chain("dve", lambda e, o=dtv(mg), i=pa, b=b_dt[:, mg:mg + 1]:
                  e.tensor_scalar(o, i, b, 0.0, ALU.add, ALU.max))
            tkd = chain("dve", lambda e, o=dtv(mg), a=dtv(mg), b=DAs:
                        e.tensor_tensor(o, a, b, ALU.add))
            return tkd
        mm_fm(P, E, [(W["W_dt"], lambda kg, a, n: XDT[0:64, 0:n], tk[0])],
              dt_cb, M=2048, col_tiles=ctl)
        for g in range(EG):
            chain("dve", lambda e, o=dtuv(g), a=dtv(g), b=u2v(g):
                  e.tensor_tensor(o, a, b, ALU.mult))
        chain("dve", lambda e, o=Y[:], i=Y[:]: e.memset(o, 0.0))
        for n in range(16):
            biB, psB, ftB = E.bank()
            biC, psC, ftC = E.bank()
            chain("pe", lambda e, o=psB[:, 0:Tc], l=W["SELB"][:, n * 128:(n + 1) * 128],
                  r=XB_[0:16, 0:Tc]: e.matmul(o, l, r, start=True, stop=True),
                  extra=[ftB] if ftB else [])
            chain("pe", lambda e, o=psC[:, 0:Tc], l=W["SELB"][:, n * 128:(n + 1) * 128],
                  r=XC_[0:16, 0:Tc]: e.matmul(o, l, r, start=True, stop=True),
                  extra=[ftC] if ftC else [])
            for g4 in range(0, EG, 4):
                for j in range(4):
                    g = g4 + j
                    col = n * 16 + g
                    chain("dve", lambda e, o=DA4[:, j * Tc:(j + 1) * Tc], i=dtv(g),
                          s=A_[:, col:col + 1]:
                          e.tensor_single_scalar(o, i, s, ALU.mult))
                chain("act", lambda e, o=DA4, i=DA4: e.activation(o, i, AF.Exp))
                for j in range(4):
                    g = g4 + j
                    col = n * 16 + g
                    das = DA4[:, j * Tc:(j + 1) * Tc]
                    chain("dve", lambda e, o=DTBs, a=dtuv(g), b=psB[:, 0:Tc]:
                          e.tensor_tensor(o, a, b, ALU.mult))
                    chain("dve", lambda e, o=Hs, a=das, d1=DTBs, init=HCM[:, col:col + 1]:
                          e.tensor_tensor_scan(o, a, d1, init, ALU.mult, ALU.add))
                    chain("dve", lambda e, o=HCM[:, col:col + 1], i=Hs[:, Tc - 1:Tc]:
                          e.tensor_copy(o, i))
                    chain("dve", lambda e, o=Hs, a=Hs, b=psC[:, 0:Tc]:
                          e.tensor_tensor(o, a, b, ALU.mult))
                    chain("dve", lambda e, o=yv(g), a=yv(g), b=Hs:
                          e.tensor_tensor(o, a, b, ALU.add))
            E.bank_free(biB, tk[0])
            E.bank_free(biC, tk[0])
        for g in range(EG):
            chain("dve", lambda e, o=yv(g), i=u2v(g), s=Dp[:, g:g + 1], b=yv(g):
                  e.scalar_tensor_tensor(o, i, s, b, ALU.mult, ALU.add))
            chain("dve", lambda e, o=yv(g), a=yv(g), b=zv(g):
                  e.tensor_tensor(o, a, b, ALU.mult))
            chain("dve", lambda e, o=dtuv(g), i=yv(g): e.tensor_copy(o, i))

        def out_cb(mg, m0, nm, cc0, nc, pa, tok):
            return P.op("dve", lambda e, o=E.xres(mg, cc0, nc), i=pa, s=rs[:, 0:1],
                        b=E.xres(mg, cc0, nc):
                        e.scalar_tensor_tensor(o, i, s, b, ALU.mult, ALU.add),
                        waits=[tok], sig=True)
        mm_fm(P, E, [(W["W_out"], lambda kg, a, n: dtuv(kg), tk[0])], out_cb,
              M=D, col_tiles=ctl)
    P.barrier()


def emit_top2(P, E, W, chain, LG, GT, nc_cols, renorm):
    idf, ltm, idb8 = W["identf"], W["LTM"], W["IDB8"]
    rt = E.rt_area
    RT = rt[:, 0:8]
    PW = rt[:, 8:72]
    PW2 = rt[:, 72:136]
    RANK = rt[:, 136:144]
    ME = rt[:, 144:152]
    M1 = rt[:, 152:153]
    DEN = rt[:, 153:154]
    R4 = rt[:, 154:158]
    R2 = rt[:, 158:160]
    pw3 = lambda a: a.rearrange("p (e f) -> p e f", e=8)
    last = [None]
    for tt in range(nc_cols // 128):
        bi, ps, ft = E.bank()
        chain("pe", lambda e, o=ps[:, 0:8], i=LG[0:8, tt * 128:(tt + 1) * 128],
              d=idf[0:8, 0:8]: e.transpose(o, i, d), extra=[ft] if ft else [])
        t_rt = chain("dve", lambda e, o=RT, i=ps[:, 0:8]: e.tensor_copy(o, i))
        E.bank_free(bi, t_rt)
        b1 = RT.unsqueeze(2).broadcast_to([128, 8, 8])
        b2 = RT.unsqueeze(1).broadcast_to([128, 8, 8])
        chain("dve", lambda e, o=pw3(PW), a=b1, b=b2:
              e.tensor_tensor(o, a, b, ALU.is_lt))
        chain("dve", lambda e, o=pw3(PW2), a=b1, b=b2:
              e.tensor_tensor(o, a, b, ALU.is_equal))
        chain("dve", lambda e, o=PW2, a=PW2, b=ltm:
              e.tensor_tensor(o, a, b, ALU.mult))
        chain("dve", lambda e, o=PW, a=PW, b=PW2:
              e.tensor_tensor(o, a, b, ALU.add))
        p3 = pw3(PW)
        chain("dve", lambda e, o=p3[:, :, 0:4], a=p3[:, :, 0:4], b=p3[:, :, 4:8]:
              e.tensor_tensor(o, a, b, ALU.add))
        chain("dve", lambda e, o=p3[:, :, 0:2], a=p3[:, :, 0:2], b=p3[:, :, 2:4]:
              e.tensor_tensor(o, a, b, ALU.add))
        chain("dve", lambda e, o=RANK.unsqueeze(2), a=p3[:, :, 0:1], b=p3[:, :, 1:2]:
              e.tensor_tensor(o, a, b, ALU.add))
        chain("dve", lambda e, o=R4, a=RT[:, 0:4], b=RT[:, 4:8]:
              e.tensor_tensor(o, a, b, ALU.max))
        chain("dve", lambda e, o=R2, a=R4[:, 0:2], b=R4[:, 2:4]:
              e.tensor_tensor(o, a, b, ALU.max))
        chain("dve", lambda e, o=M1, a=R2[:, 0:1], b=R2[:, 1:2]:
              e.tensor_tensor(o, a, b, ALU.max))
        chain("dve", lambda e, o=M1, i=M1: e.tensor_scalar_mul(o, i, -1.0))
        chain("act", lambda e, o=RT, i=RT, b=M1: e.activation(o, i, AF.Exp, bias=b))
        chain("dve", lambda e, o=RANK, i=RANK:
              e.tensor_single_scalar(o, i, 1.5, ALU.is_lt))
        chain("dve", lambda e, o=ME, a=RT, b=RANK:
              e.tensor_tensor(o, a, b, ALU.mult))
        S = ME if renorm else RT
        chain("dve", lambda e, o=R4, a=S[:, 0:4], b=S[:, 4:8]:
              e.tensor_tensor(o, a, b, ALU.add))
        chain("dve", lambda e, o=R2, a=R4[:, 0:2], b=R4[:, 2:4]:
              e.tensor_tensor(o, a, b, ALU.add))
        chain("dve", lambda e, o=DEN, a=R2[:, 0:1], b=R2[:, 1:2]:
              e.tensor_tensor(o, a, b, ALU.add))
        chain("dve", lambda e, o=DEN, i=DEN: e.reciprocal(o, i))
        chain("dve", lambda e, o=ME, i=ME, s=DEN:
              e.tensor_single_scalar(o, i, s, ALU.mult))
        bi2, ps2, ft2 = E.bank()
        chain("pe", lambda e, o=ps2[0:8, 0:128], i=ME, d=idf[:]:
              e.transpose(o, i, d), extra=[ft2] if ft2 else [])
        tkd = chain("dve", lambda e, o=GT[0:8, tt * 128:(tt + 1) * 128],
                    i=ps2[0:8, 0:128]: e.tensor_copy(o, i))
        E.bank_free(bi2, tkd)
        last[0] = tkd
    return last[0]


def tk_cur(P):
    for nm in ("s_dve",):
        pass
    return ("s_dve", P.sems["s_dve"][1])


L2_STAGE = 99


def emit_moe(P, E, W, layer):
    E.scr_reset()
    ln_tok = emit_ln(P, E, lambda g: E.xn(g), lambda g: E.xn(g),
                     src=lambda g: E.xres(g))
    Tc = 512
    E.rt_area = E.scrf(176)
    LG = E.scrf(Tc)
    GT = E.scrb(Tc)
    GROW = E.scrb(Tc)
    tk = [None]

    def chain(eng, fn, extra=()):
        tk[0] = P.op(eng, fn, waits=[tk[0]] + list(extra))
        return tk[0]
    if layer == 1:
        H_ = E.scrb(32 * Tc)
        ACC = E.scrf(G * Tc)
        hv = lambda g: H_[:, g * Tc:(g + 1) * Tc]
        av = lambda g: ACC[:, g * Tc:(g + 1) * Tc]
        for c0 in range(0, T, Tc):
            ctl = [(c0, Tc)]

            def rout_cb(mg, m0, nm, cc0, ncx, pa, tok):
                return chain("dve", lambda e, o=LG[0:8, 0:Tc], i=pa:
                             e.tensor_copy(o, i), extra=[tok])
            mm_fm(P, E, [(W["router_wT"], lambda kg, a, n: E.xn(kg, a, n),
                          ln_tok if c0 == 0 else None)], rout_cb, M=8,
                  col_tiles=ctl)
            gt_tok = emit_top2(P, E, W, chain, LG, GT, Tc, renorm=False)
            for g in range(G):
                chain("dve", lambda e, o=av(g), i=av(g): e.memset(o, 0.0))
            for ei in range(8):
                bi, ps, ft = E.bank()
                stok = P.op("pe", lambda e, o=ps[0:1, 0:Tc], l=W["IDB8"][:, ei:ei + 1],
                            r=GT[0:8, 0:Tc]: e.matmul(o, l, r, start=True, stop=True),
                            waits=[gt_tok] + ([ft] if ft else []))
                gtok = P.op("dve", lambda e, o=GROW[0:1, 0:Tc], i=ps[0:1, 0:Tc]:
                            e.tensor_copy(o, i), waits=[stok])
                biw, psw, ftw = E.bank(hold=True)
                wtok = P.op("pe", lambda e, o=psw[:, 0:Tc], r=GROW[0:1, 0:Tc]:
                            e.matmul(o, E.ONE_R[0:1, 0:128], r, start=True, stop=True),
                            waits=[gtok] + ([ftw] if ftw else []))
                E.bank_free(bi, wtok)

                def h_cb(mg, m0, nm, cc0, ncx, pa, tok):
                    chain("act", lambda e, o=hv(mg), i=pa:
                          e.activation(o, i, AF.Gelu), extra=[tok])
                    return chain("dve", lambda e, o=hv(mg), a=hv(mg), b=psw[:, 0:Tc]:
                                 e.tensor_tensor(o, a, b, ALU.mult), extra=[wtok])
                mm_fm(P, E, [(W["w1T"][ei * 1024:(ei + 1) * 1024, :],
                              lambda kg, a, n: E.xn(kg, a, n),
                              ln_tok if (c0 == 0 and ei == 0) else wtok)],
                      h_cb, M=4096, col_tiles=ctl)
                E.bank_free(biw, tk[0])

                def y_cb(mg, m0, nm, cc0, ncx, pa, tok):
                    return chain("dve", lambda e, o=av(mg), a=av(mg), b=pa:
                                 e.tensor_tensor(o, a, b, ALU.add), extra=[tok])
                mm_fm(P, E, [(W["w2T"][ei * 4096:ei * 4096 + 2048, :],
                              lambda kg, a, n: hv(kg), tk[0]),
                             (W["w2T"][ei * 4096 + 2048:(ei + 1) * 4096, :],
                              lambda kg, a, n: hv(16 + kg), None)],
                      y_cb, M=D, col_tiles=ctl)
            for g in range(G):
                chain("dve", lambda e, o=E.xres(g, c0, Tc), i=av(g),
                      b=E.xres(g, c0, Tc):
                      e.scalar_tensor_tensor(o, i, 0.5, b, ALU.mult, ALU.add))
    else:
        W["rb1x2"] = E.cons(1)
        P.op("dve", lambda e, o=W["rb1x2"][0:64, :], i=W["rb1"][0:64, :]:
             e.tensor_scalar_mul(o, i, 2.0), sig=True)
        SP = E.scrb(16 * Tc)
        CONTB = E.scrb(Tc)
        T1B = E.scrb(Tc)
        H_ = E.scrb(G * Tc)
        ACCB = E.scrb(Tc)
        LNO = E.scrb(G * Tc)
        ACC = E.scrf(Tc)
        DEC = E.scrf(G * Tc)
        spv = lambda g: SP[:, g * Tc:(g + 1) * Tc]
        hv = lambda g: H_[:, g * Tc:(g + 1) * Tc]
        dcv = lambda g: DEC[:, g * Tc:(g + 1) * Tc]
        lnv = lambda g: LNO[:, g * Tc:(g + 1) * Tc]
        for c0 in range(0, T, Tc):
            ctl = [(c0, Tc)]

            def enc_cb(mg, m0, nm, cc0, ncx, pa, tok):
                return chain("act", lambda e, o=spv(mg), i=pa,
                             b=W["enc_b"][:, mg:mg + 1]:
                             e.activation(o, i, AF.Sigmoid, bias=b), extra=[tok])
            mm_fm(P, E, [(W["enc_W"], lambda kg, a, n: E.xn(kg, a, n),
                          ln_tok if c0 == 0 else None)], enc_cb, M=2048,
                  col_tiles=ctl)

            if L2_STAGE < 1:
                continue

            def s2c_cb(mg, m0, nm, cc0, ncx, pa, tok):
                chain("dve", lambda e, o=pa, i=pa, s=W["s2c_b"][0:64, 0:1]:
                      e.tensor_single_scalar(o, i, s, ALU.add), extra=[tok])
                return chain("dve", lambda e, o=CONTB[0:64, 0:Tc], i=pa:
                             e.tensor_copy(o, i))
            mm_fm(P, E, [(W["s2c_W"], lambda kg, a, n: spv(kg), tk[0])],
                  s2c_cb, M=64, col_tiles=ctl)

            def r1_cb(mg, m0, nm, cc0, ncx, pa, tok):
                chain("act", lambda e, o=T1B[0:64, 0:Tc], i=pa,
                      b=W["rb1x2"][0:64, 0:1]:
                      e.activation(o, i, AF.Sigmoid, bias=b, scale=2.0), extra=[tok])
                return chain("dve", lambda e, o=T1B[0:64, 0:Tc], i=T1B[0:64, 0:Tc]:
                             e.tensor_scalar(o, i, 2.0, -1.0, ALU.mult, ALU.add))
            mm_fm(P, E, [(W["rW1"], lambda kg, a, n: CONTB[0:64, 0:n], tk[0])],
                  r1_cb, M=64, col_tiles=ctl)

            def r2_cb(mg, m0, nm, cc0, ncx, pa, tok):
                chain("dve", lambda e, o=pa, i=pa, s=W["rb2"][0:8, 0:1]:
                      e.tensor_single_scalar(o, i, s, ALU.add), extra=[tok])
                return chain("dve", lambda e, o=LG[0:8, 0:Tc], i=pa:
                             e.tensor_copy(o, i))
            mm_fm(P, E, [(W["rW2"], lambda kg, a, n: T1B[0:64, 0:n], tk[0])],
                  r2_cb, M=8, col_tiles=ctl)
            if L2_STAGE < 2:
                continue
            gt_tok = emit_top2(P, E, W, chain, LG, GT, Tc, renorm=True)
            chain("dve", lambda e, o=ACC[0:64, 0:Tc], i=ACC[0:64, 0:Tc]:
                  e.memset(o, 0.0))
            cont_tok = tk[0]
            for ei in range(8 if L2_STAGE >= 3 else 0):
                bi, ps, ft = E.bank()
                stok = P.op("pe", lambda e, o=ps[0:1, 0:Tc], l=W["IDB8"][:, ei:ei + 1],
                            r=GT[0:8, 0:Tc]: e.matmul(o, l, r, start=True, stop=True),
                            waits=[gt_tok] + ([ft] if ft else []))
                gtok = P.op("dve", lambda e, o=GROW[0:1, 0:Tc], i=ps[0:1, 0:Tc]:
                            e.tensor_copy(o, i), waits=[stok])
                biw, psw, ftw = E.bank(hold=True)
                wtok = P.op("pe", lambda e, o=psw[:, 0:Tc], r=GROW[0:1, 0:Tc]:
                            e.matmul(o, E.ONE_R[0:1, 0:128], r, start=True, stop=True),
                            waits=[gtok] + ([ftw] if ftw else []))
                E.bank_free(bi, wtok)

                def h_cb(mg, m0, nm, cc0, ncx, pa, tok):
                    chain("act", lambda e, o=hv(mg), i=pa,
                          b=W["be1"][:, ei * 8 + mg:ei * 8 + mg + 1]:
                          e.activation(o, i, AF.Relu, bias=b), extra=[tok])
                    return chain("dve", lambda e, o=hv(mg), a=hv(mg), b=psw[:, 0:Tc]:
                                 e.tensor_tensor(o, a, b, ALU.mult), extra=[wtok])
                mm_fm(P, E, [(W["We1"][ei * 64:(ei + 1) * 64, :],
                              lambda kg, a, n: CONTB[0:64, 0:n], cont_tok)],
                      h_cb, M=1024, col_tiles=ctl)
                E.bank_free(biw, tk[0])
                if L2_STAGE < 4:
                    continue

                def eo_cb(mg, m0, nm, cc0, ncx, pa, tok):
                    return chain("dve", lambda e, o=ACC[0:64, 0:Tc], a=ACC[0:64, 0:Tc],
                                 b=pa: e.tensor_tensor(o, a, b, ALU.add), extra=[tok])
                mm_fm(P, E, [(W["We2"][ei * 1024:(ei + 1) * 1024, :],
                              lambda kg, a, n: hv(kg), tk[0])], eo_cb, M=64,
                      col_tiles=ctl,
                      extra_rows=(lambda m0, nm: W["BE2"][0:1, ei * 64 + m0:ei * 64 + m0 + nm],
                                  lambda a, n: GROW[0:1, 0:n], tk[0]))
            if L2_STAGE < 5:
                continue
            chain("dve", lambda e, o=ACCB[0:64, 0:Tc], i=ACC[0:64, 0:Tc]:
                  e.tensor_copy(o, i))

            def c2s_cb(mg, m0, nm, cc0, ncx, pa, tok):
                return chain("act", lambda e, o=spv(mg), i=pa,
                             b=W["c2s_b"][:, mg:mg + 1]:
                             e.activation(o, i, AF.Sigmoid, bias=b), extra=[tok])
            mm_fm(P, E, [(W["c2s_W"], lambda kg, a, n: ACCB[0:64, 0:n], tk[0])],
                  c2s_cb, M=2048, col_tiles=ctl)

            def dec_cb(mg, m0, nm, cc0, ncx, pa, tok):
                return chain("act", lambda e, o=dcv(mg), i=pa,
                             b=W["dec_b"][:, mg:mg + 1]:
                             e.activation(o, i, AF.Sigmoid, bias=b), extra=[tok])
            mm_fm(P, E, [(W["dec_W"], lambda kg, a, n: spv(kg), tk[0])],
                  dec_cb, M=D, col_tiles=ctl)
            if L2_STAGE < 6:
                continue
            lt = emit_ln(P, E, lambda g: lnv(g), lambda g: lnv(g), ncols=Tc,
                         affine=(W["ln_g"], W["ln_b"]),
                         src=lambda g: dcv(g), src_tok=tk[0], sq_scr=GROW,
                         apply_src=lambda g: dcv(g))
            tk[0] = lt
            for g in range(G):
                chain("dve", lambda e, o=E.xres(g, c0, Tc), i=lnv(g),
                      b=E.xres(g, c0, Tc):
                      e.scalar_tensor_tensor(o, i, 0.5, b, ALU.mult, ALU.add))
    P.barrier()


def emit_l2_attn(P, E, W):
    nc = E.nc
    E.scr_reset()
    P.strict_same = False
    ln_tok = emit_ln(P, E, lambda g: E.xn(g), lambda g: E.xn(g),
                     src=lambda g: E.xres(g))
    tk = [None]

    def chain(eng, fn, extra=()):
        tk[0] = P.op(eng, fn, waits=[tk[0]] + list(extra))
        return tk[0]
    UTM, KTM, UF, KF, YSP = (W["UTM"], W["KTM"], W["UF"], W["KF"], W["YSP"])

    WINB = E.scrb(8192)
    BROW = E.scrb(1024)
    ESTRIP = E.scrb(512)
    t1 = P.dma("pool", WINB.rearrange("p (kg h) -> p kg h", kg=8),
               W["in_W"].rearrange("(kg p) h -> p kg h", p=128))
    t2 = P.dma("pool", BROW[0:1, :], W["in_b_row"])
    for tc in range(16):
        for hh in range(2):
            bi, ps, ft = E.bank()
            w0 = ([t1, t2, ln_tok] if (tc == 0 and hh == 0) else [])
            if ft is not None:
                w0.append(ft)
            for kg in range(8):
                P.op("pe", lambda e, o=ps[:], l=E.xn(kg, tc * 128, 128),
                     r=WINB[:, kg * 1024 + hh * 512: kg * 1024 + hh * 512 + 512],
                     st=(kg == 0): e.matmul(o, l, r, start=st, stop=False),
                     waits=w0 if kg == 0 else ())
            mtok = P.op("pe", lambda e, o=ps[:], l=E.ONE_R[0:1, 0:128],
                        r=BROW[0:1, hh * 512:hh * 512 + 512]:
                        e.matmul(o, l, r, start=False, stop=True), sig=True)
            ctok = chain("dve", lambda e, o=ESTRIP[:], i=ps[:]: e.tensor_copy(o, i),
                         extra=[mtok])
            dtok = P.dma("sp", UTM[tc * 128:(tc + 1) * 128, hh * 512:hh * 512 + 512],
                         ESTRIP[:], waits=[ctok])
            E.bank_free(bi, ctok)
            tk[0] = dtok
    utm_done = tk[0]

    E.scr_reset()
    SBF = E.SCRB.bitcast(F32)
    sf = lambda i: E.SCRF[0:32, i * 1024:(i + 1) * 1024]
    sb_ = lambda i: SBF[0:32, i * 1024:(i + 1) * 1024]
    DT_B, AR, AI, T1, T2, SQT = sf(0), sf(1), sf(2), sf(3), sf(4), sf(5)
    LR, LI, CR, CI, P64R, P64I = sb_(0), sb_(1), sb_(2), sb_(3), sb_(4), sb_(5)
    lt1 = P.dma("sp", LR[0:1, 0:1024], W["log_dt_row"])
    chain("act", lambda e, o=LR[0:1, :], i=LR[0:1, :]: e.activation(o, i, AF.Exp),
          extra=[lt1])
    chain("dve", lambda e, o=P64R[0:1, 0:32], i=P64R[0:1, 0:32]: e.memset(o, 1.0))
    bi, ps, ft = E.bank()
    for hh in range(2):
        chain("pe", lambda e, o=ps[0:32, 0:512], l=P64R[0:1, 0:32],
              r=LR[0:1, hh * 512:hh * 512 + 512]:
              e.matmul(o, l, r, start=True, stop=True),
              extra=[ft] if (hh == 0 and ft) else [])
        chain("dve", lambda e, o=DT_B[:, hh * 512:hh * 512 + 512], i=ps[0:32, 0:512]:
              e.tensor_copy(o, i))
    E.bank_free(bi, tk[0])
    la1 = P.dma("sp", AR, W["logA_T"])
    la2 = P.dma("sp", AI, W["Aim_T"])
    chain("act", lambda e, o=AR, i=AR: e.activation(o, i, AF.Exp), extra=[la1, la2])
    chain("dve", lambda e, o=AR, a=AR, b=DT_B: e.tensor_tensor(o, a, b, ALU.mult))
    chain("dve", lambda e, o=AR, i=AR: e.tensor_scalar_mul(o, i, -1.0))
    chain("dve", lambda e, o=AI, a=AI, b=DT_B: e.tensor_tensor(o, a, b, ALU.mult))
    chain("act", lambda e, o=T2, i=AR: e.activation(o, i, AF.Exp))

    def sincos(dst, phase):
        chain("dve", lambda e, o=T1, i=AI: e.tensor_scalar(
            o, i, 1.0 / (2 * np.pi), MAGIC, ALU.mult, ALU.add))
        chain("dve", lambda e, o=T1, i=T1:
              e.tensor_scalar(o, i, MAGIC, None, ALU.subtract))
        chain("dve", lambda e, o=dst, i=T1, b=AI:
              e.scalar_tensor_tensor(o, i, -6.28125, b, ALU.mult, ALU.add))
        chain("dve", lambda e, o=dst, i=T1, b=dst:
              e.scalar_tensor_tensor(o, i, -0.0019302368, b, ALU.mult, ALU.add))
        chain("dve", lambda e, o=dst, i=T1, b=dst:
              e.scalar_tensor_tensor(o, i, -6.287769e-06, b, ALU.mult, ALU.add))
        if phase != 0.0:
            chain("dve", lambda e, o=dst, i=dst:
                  e.tensor_scalar(o, i, phase, None, ALU.add))
            chain("dve", lambda e, o=SQT, i=dst:
                  e.tensor_scalar(o, i, np.pi, None, ALU.is_gt))
            chain("dve", lambda e, o=dst, i=SQT, b=dst:
                  e.scalar_tensor_tensor(o, i, -2 * np.pi, b, ALU.mult, ALU.add))
        chain("act", lambda e, o=dst, i=dst: e.activation(o, i, AF.Sin))
    sincos(LI, 0.0)
    sincos(LR, np.pi / 2)
    chain("dve", lambda e, o=LI, a=LI, b=T2: e.tensor_tensor(o, a, b, ALU.mult))
    chain("dve", lambda e, o=LR, a=LR, b=T2: e.tensor_tensor(o, a, b, ALU.mult))
    chain("dve", lambda e, o=T1, i=LR: e.tensor_scalar(o, i, -1.0, None, ALU.add))
    chain("dve", lambda e, o=T2, a=AR, b=AR: e.tensor_tensor(o, a, b, ALU.mult))
    chain("dve", lambda e, o=SQT, a=AI, b=AI: e.tensor_tensor(o, a, b, ALU.mult))
    chain("dve", lambda e, o=T2, a=T2, b=SQT: e.tensor_tensor(o, a, b, ALU.add))
    chain("dve", lambda e, o=T2, i=T2: e.reciprocal(o, i))
    chain("dve", lambda e, o=CR, a=T1, b=AR: e.tensor_tensor(o, a, b, ALU.mult))
    chain("dve", lambda e, o=SQT, a=LI, b=AI: e.tensor_tensor(o, a, b, ALU.mult))
    chain("dve", lambda e, o=CR, a=CR, b=SQT: e.tensor_tensor(o, a, b, ALU.add))
    chain("dve", lambda e, o=CR, a=CR, b=T2: e.tensor_tensor(o, a, b, ALU.mult))
    chain("dve", lambda e, o=CI, a=LI, b=AR: e.tensor_tensor(o, a, b, ALU.mult))
    chain("dve", lambda e, o=SQT, a=T1, b=AI: e.tensor_tensor(o, a, b, ALU.mult))
    chain("dve", lambda e, o=CI, a=CI, b=SQT: e.tensor_tensor(o, a, b, ALU.subtract))
    chain("dve", lambda e, o=CI, a=CI, b=T2: e.tensor_tensor(o, a, b, ALU.mult))
    c1 = P.dma("sp", T1, W["C_reT"], waits=[tk[0]])
    c2 = P.dma("sp", T2, W["C_imT"])
    chain("dve", lambda e, o=AR, a=T1, b=CR: e.tensor_tensor(o, a, b, ALU.mult),
          extra=[c1, c2])
    chain("dve", lambda e, o=SQT, a=T2, b=CI: e.tensor_tensor(o, a, b, ALU.mult))
    chain("dve", lambda e, o=AR, a=AR, b=SQT: e.tensor_tensor(o, a, b, ALU.subtract))
    chain("dve", lambda e, o=AI, a=T1, b=CI: e.tensor_tensor(o, a, b, ALU.mult))
    chain("dve", lambda e, o=SQT, a=T2, b=CR: e.tensor_tensor(o, a, b, ALU.mult))
    chain("dve", lambda e, o=AI, a=AI, b=SQT: e.tensor_tensor(o, a, b, ALU.add))
    chain("dve", lambda e, o=CR, a=AR, b=DT_B: e.tensor_tensor(o, a, b, ALU.mult))
    chain("dve", lambda e, o=CR, i=CR: e.tensor_scalar_mul(o, i, 2.0))
    chain("dve", lambda e, o=CI, a=AI, b=DT_B: e.tensor_tensor(o, a, b, ALU.mult))
    chain("dve", lambda e, o=CI, i=CI: e.tensor_scalar_mul(o, i, 2.0))
    chain("dve", lambda e, o=P64R, i=LR: e.tensor_copy(o, i))
    chain("dve", lambda e, o=P64I, i=LI: e.tensor_copy(o, i))
    for _ in range(6):
        chain("dve", lambda e, o=SQT, a=P64R, b=P64I: e.tensor_tensor(o, a, b, ALU.mult))
        chain("dve", lambda e, o=P64R, a=P64R, b=P64R: e.tensor_tensor(o, a, b, ALU.mult))
        chain("dve", lambda e, o=T1, a=P64I, b=P64I: e.tensor_tensor(o, a, b, ALU.mult))
        chain("dve", lambda e, o=P64R, a=P64R, b=T1: e.tensor_tensor(o, a, b, ALU.subtract))
        chain("dve", lambda e, o=P64I, i=SQT: e.tensor_scalar_mul(o, i, 2.0))
    if W.get("DBG2") is not None:
        chain("dve", lambda e: e.engine_nop())
        for cri in range(4):
            P.dma("sp", W["DBG2"][cri * 128:cri * 128 + 96, :],
                  E.SCRF[0:96, cri * 1024:(cri + 1) * 1024],
                  waits=[("s_dve", P.sems["s_dve"][1])])
    LAM2 = E.SCRB[:, 12288:20480]
    PPK = E.SCRB[:, 20480:24576]
    KEV = E.SCRF.bitcast(BF16)[:, 10240:10752]

    def build_powers(dst_packed, L, hoff, seed_re, seed_im, fac_re, fac_im):
        DV = dst_packed.rearrange("p (h l) -> p h l", l=L)
        for hs in range(4):
            ha = hoff + hs * 32
            FR = E.SCRF[0:32, 0:32 * L].rearrange("p (h l) -> p h l", l=L)
            FI = E.SCRF[0:32, 2048:2048 + 32 * L].rearrange("p (h l) -> p h l", l=L)
            if seed_re is None:
                chain("dve", lambda e, o=FR[:, :, 0:1], i=FR[:, :, 0:1]: e.memset(o, 1.0))
                chain("dve", lambda e, o=FI[:, :, 0:1], i=FI[:, :, 0:1]: e.memset(o, 0.0))
            else:
                chain("dve", lambda e, o=FR[:, :, 0:1],
                      i=seed_re[:, ha:ha + 32].unsqueeze(2): e.tensor_copy(o, i))
                chain("dve", lambda e, o=FI[:, :, 0:1],
                      i=seed_im[:, ha:ha + 32].unsqueeze(2): e.tensor_copy(o, i))
            pre, pim = fac_re[:, ha:ha + 32], fac_im[:, ha:ha + 32]
            m = 1
            while m < L:
                for s0 in range(0, m, 16):
                    w = min(16, m - s0)
                    br = pre.unsqueeze(2).broadcast_to([32, 32, w])
                    bi_ = pim.unsqueeze(2).broadcast_to([32, 32, w])
                    sq = E.SCRF[0:32, 4224:4224 + 32 * w].rearrange("p (h l) -> p h l", l=w)
                    chain("dve", lambda e, o=FR[:, :, m + s0:m + s0 + w],
                          a=FR[:, :, s0:s0 + w], b=br: e.tensor_tensor(o, a, b, ALU.mult))
                    chain("dve", lambda e, o=sq, a=FI[:, :, s0:s0 + w], b=bi_:
                          e.tensor_tensor(o, a, b, ALU.mult))
                    chain("dve", lambda e, o=FR[:, :, m + s0:m + s0 + w],
                          a=FR[:, :, m + s0:m + s0 + w], b=sq:
                          e.tensor_tensor(o, a, b, ALU.subtract))
                    chain("dve", lambda e, o=FI[:, :, m + s0:m + s0 + w],
                          a=FR[:, :, s0:s0 + w], b=bi_: e.tensor_tensor(o, a, b, ALU.mult))
                    chain("dve", lambda e, o=sq, a=FI[:, :, s0:s0 + w], b=br:
                          e.tensor_tensor(o, a, b, ALU.mult))
                    chain("dve", lambda e, o=FI[:, :, m + s0:m + s0 + w],
                          a=FI[:, :, m + s0:m + s0 + w], b=sq:
                          e.tensor_tensor(o, a, b, ALU.add))
                if 2 * m < L:
                    PT = E.SCRF[0:32, 4096:4224]
                    pa_, pb_, pc_, pd_ = (PT[:, 0:32], PT[:, 32:64],
                                          PT[:, 64:96], PT[:, 96:128])
                    chain("dve", lambda e, o=pc_, a=pre, b=pim:
                          e.tensor_tensor(o, a, b, ALU.mult))
                    chain("dve", lambda e, o=pa_, a=pre, b=pre:
                          e.tensor_tensor(o, a, b, ALU.mult))
                    chain("dve", lambda e, o=pd_, a=pim, b=pim:
                          e.tensor_tensor(o, a, b, ALU.mult))
                    chain("dve", lambda e, o=pa_, a=pa_, b=pd_:
                          e.tensor_tensor(o, a, b, ALU.subtract))
                    chain("dve", lambda e, o=pb_, i=pc_:
                          e.tensor_scalar_mul(o, i, 2.0))
                    pre, pim = pa_, pb_
                m *= 2
            chain("dve", lambda e, o=DV[0:32, hs * 32:(hs + 1) * 32, :], i=FR:
                  e.tensor_copy(o, i))
            chain("dve", lambda e, o=DV[32:64, hs * 32:(hs + 1) * 32, :], i=FI:
                  e.tensor_copy(o, i))

    for hc in range(8):
        h0 = hc * 128
        build_powers(LAM2, 64, h0, None, None, LR, LI)
        build_powers(PPK, 32, h0, CR, CI, P64R, P64I)
        chain("dve", lambda e, o=PPK[32:64, :], i=PPK[32:64, :]:
              e.tensor_scalar_mul(o, i, -1.0))
        for hb in range(8):
            bi, ps, ft = E.bank()
            mt = None
            for hl in range(16):
                h = hb * 16 + hl
                w0 = [tk[0]] if hl == 0 else []
                if hl == 0 and ft is not None:
                    w0.append(ft)
                mt = P.op("pe", lambda e, o=ps[0:64, hl * 32:(hl + 1) * 32],
                          l=LAM2[0:64, h * 64:(h + 1) * 64], r=PPK[0:64, h * 32:(h + 1) * 32]:
                          e.matmul(o, l, r, start=True, stop=True),
                          waits=w0, sig=(hl == 15))
            ctok = chain("dve", lambda e,
                         o=KEV[0:64, :].rearrange("p (l h) -> p l h", h=16),
                         i=ps[0:64, 0:512].rearrange("p (h l) -> p l h", l=32):
                         e.tensor_copy(o, i), extra=[mt])
            dtok = P.dma("sp", KTM.rearrange("(l1 l2) h -> l2 l1 h", l2=64)[
                :, :, h0 + hb * 16: h0 + hb * 16 + 16],
                KEV[0:64, :].rearrange("p (l h) -> p l h", h=16), waits=[ctok])
            E.bank_free(bi, ctok)
            tk[0] = dtok
    ktm_done = tk[0]

    E.scr_reset()
    TMC = E.scrb(8192)
    FEV = E.scrb(512)
    fMs = [(m, 128) for m in range(0, 4352, 128)]
    for src_sp, dst_sp, wtok in ((UTM, UF, utm_done), (KTM, KF, ktm_done)):
        for cc in range(2):
            ch0 = cc * 512
            ltok = P.dma("pool", TMC.rearrange("p (lg c) -> p lg c", lg=16),
                         src_sp[:, ch0:ch0 + 512].rearrange("(lg p) c -> p lg c", p=128),
                         waits=[wtok, tk[0]])

            def f_cb(mg, m0, nm, cc0, ncx, pa, tok):
                ct = chain("dve", lambda e, o=FEV[0:nm, 0:512], i=pa:
                           e.tensor_copy(o, i), extra=[tok])
                dt_ = P.dma("sp", dst_sp[m0:m0 + nm, ch0:ch0 + 512],
                            FEV[0:nm, 0:512], waits=[ct])
                tk[0] = dt_
                return ct
            mm_fm(P, E, [(W["FWD"], lambda kg, a, n: TMC[:, kg * 512:kg * 512 + 512],
                          ltok)], f_cb, M=4352, Ms=fMs, col_tiles=[(0, 512)])
    fwd_done = tk[0]

    E.scr_reset()
    UFS = E.scrb(4352)
    KFS = E.scrb(4352)
    TPA = E.scrb(128)
    TPB = E.scrb(128)
    UFMC = E.scrb(2048)
    YGC = E.scrb(2048)
    RH2 = [E.scrb(2048), E.scrb(2048)]
    rh_tok = [None, None]
    for hc in range(8):
        h0 = hc * 128
        u1 = P.dma("pool", UFS.rearrange("p (g h) -> p g h", g=34),
                   UF[:, h0:h0 + 128].rearrange("(g p) h -> p g h", p=128),
                   waits=[fwd_done, tk[0]])
        u2_ = P.dma("pool", KFS.rearrange("p (g h) -> p g h", g=34),
                    KF[:, h0:h0 + 128].rearrange("(g p) h -> p g h", p=128))
        for fg in range(17):
            uR = UFS[:, fg * 128:(fg + 1) * 128]
            uI = UFS[:, (17 + fg) * 128:(18 + fg) * 128]
            kR = KFS[:, fg * 128:(fg + 1) * 128]
            kI = KFS[:, (17 + fg) * 128:(18 + fg) * 128]
            w0 = [u1, u2_] if fg == 0 else []
            chain("dve", lambda e, o=TPA[:], a=uR, b=kR: e.tensor_tensor(o, a, b, ALU.mult),
                  extra=w0)
            chain("dve", lambda e, o=TPB[:], a=uI, b=kI: e.tensor_tensor(o, a, b, ALU.mult))
            chain("dve", lambda e, o=TPA[:], a=TPA[:], b=TPB[:]:
                  e.tensor_tensor(o, a, b, ALU.subtract))
            chain("dve", lambda e, o=TPB[:], a=uR, b=kI: e.tensor_tensor(o, a, b, ALU.mult))
            chain("dve", lambda e, o=uI, a=uI, b=kR: e.tensor_tensor(o, a, b, ALU.mult))
            chain("dve", lambda e, o=uI, a=uI, b=TPB[:]: e.tensor_tensor(o, a, b, ALU.add))
            chain("dve", lambda e, o=uR, i=TPA[:]: e.tensor_copy(o, i))

        def ufm_cb(mg, m0, nm, cc0, ncx, pa, tok):
            return chain("act", lambda e, o=UFMC[:, cc0:cc0 + ncx], i=pa,
                         b=W["in_b_col"][:, hc:hc + 1]:
                         e.activation(o, i, AF.Identity, bias=b), extra=[tok])
        mm_fm(P, E, [(W["in_W"][:, h0:h0 + 128], lambda kg, a, n: E.xn(kg, a, n),
                      None)], ufm_cb, M=128, Ms=[(0, 128)])
        held = [E.bank(hold=True) for _ in range(4)]
        x_rd = tk[0]
        for fg in range(34):
            rbuf = RH2[fg % 2]
            rw = [rh_tok[fg % 2]] if rh_tok[fg % 2] else []
            rtok = P.dma("pool", rbuf[:], W["INV"][fg * 128:(fg + 1) * 128, :],
                         waits=rw)
            mt = None
            for lt in range(4):
                bi, ps, ft = held[lt]
                w0 = [rtok]
                if fg == 0:
                    w0.append(x_rd)
                    if ft is not None:
                        w0.append(ft)
                mt = P.op("pe", lambda e, o=ps[:], l=UFS[:, fg * 128:(fg + 1) * 128],
                          r=rbuf[:, lt * 512:(lt + 1) * 512], st=(fg == 0), sp=(fg == 33):
                          e.matmul(o, l, r, start=st, stop=sp, skip_group_check=True),
                          waits=w0, sig=(fg == 33 or lt == 3))
            rh_tok[fg % 2] = mt
        for lt in range(4):
            bi, ps, ft = held[lt]
            chain("dve", lambda e, o=ps[:], i=UFMC[:, lt * 512:(lt + 1) * 512],
                  s=W["D_col"][:, hc:hc + 1], b=ps[:]:
                  e.scalar_tensor_tensor(o, i, s, b, ALU.mult, ALU.add),
                  extra=[mt] if lt == 0 else [])
            ctok = chain("act", lambda e, o=YGC[:, lt * 512:(lt + 1) * 512], i=ps[:]:
                         e.activation(o, i, AF.Gelu))
            E.bank_free(bi, ctok)
        dtok = P.dma("sp", YSP[h0:h0 + 128, :], YGC[:], waits=[tk[0]])
        tk[0] = dtok
    inv_done = tk[0]
    inv_done = tk[0]

    E.scr_reset()
    YGB = E.scrb(G * 512)
    ASB = E.scrb(512)
    SSB = E.scrf(512)
    Y2B = E.scrb(G * 512)
    ygv = lambda g: YGB[:, g * 512:(g + 1) * 512]
    y2v = lambda g: Y2B[:, g * 512:(g + 1) * 512]
    for c0 in range(0, T, 512):
        ltoks = []
        for g in range(G):
            ltoks.append(P.dma("pool", ygv(g), YSP[g * 128:(g + 1) * 128, c0:c0 + 512],
                               waits=[inv_done, tk[0]] if g == 0 else ()))
        x_tok = ltoks[-1]

        def glu_cb(mg, m0, nm, cc0, ncx, pa, tok):
            if mg < 8:
                return chain("dve", lambda e, o=y2v(mg), i=pa, s=W["glu_b"][:, mg:mg + 1]:
                             e.tensor_scalar(o, i, s, None, ALU.add), extra=[tok])
            g2 = mg - 8
            chain("act", lambda e, o=SSB[:, 0:512], i=pa, b=W["glu_b"][:, mg:mg + 1]:
                  e.activation(o, i, AF.Sigmoid, bias=b), extra=[tok])
            return chain("dve", lambda e, o=y2v(g2), a=y2v(g2), b=SSB[:, 0:512]:
                         e.tensor_tensor(o, a, b, ALU.mult))
        mm_fm(P, E, [(W["glu_WT"], lambda kg, a, n: ygv(kg), x_tok)], glu_cb,
              M=2048, col_tiles=[(c0, 512)])

        def out_cb(mg, m0, nm, cc0, ncx, pa, tok):
            chain("act", lambda e, o=pa, i=pa, b=W["out_b"][:, mg:mg + 1]:
                  e.activation(o, i, AF.Identity, bias=b), extra=[tok])
            chain("dve", lambda e, o=pa, a=pa, b=E.xn(mg, cc0, ncx):
                  e.tensor_tensor(o, a, b, ALU.add))
            return chain("dve", lambda e, o=E.xres(mg, cc0, ncx),
                         a=E.xres(mg, cc0, ncx), b=pa:
                         e.tensor_tensor(o, a, b, ALU.add))
        mm_fm(P, E, [(W["out_WT"], lambda kg, a, n: y2v(kg), tk[0])], out_cb,
              M=D, col_tiles=[(c0, 512)])
    P.strict_same = True
    P.barrier()



def t_(a):
    return np.ascontiguousarray(np.asarray(a).T)


def colv(a):
    return np.ascontiguousarray(np.asarray(a).reshape(-1, 128).T)


def host_prepare(inputs):
    H = {}
    H["x_fm"] = np.ascontiguousarray(np.asarray(inputs["L0_attn_x"]).transpose(0, 2, 1))
    H["L0f_A_R"] = np.ascontiguousarray(np.asarray(inputs["L0_ffn_G1_R"])[0].transpose(1, 0, 2).reshape(4, 1024))
    H["L0f_A_I"] = np.ascontiguousarray(np.asarray(inputs["L0_ffn_G1_I"])[0].transpose(1, 0, 2).reshape(4, 1024))
    H["L0f_B_R"] = np.ascontiguousarray(np.asarray(inputs["L0_ffn_G2_R"])[:, :, 0, :].transpose(0, 2, 1).reshape(4, 256))
    H["L0f_B_I"] = np.ascontiguousarray(np.asarray(inputs["L0_ffn_G2_I"])[:, :, 0, :].transpose(0, 2, 1).reshape(4, 256))
    H["L0f_bias_real"] = colv(inputs["L0_ffn_bias_real"])
    H["L0f_bias_imag"] = colv(inputs["L0_ffn_bias_imag"])
    H["L3a_W_omT"] = t_(inputs["L3_attn_W_omega"])
    H["L3a_W_zT"] = t_(inputs["L3_attn_W_zeta"])
    H["L3a_W_BT"] = t_(inputs["L3_attn_W_B"])
    H["L3a_C_T"] = t_(inputs["L3_attn_C"])
    H["L3a_DmatT"] = t_(inputs["L3_attn_D_mat"])
    H["L3a_b_omega"] = np.ascontiguousarray(np.asarray(inputs["L3_attn_b_omega"]).reshape(-1, 1))
    H["L3a_b_zeta"] = np.ascontiguousarray(np.asarray(inputs["L3_attn_b_zeta"]).reshape(-1, 1))
    H["L3a_b_B"] = np.ascontiguousarray(np.asarray(inputs["L3_attn_b_B"]).reshape(-1, 1))
    H["L0a_proj_wT"] = t_(inputs["L0_attn_proj_w"])
    H["L0a_gp_wT"] = t_(inputs["L0_attn_gp_w"])
    H["L0a_ip_wT"] = t_(inputs["L0_attn_ip_w"])
    H["L0a_out_wT"] = t_(inputs["L0_attn_out_w"])
    cw = np.asarray(inputs["L0_attn_conv_w"]).reshape(1536, 4)
    H["L0a_convw"] = np.ascontiguousarray(cw.reshape(12, 128, 4).transpose(1, 0, 2).reshape(128, 48))
    H["L0a_lam"] = colv(inputs["L0_attn_forget_lambda"])
    H["L0a_gp_b"] = colv(inputs["L0_attn_gp_b"])
    H["L0a_ip_b"] = colv(inputs["L0_attn_ip_b"])
    H["L0a_conv_b"] = colv(inputs["L0_attn_conv_b"])
    H["L1a_W_in"] = np.ascontiguousarray(np.asarray(inputs["L1_attn_W_in"]))
    xp = np.asarray(inputs["L1_attn_W_xp"])
    H["L1a_W_xp_dt"] = np.ascontiguousarray(xp[:, 0:64])
    H["L1a_W_xp_B"] = np.ascontiguousarray(xp[:, 64:80])
    H["L1a_W_xp_C"] = np.ascontiguousarray(xp[:, 80:96])
    H["L1a_W_dt"] = np.ascontiguousarray(np.asarray(inputs["L1_attn_W_dt"]))
    H["L1a_W_out"] = np.ascontiguousarray(np.asarray(inputs["L1_attn_W_out"]))
    cw1 = np.asarray(inputs["L1_attn_conv_w"])
    H["L1a_convw"] = np.ascontiguousarray(cw1.reshape(16, 128, 4).transpose(1, 0, 2).reshape(128, 64))
    H["L1a_conv_b"] = colv(inputs["L1_attn_conv_b"])
    H["L1a_b_dt"] = colv(inputs["L1_attn_b_dt"])
    H["L1a_D_param"] = colv(inputs["L1_attn_D_param"])
    H["L1a_ln_g"] = colv(inputs["L1_attn_ln_g"])
    H["L1a_ln_b"] = colv(inputs["L1_attn_ln_b"])
    H["L1a_ALOGC"] = np.ascontiguousarray(np.asarray(inputs["L1_attn_A_log"]).reshape(16, 128, 16).transpose(1, 2, 0).reshape(128, 256))
    H["L1a_rs"] = np.ascontiguousarray(np.broadcast_to(np.asarray(inputs["L1_attn_res_scale"]).reshape(1, 1), (128, 1)))
    H["L1a_SELB"] = np.kron(np.eye(16, dtype=np.float32), np.ones((1, 128), np.float32))
    H["identf"] = np.eye(128, dtype=np.float32)
    H["IDB8"] = np.eye(8, dtype=np.float32)
    ltm = np.zeros((8, 8), np.float32)
    for e in range(8):
        for e2 in range(e):
            ltm[e, e2] = 1.0
    H["LTM"] = np.ascontiguousarray(np.broadcast_to(ltm.reshape(1, 64), (128, 64)))
    H["L1f_router_wT"] = t_(inputs["L1_ffn_router_w"])
    H["L1f_w1T"] = np.ascontiguousarray(np.asarray(inputs["L1_ffn_w1"]).transpose(0, 2, 1).reshape(8192, 4096))
    H["L1f_w2T"] = np.ascontiguousarray(np.asarray(inputs["L1_ffn_w2"]).transpose(0, 2, 1).reshape(32768, 1024))
    H["L2f_enc_W"] = np.ascontiguousarray(np.asarray(inputs["L2_ffn_enc_W"]))
    H["L2f_enc_b"] = colv(inputs["L2_ffn_enc_b"])
    H["L2f_s2c_W"] = np.ascontiguousarray(np.asarray(inputs["L2_ffn_s2c_W"]))
    H["L2f_s2c_b"] = np.ascontiguousarray(np.asarray(inputs["L2_ffn_s2c_b"]).reshape(-1, 1))
    H["L2f_rW1"] = np.ascontiguousarray(np.asarray(inputs["L2_ffn_rW1"]))
    H["L2f_rb1"] = np.ascontiguousarray(np.asarray(inputs["L2_ffn_rb1"]).reshape(-1, 1))
    H["L2f_rW2"] = np.ascontiguousarray(np.asarray(inputs["L2_ffn_rW2"]))
    H["L2f_rb2"] = np.ascontiguousarray(np.asarray(inputs["L2_ffn_rb2"]).reshape(-1, 1))
    H["L2f_We1"] = np.ascontiguousarray(np.asarray(inputs["L2_ffn_We1"]).reshape(512, 1024))
    H["L2f_be1"] = np.ascontiguousarray(np.asarray(inputs["L2_ffn_be1"]).reshape(8, 8, 128).transpose(2, 0, 1).reshape(128, 64))
    H["L2f_We2"] = np.ascontiguousarray(np.asarray(inputs["L2_ffn_We2"]).reshape(8192, 64))
    H["L2f_BE2"] = np.ascontiguousarray(np.asarray(inputs["L2_ffn_be2"]).reshape(1, 512))
    H["L2f_c2s_W"] = np.ascontiguousarray(np.asarray(inputs["L2_ffn_c2s_W"]))
    H["L2f_c2s_b"] = colv(inputs["L2_ffn_c2s_b"])
    H["L2f_dec_W"] = np.ascontiguousarray(np.asarray(inputs["L2_ffn_dec_W"]))
    H["L2f_dec_b"] = colv(inputs["L2_ffn_dec_b"])
    H["L2f_ln_g"] = colv(inputs["L2_ffn_ln_g"])
    H["L2f_ln_b"] = colv(inputs["L2_ffn_ln_b"])
    import ml_dtypes
    H["L2a_in_W"] = t_(inputs["L2_attn_in_W"])
    H["L2a_in_b_row"] = np.ascontiguousarray(np.asarray(inputs["L2_attn_in_b"]).reshape(1, 1024))
    H["L2a_in_b_col"] = colv(inputs["L2_attn_in_b"])
    H["L2a_logA_T"] = t_(inputs["L2_attn_log_A_real"])
    H["L2a_Aim_T"] = t_(inputs["L2_attn_A_imag"])
    H["L2a_log_dt_row"] = np.ascontiguousarray(np.asarray(inputs["L2_attn_log_dt"]).reshape(1, 1024))
    H["L2a_C_reT"] = t_(np.asarray(inputs["L2_attn_C"])[:, :, 0])
    H["L2a_C_imT"] = t_(np.asarray(inputs["L2_attn_C"])[:, :, 1])
    H["L2a_D_col"] = colv(inputs["L2_attn_D_skip"])
    H["L2a_glu_WT"] = t_(inputs["L2_attn_glu_W"])
    H["L2a_glu_b"] = colv(inputs["L2_attn_glu_b"])
    H["L2a_out_WT"] = t_(inputs["L2_attn_out_W"])
    H["L2a_out_b"] = colv(inputs["L2_attn_out_b"])
    lv = np.arange(2048, dtype=np.float64)[:, None]
    fv = np.arange(2049, dtype=np.float64)[None, :]
    ang = 2.0 * np.pi * lv * fv / 4096.0
    fwd = np.zeros((2048, 4352), np.float32)
    fwd[:, 0:2049] = np.cos(ang)
    fwd[:, 2176:4225] = -np.sin(ang)
    H["L2a_FWD"] = fwd.astype(ml_dtypes.bfloat16)
    wgt = np.full((1, 2049), 2.0); wgt[0, 0] = 1.0; wgt[0, 2048] = 1.0
    inv = np.zeros((4352, 2048), np.float32)
    inv[0:2049, :] = (wgt.T * np.cos(ang.T) / 4096.0)
    inv[2176:4225, :] = (wgt.T * -np.sin(ang.T) / 4096.0)
    H["L2a_INV"] = inv.astype(ml_dtypes.bfloat16)
    H["L3f_W_rT"] = t_(inputs["L3_ffn_W_r"])
    H["L3f_W_kT"] = t_(inputs["L3_ffn_W_k"])
    H["L3f_W_vT"] = t_(inputs["L3_ffn_W_v"])
    H["L3f_mu_r"] = colv(inputs["L3_ffn_mu_r"])
    H["L3f_mu_k"] = colv(inputs["L3_ffn_mu_k"])
    return H



def setup_weights(nc, P, E, specs, loaded, prefix):
    W = {}
    ltoks = []
    for name, shape in specs:
        base = name.split("@")[0]
        key = base if base in GLOBAL_KEYS else prefix + base
        if key in loaded:
            W[base] = loaded[key]
            continue
        dt_ = BF16 if base in ("FWD", "INV") else F32
        dr = nc.dram_tensor(key, list(shape), dt_, kind="ExternalInput").ap()
        if name.endswith("@cons"):
            tile = E.cons(shape[1])
            ltoks.append(P.dma("sp", tile[0:shape[0], :], dr))
            W[base] = tile
        elif name.endswith("@sbufbf") or base == "SELB":
            tile = E.es.enter_context(nc.sbuf_tensor(key + "_t", list(shape), BF16))
            ltoks.append(P.dma("pool", tile[:], dr))
            W[base] = tile
        elif name.endswith("@sbuff32"):
            tile = E.es.enter_context(nc.sbuf_tensor(key + "_t", list(shape), F32))
            ltoks.append(P.dma("sp", tile[:], dr))
            W[base] = tile
        else:
            W[base] = dr
        loaded[key] = W[base]
    return W, ltoks


def build_single(sub, weight_specs=None, emit_fn=None, debug=False):
    nc = bass.Bass("TRN2", target_bir_lowering=False, debug=False)
    es = ExitStack()
    x_in = nc.dram_tensor("x_in", [D, T], F32, kind="ExternalInput").ap()
    x_out = nc.dram_tensor("x_out", [D, T], F32, kind="ExternalOutput").ap()
    P = Prog(nc, es)
    E = Env(nc, es, P)
    ltoks = []
    for g in range(G):
        ltoks.append(P.dma("sp", E.xres(g), x_in[g * 128:(g + 1) * 128, :]))
    W, wl = setup_weights(nc, P, E, weight_specs, {}, "")
    ltoks += wl
    for e in ENGS:
        for tk in ltoks:
            P._wait(e, tk)
    emit_fn(P, E, W)
    stoks = []
    for g in range(G):
        stoks.append(P.dma("sp", x_out[g * 128:(g + 1) * 128, :], E.xres(g)))
    if debug:
        dbg = nc.dram_tensor("dbg_scrf", [128, SCRF_COLS], F32, kind="ExternalOutput").ap()
        stoks.append(P.dma("sp", dbg, E.SCRF[:]))
    for tk in stoks:
        P._wait("sp", tk)
    P.finalize()
    es.close()
    return nc


SUB_SPECS = {
    "l3_ffn": [("W_rT", (D, D)), ("W_kT", (D, D)), ("W_vT", (D, D)),
               ("mu_r@cons", (128, G)), ("mu_k@cons", (128, G))],
    "l0_ffn": [("A_R", (4, 1024)), ("A_I", (4, 1024)), ("B_R", (4, 256)),
               ("B_I", (4, 256)), ("bias_real@cons", (128, 4)),
               ("bias_imag@cons", (128, 4))],
    "l3_attn": [("W_omT", (D, 64)), ("W_zT", (D, 64)), ("W_BT", (D, 128)),
                ("C_T", (128, D)), ("DmatT", (D, D)), ("b_omega@cons", (64, 1)),
                ("b_zeta@cons", (64, 1)), ("b_B@cons", (128, 1))],
    "l0_attn": [("proj_wT", (D, 3072)), ("gp_wT", (1536, 1536)),
                ("ip_wT", (1536, 1536)), ("out_wT", (1536, D)),
                ("convw@cons", (128, 48)), ("lam@cons", (128, 12)),
                ("gp_b@cons", (128, 12)), ("ip_b@cons", (128, 12)),
                ("conv_b@cons", (128, 12))],
    "l1_attn": [("W_in", (D, 4096)), ("W_xp_dt", (2048, 64)), ("W_xp_B", (2048, 16)),
                ("W_xp_C", (2048, 16)), ("W_dt", (64, 2048)), ("W_out", (2048, D)),
                ("convw@cons", (128, 64)), ("conv_b@cons", (128, 16)),
                ("b_dt@cons", (128, 16)), ("D_param@cons", (128, 16)),
                ("ln_g@cons", (128, 8)), ("ln_b@cons", (128, 8)),
                ("ALOGC@cons", (128, 256)), ("rs@cons", (128, 1)), ("SELB", (16, 2048))],
    "l1_ffn": [("router_wT", (D, 8)), ("w1T", (8192, 4096)), ("w2T", (32768, 1024)),
               ("identf@sbuff32", (128, 128)), ("IDB8@sbufbf", (8, 8)),
               ("LTM@cons", (128, 64))],
    "l2_ffn": [("enc_W", (D, 2048)), ("s2c_W", (2048, 64)), ("rW1", (64, 64)),
               ("rW2", (64, 8)), ("We1", (512, 1024)), ("We2", (8192, 64)),
               ("c2s_W", (64, 2048)), ("dec_W", (2048, D)),
               ("identf@sbuff32", (128, 128)), ("IDB8@sbufbf", (8, 8)),
               ("BE2@sbufbf", (1, 512)), ("LTM@cons", (128, 64)),
               ("enc_b@cons", (128, 16)), ("s2c_b@cons", (64, 1)),
               ("rb1@cons", (64, 1)), ("rb2@cons", (8, 1)), ("be1@cons", (128, 64)),
               ("c2s_b@cons", (128, 16)), ("dec_b@cons", (128, 8)),
               ("ln_g@cons", (128, 8)), ("ln_b@cons", (128, 8))],
    "l2_attn": [("in_W", (D, D)), ("in_b_row", (1, D)), ("log_dt_row", (1, D)),
                ("logA_T", (32, D)), ("Aim_T", (32, D)), ("C_reT", (32, D)),
                ("C_imT", (32, D)), ("glu_WT", (D, 2048)), ("out_WT", (D, D)),
                ("FWD", (2048, 4352)), ("INV", (4352, 2048)),
                ("in_b_col@cons", (128, 8)), ("D_col@cons", (128, 8)),
                ("glu_b@cons", (128, 16)), ("out_b@cons", (128, 8))],
}

SUB_PREFIX = {"l3_ffn": "L3f_", "l0_ffn": "L0f_", "l3_attn": "L3a_",
              "l0_attn": "L0a_", "l1_attn": "L1a_", "l1_ffn": "L1f_",
              "l2_ffn": "L2f_", "l2_attn": "L2a_"}
GLOBAL_KEYS = {"identf", "IDB8", "LTM", "SELB"}


def prep_internal(nc, sub, W):
    if sub == "l0_ffn" and "W0R" not in W:
        for nm in ("W0R", "W0I", "W0In"):
            W[nm] = nc.dram_tensor(nm, [512, 512], BF16, kind="Internal").ap()
    if sub == "l2_attn" and "UTM" not in W:
        for nm, shp in (("UTM", [2048, 1024]), ("KTM", [2048, 1024]),
                        ("UF", [4352, 1024]), ("KF", [4352, 1024]),
                        ("YSP", [1024, 2048])):
            W[nm] = nc.dram_tensor(nm, shp, BF16, kind="Internal").ap()


SUB_EMIT = {}


def build_l3_ffn_single():
    return build_single("l3_ffn", SUB_SPECS["l3_ffn"], emit_l3_ffn)


def build_l0_ffn_single():
    def emit(P, E, W):
        W = dict(W)
        prep_internal(E.nc, "l0_ffn", W)
        emit_l0_ffn(P, E, W, build=True)
    return build_single("l0_ffn", SUB_SPECS["l0_ffn"], emit)


def build_l3_attn_single():
    return build_single("l3_attn", weight_specs=SUB_SPECS["l3_attn"],
                        emit_fn=emit_l3_attn)


def build_l0_attn_single():
    return build_single("l0_attn", weight_specs=SUB_SPECS["l0_attn"],
                        emit_fn=emit_l0_attn)


def build_l1_attn_single():
    return build_single("l1_attn", weight_specs=SUB_SPECS["l1_attn"],
                        emit_fn=emit_l1_attn)


def build_l1_ffn_single():
    return build_single("l1_ffn", weight_specs=SUB_SPECS["l1_ffn"],
                        emit_fn=lambda P, E, W: emit_moe(P, E, W, 1))


def build_l2_ffn_single():
    return build_single("l2_ffn", weight_specs=SUB_SPECS["l2_ffn"],
                        emit_fn=lambda P, E, W: emit_moe(P, E, W, 2))


def build_l2_attn_single():
    def emit(P, E, W):
        W = dict(W)
        prep_internal(E.nc, "l2_attn", W)
        emit_l2_attn(P, E, W)
    return build_single("l2_attn", weight_specs=SUB_SPECS["l2_attn"], emit_fn=emit)


CONVERT_ENABLE = True


def build_fused():
    nc = bass.Bass("TRN2", target_bir_lowering=False, debug=False)
    es = ExitStack()
    x_in = nc.dram_tensor("x_in", [D, T], F32, kind="ExternalInput").ap()
    x_out = nc.dram_tensor("x_out", [D, T], F32, kind="ExternalOutput").ap()
    P = Prog(nc, es)
    E = Env(nc, es, P)
    loaded = {}
    subW = {}
    ltoks = []
    for g in range(G):
        ltoks.append(P.dma("sp", E.xres(g), x_in[g * 128:(g + 1) * 128, :]))
    for sub in SUB_SPECS:
        W, wl = setup_weights(nc, P, E, SUB_SPECS[sub], loaded, SUB_PREFIX[sub])
        prep_internal(nc, sub, W)
        subW[sub] = W
        ltoks += wl
    for e in ENGS:
        for tk in ltoks:
            P._wait(e, tk)
    if CONVERT_ENABLE:
        emit_weight_convert(nc, P, E, subW)
        P.barrier()
    first_l0f = [True]
    for sub, pref, keys in _LAUNCH_PLAN:
        W = subW[sub]
        if sub == "l0_ffn":
            emit_l0_ffn(P, E, W, build=first_l0f[0])
            first_l0f[0] = False
        elif sub == "l1_ffn":
            emit_moe(P, E, W, 1)
        elif sub == "l2_ffn":
            emit_moe(P, E, W, 2)
        elif sub == "l0_attn":
            emit_l0_attn(P, E, W)
        elif sub == "l1_attn":
            emit_l1_attn(P, E, W)
        elif sub == "l2_attn":
            emit_l2_attn(P, E, W)
        elif sub == "l3_ffn":
            emit_l3_ffn(P, E, W)
        elif sub == "l3_attn":
            emit_l3_attn(P, E, W)
    stoks = []
    for g in range(G):
        stoks.append(P.dma("sp", x_out[g * 128:(g + 1) * 128, :], E.xres(g)))
    for tk in stoks:
        P._wait("sp", tk)
    P.finalize()
    es.close()
    return nc


_FUSED_NC = None


def kernel(**inputs):
    global _FUSED_NC
    H = host_prepare(inputs)
    if _FUSED_NC is None:
        _FUSED_NC = build_fused()
    nc = _FUSED_NC
    ncores = 8
    wmap = {}
    for sub, specs in SUB_SPECS.items():
        pref = SUB_PREFIX[sub]
        for name, shape in specs:
            base = name.split("@")[0]
            key = base if base in GLOBAL_KEYS else pref + base
            v = H.get(key)
            if v is None:
                v = H.get(base)
            if v is None:
                v = H.get(pref + base)
            assert v is not None, key
            wmap[key] = v
    in_maps = [dict(wmap, x_in=np.ascontiguousarray(H["x_fm"][b]))
               for b in range(ncores)]
    res = run_bass_kernel_spmd(nc, in_maps, core_ids=list(range(ncores)))
    out = np.stack([np.asarray(res.results[b]["x_out"]).T for b in range(ncores)],
                   axis=0)
    return np.ascontiguousarray(out.astype(np.float32))


_SUB_BUILDERS = None


def _get_builders():
    global _SUB_BUILDERS
    if _SUB_BUILDERS is None:
        _SUB_BUILDERS = {
            "l0_ffn": build_l0_ffn_single(),
            "l0_attn": build_l0_attn_single(),
            "l1_ffn": build_l1_ffn_single(),
            "l1_attn": build_l1_attn_single(),
            "l2_ffn": build_l2_ffn_single(),
            "l2_attn": build_l2_attn_single(),
            "l3_ffn": build_l3_ffn_single(),
            "l3_attn": build_l3_attn_single(),
        }
    return _SUB_BUILDERS


CONVERT_KEYS = {
    "l3_ffn": ["W_rT", "W_kT", "W_vT"],
    "l3_attn": ["W_omT", "W_zT", "W_BT", "C_T", "DmatT"],
    "l0_attn": ["proj_wT", "gp_wT", "ip_wT", "out_wT"],
    "l1_attn": ["W_in", "W_xp_dt", "W_xp_B", "W_xp_C", "W_dt", "W_out"],
    "l1_ffn": ["router_wT", "w1T", "w2T"],
    "l2_ffn": ["enc_W", "s2c_W", "rW1", "rW2", "We1", "We2", "c2s_W", "dec_W"],
    "l2_attn": ["in_W", "glu_WT", "out_WT"],
}


def emit_weight_convert(nc, P, E, subW):
    done = {}
    state = [None, None]
    idx = [0]
    for sub, keys in CONVERT_KEYS.items():
        for k in keys:
            src_ap = subW[sub][k]
            name = src_ap.tensor.name if hasattr(src_ap, "tensor") else k
            if name in done:
                subW[sub][k] = done[name]
                continue
            Ks, Ms = src_ap.shape
            img = nc.dram_tensor("bfi_" + name, [Ks, Ms], BF16, kind="Internal").ap()
            for r0 in range(0, Ks, 128):
                rp = min(128, Ks - r0)
                for c0 in range(0, Ms, 2176):
                    cw = min(2176, Ms - c0)
                    i = idx[0]
                    idx[0] ^= 1
                    buf = E.WB[i]
                    w0 = [state[i]] if state[i] else []
                    t1 = P.dma("pool", buf[0:rp, 0:cw],
                               src_ap[r0:r0 + rp, c0:c0 + cw], waits=w0)
                    t2 = P.dma("sp", img[r0:r0 + rp, c0:c0 + cw],
                               buf[0:rp, 0:cw], waits=[t1])
                    state[i] = t2
            done[name] = img
            subW[sub][k] = img
    E.wb_tok[0] = state[0]
    E.wb_tok[1] = state[1]


_LAUNCH_PLAN = [
    ("l0_ffn", "L0f_", ["A_R", "A_I", "B_R", "B_I", "bias_real", "bias_imag"]),
    ("l0_attn", "L0a_", ["proj_wT", "gp_wT", "ip_wT", "out_wT", "convw", "lam",
                         "gp_b", "ip_b", "conv_b"]),
    ("l0_ffn", "L0f_", None),
    ("l1_ffn", "L1f_", ["router_wT", "w1T", "w2T", "identf", "IDB8", "LTM"]),
    ("l1_attn", "L1a_", ["W_in", "W_xp_dt", "W_xp_B", "W_xp_C", "W_dt", "W_out",
                         "convw", "conv_b", "b_dt", "D_param", "ln_g", "ln_b",
                         "ALOGC", "rs", "SELB"]),
    ("l1_ffn", "L1f_", None),
    ("l2_ffn", "L2f_", ["enc_W", "s2c_W", "rW1", "rW2", "We1", "We2", "c2s_W",
                        "dec_W", "BE2", "identf", "IDB8", "LTM", "enc_b", "s2c_b",
                        "rb1", "rb2", "be1", "c2s_b", "dec_b", "ln_g", "ln_b"]),
    ("l2_attn", "L2a_", ["in_W", "in_b_row", "log_dt_row", "logA_T", "Aim_T",
                         "C_reT", "C_imT", "glu_WT", "out_WT", "FWD", "INV",
                         "in_b_col", "D_col", "glu_b", "out_b"]),
    ("l2_ffn", "L2f_", None),
    ("l3_ffn", "L3f_", ["W_rT", "W_kT", "W_vT", "mu_r", "mu_k"]),
    ("l3_attn", "L3a_", ["W_omT", "W_zT", "W_BT", "C_T", "DmatT", "b_omega",
                         "b_zeta", "b_B"]),
    ("l3_ffn", "L3f_", None),
]


def kernel_multilaunch(**inputs):
    H = host_prepare(inputs)
    builders = _get_builders()
    ncores = 8
    xs = [np.ascontiguousarray(H["x_fm"][b]) for b in range(ncores)]
    plan_keys = {}
    for sub, pref, keys in _LAUNCH_PLAN:
        if keys is not None:
            plan_keys[sub] = (pref, keys)
    for sub, pref, keys in _LAUNCH_PLAN:
        pref, keys = plan_keys[sub]
        nc = builders[sub]
        wmap = {}
        for k in keys:
            wmap[k] = H.get(pref + k, H.get(k))
        in_maps = [dict(wmap, x_in=xs[b]) for b in range(ncores)]
        res = run_bass_kernel_spmd(nc, in_maps, core_ids=list(range(ncores)))
        xs = [np.asarray(res.results[b]["x_out"]) for b in range(ncores)]
    out = np.stack([x.T for x in xs], axis=0)
    return np.ascontiguousarray(out.astype(np.float32))
